# version 04
# baseline (speedup 1.0000x reference)
import contextlib
import numpy as np
import ml_dtypes
import concourse.bass as bass
import concourse.mybir as mybir
from concourse.bass_utils import run_bass_kernel_spmd

F32 = mybir.dt.float32
BF16 = mybir.dt.bfloat16
AF = mybir.ActivationFunctionType
ALU = mybir.AluOpType

S, H, NH, NKV, D, I = 2048, 2048, 16, 8, 128, 8192
TOPK = 819
EPS = 1e-5
ISCALE = float(1.0 / np.sqrt(128.0))
CSHIFT = 25.0
BISECT_ITERS = 42

ENGINES = ("tensor", "vector", "scalar", "gpsimd", "sync")


class Sem:
    __slots__ = ("h", "n", "name")

    def __init__(self, h, name):
        self.h, self.n, self.name = h, 0, name


def at(s):
    return (s, s.n)


class Prog:
    def __init__(self):
        self.ops = {e: [] for e in ENGINES}

    def emit(self, eng, fn, waits=(), inc=None, incn=1):
        w = [(s.h, v) for (s, v) in waits if s is not None and v > 0]
        self.ops[eng].append((fn, w, None if inc is None else inc.h, incn))
        if inc is not None:
            inc.n += incn

    def run_in(self, block):
        for eng in ENGINES:
            ops = self.ops[eng]

            def body(engine, _ops=ops):
                for fn, waits, inch, incn in _ops:
                    for (sh, v) in waits:
                        engine.wait_ge(sh, v)
                    inst = fn(engine)
                    if inch is not None:
                        inst.then_inc(inch, incn)

            getattr(block, eng)(body)


def build_program(debug=False):
    nc = bass.Bass("TRN2", target_bir_lowering=False)
    prog = Prog()
    stk = contextlib.ExitStack()
    E = stk.enter_context

    d_x = nc.dram_tensor("x", [S, H], F32, kind="ExternalInput").ap()
    d_wq = nc.dram_tensor("wq", [H, NH * D], F32, kind="ExternalInput").ap()
    d_wk = nc.dram_tensor("wk", [H, NKV * D], F32, kind="ExternalInput").ap()
    d_wv = nc.dram_tensor("wv", [H, NKV * D], F32, kind="ExternalInput").ap()
    d_wo = nc.dram_tensor("wo", [NH * D, H], F32, kind="ExternalInput").ap()
    d_wr = nc.dram_tensor("wr", [H, NH], F32, kind="ExternalInput").ap()
    d_bqfm = nc.dram_tensor("bqfm", [128, NH], F32, kind="ExternalInput").ap()
    d_bkfm = nc.dram_tensor("bkfm", [128, NKV], F32, kind="ExternalInput").ap()
    d_bv = nc.dram_tensor("bv", [1, NKV * D], F32, kind="ExternalInput").ap()
    d_bo = nc.dram_tensor("bo", [1, H], F32, kind="ExternalInput").ap()
    d_br = nc.dram_tensor("br", [NH, 1], F32, kind="ExternalInput").ap()
    d_wg = nc.dram_tensor("wg", [H, I], F32, kind="ExternalInput").ap()
    d_wu = nc.dram_tensor("wu", [H, I], F32, kind="ExternalInput").ap()
    d_wd = nc.dram_tensor("wd", [I, H], F32, kind="ExternalInput").ap()
    d_ident = nc.dram_tensor("ident", [128, 128], BF16, kind="ExternalInput").ap()
    d_identf = nc.dram_tensor("identf", [128, 128], F32, kind="ExternalInput").ap()
    d_cos = nc.dram_tensor("cosfm", [128, S], F32, kind="ExternalInput").ap()
    d_sin = nc.dram_tensor("sinfm", [128, S], F32, kind="ExternalInput").ap()
    d_ones_bf = nc.dram_tensor("ones_bf", [1, 128], BF16, kind="ExternalInput").ap()
    d_ones_f32 = nc.dram_tensor("ones_f32", [1, 128], F32, kind="ExternalInput").ap()
    spk = "ExternalOutput" if debug else "Internal"
    d_qsp = nc.dram_tensor("qsp", [NH * D, S], BF16, kind=spk).ap()
    d_osp = nc.dram_tensor("osp", [NH * D, S], BF16, kind=spk).ap()
    d_ysp = nc.dram_tensor("ysp", [S, H], F32, kind=spk).ap()
    d_out = nc.dram_tensor("out", [S, H], F32, kind="ExternalOutput").ap()
    if debug:
        d_dbgz = nc.dram_tensor("dbgz", [NH, S], F32, kind="ExternalOutput").ap()
        d_dbgmb = nc.dram_tensor("dbgmb", [128, 256], F32, kind="ExternalOutput").ap()
        d_dbglo = nc.dram_tensor("dbglo", [NH, 3], F32, kind="ExternalOutput").ap()
        d_dbgzkm = nc.dram_tensor("dbgzkm", [128, 256], F32, kind="ExternalOutput").ap()
        d_dbgk = nc.dram_tensor("dbgk", [128, NKV * S], BF16, kind="ExternalOutput").ap()
        d_dbge = nc.dram_tensor("dbge", [128, 16 * 512], BF16, kind="ExternalOutput").ap()
        d_dbgpv = nc.dram_tensor("dbgpv", [128, 129], F32, kind="ExternalOutput").ap()
        d_dbga = nc.dram_tensor("dbga", [128, 8192], F32, kind="ExternalOutput").ap()
        d_dbghfm = nc.dram_tensor("dbghfm", [128, 64 * 512], BF16, kind="ExternalOutput").ap()
        d_dbgb = nc.dram_tensor("dbgb", [128, 4 * 2048], BF16, kind="ExternalOutput").ap()
        d_dbgct = nc.dram_tensor("dbgct", [128, 2048], F32, kind="ExternalOutput").ap()
        d_dbgst = nc.dram_tensor("dbgst", [128, 16], F32, kind="ExternalOutput").ap()
        d_dbgsq = nc.dram_tensor("dbgsq", [128, 2048], BF16, kind="ExternalOutput").ap()
        d_dbghq = nc.dram_tensor("dbghq", [128, 4096], F32, kind="ExternalOutput").ap()
        d_dbgscr = nc.dram_tensor("dbgscr", [128, 2560], F32, kind="ExternalOutput").ap()

    U1 = E(nc.sbuf_tensor("U1", [128, 65536], mybir.dt.uint8)).ap()
    U2 = E(nc.sbuf_tensor("U2", [128, 33024], mybir.dt.uint8)).ap()
    U3 = E(nc.sbuf_tensor("U3", [128, 32768], mybir.dt.uint8)).ap()
    U4 = E(nc.sbuf_tensor("U4", [128, 16384], mybir.dt.uint8)).ap()
    U5 = E(nc.sbuf_tensor("U5", [128, 16384], mybir.dt.uint8)).ap()
    U6 = E(nc.sbuf_tensor("U6", [128, 16384], mybir.dt.uint8)).ap()

    def u8v(U, b0, b1, dt):
        n = b1 - b0
        return U[:, b0:b1].bitcast(dt)

    sb_cos = E(nc.sbuf_tensor("sb_cos", [128, S], F32)).ap()
    sb_sin = E(nc.sbuf_tensor("sb_sin", [128, S], F32)).ap()
    sb_ident = E(nc.sbuf_tensor("sb_ident", [128, 128], BF16)).ap()
    sb_identf = E(nc.sbuf_tensor("sb_identf", [128, 128], F32)).ap()
    sb_ones_bf = E(nc.sbuf_tensor("sb_ones_bf", [1, 128], BF16)).ap()
    sb_ones_f32 = E(nc.sbuf_tensor("sb_ones_f32", [1, 128], F32)).ap()
    sb_bqfm = E(nc.sbuf_tensor("sb_bqfm", [128, NH], F32)).ap()
    sb_bkfm = E(nc.sbuf_tensor("sb_bkfm", [128, NKV], F32)).ap()
    sb_br = E(nc.sbuf_tensor("sb_br", [NH, 1], F32)).ap()
    sb_bv = E(nc.sbuf_tensor("sb_bv", [1, NKV * D], BF16)).ap()
    sb_zkm = E(nc.sbuf_tensor("sb_zkm", [128, 256], F32)).ap()
    sb_mb = E(nc.sbuf_tensor("sb_mb", [128, 256], F32)).ap()
    sb_lo = E(nc.sbuf_tensor("sb_lo", [NH, 1], F32)).ap()
    sb_hi = E(nc.sbuf_tensor("sb_hi", [NH, 1], F32)).ap()
    sb_mid = E(nc.sbuf_tensor("sb_mid", [NH, 1], F32)).ap()
    sb_cnt = E(nc.sbuf_tensor("sb_cnt", [NH, 1], F32)).ap()
    sb_gs = E(nc.sbuf_tensor("sb_gs", [NH, 1], F32)).ap()
    sb_t16 = E(nc.sbuf_tensor("sb_t16", [NH, 1], F32)).ap()
    sb_lor = E(nc.sbuf_tensor("sb_lor", [1, NH], F32)).ap()
    sb_lob = E(nc.sbuf_tensor("sb_lob", [128, NH], F32)).ap()
    sb_st = E(nc.sbuf_tensor("sb_st", [128, 16], F32)).ap()
    sb_rcp = E(nc.sbuf_tensor("sb_rcp", [128, 1], F32)).ap()
    sb_eps = E(nc.sbuf_tensor("sb_eps", [128, 1], F32)).ap()

    ps = [E(nc.psum_tensor(f"ps{i}", [128, 512], F32)).ap() for i in range(8)]

    def psb(i):
        return ps[i].bitcast(BF16)

    def mksem(name):
        return Sem(E(nc.semaphore(name)), name)

    s_pe, s_act, s_dve = mksem("s_pe"), mksem("s_act"), mksem("s_dve")
    s_pool = mksem("s_pool")
    s_out, s_misc = mksem("s_out"), mksem("s_misc")
    _dma_sems = [mksem(f"s_d{i}") for i in range(40)]
    _free = list(range(40))
    _slots = {}

    def slot(name):
        if name not in _slots:
            _slots[name] = _dma_sems[_free.pop(0)]
        return _slots[name]

    bguard = {}

    def bg(bi):
        return [bguard[bi]] if bi in bguard else []

    def cast_load(dst, src, sem, waits=()):
        prog.emit("gpsimd", lambda g, d=dst, s_=src: g.dma_start(d, s_),
                  waits=waits, inc=sem, incn=16)
        return at(sem)

    def sync_dma(dst, src, sem, waits=()):
        prog.emit("sync", lambda e, d=dst, s_=src: e.dma_start(d, s_),
                  waits=waits, inc=sem, incn=16)
        return at(sem)

    def act_dma(dst, src, sem, waits=()):
        prog.emit("scalar", lambda e, d=dst, s_=src: e.dma_start(d, s_),
                  waits=waits, inc=sem, incn=16)
        return at(sem)

    def mm(lhsT, rhs, out, start, stop, waits=(), inc=None):
        prog.emit("tensor",
                  lambda t, a=lhsT, b=rhs, o=out, s1=start, s2=stop:
                  t.matmul(o, a, b, start=s1, stop=s2), waits=waits, inc=inc)

    def tpose(out, in_, waits=(), inc=None):
        kdim = in_.shape[0]
        idt = sb_identf if in_.dtype == F32 else sb_ident
        prog.emit("tensor",
                  lambda t, o=out, i=in_, k=kdim, ii=idt:
                  t.transpose(o, i, ii[0:k, 0:k]),
                  waits=waits, inc=inc)

    def act(out, in_, func, bias=0.0, scale=1.0, accum_out=None, waits=(), inc=None):
        prog.emit("scalar",
                  lambda s, o=out, i=in_, f=func, b=bias, sc=scale, ao=accum_out:
                  s.activation(o, i, f, bias=b, scale=sc, accum_out=ao),
                  waits=waits, inc=inc)

    def dve(fname, *args, waits=(), inc=None, **kw):
        prog.emit("vector",
                  lambda v, f=fname, a=args, k=kw: getattr(v, f)(*a, **k),
                  waits=waits, inc=inc)

    def pool(fname, *args, waits=(), inc=None, **kw):
        prog.emit("gpsimd",
                  lambda g, f=fname, a=args, k=kw: getattr(g, f)(*a, **k),
                  waits=waits, inc=inc)

    c_tok = None
    for dst, src in ((sb_ident, d_ident), (sb_identf, d_identf), (sb_cos, d_cos), (sb_sin, d_sin),
                     (sb_ones_bf, d_ones_bf), (sb_ones_f32, d_ones_f32),
                     (sb_bqfm, d_bqfm), (sb_bkfm, d_bkfm), (sb_br, d_br)):
        c_tok = sync_dma(dst, src, s_misc)
    t_consts = c_tok
    dve("memset", sb_eps, EPS, inc=s_dve)

    x0_fm = U1.bitcast(BF16)
    stage = [u8v(U4, 0, 4096, BF16), u8v(U4, 4096, 8192, BF16)]
    x_r = d_x.rearrange("(tc p) f -> p tc f", p=128)
    stage_free = [None, None]
    ev_hist = []
    for tc in range(16):
        sl = tc % 2
        w = [t_consts] if tc == 0 else []
        if stage_free[sl] is not None:
            w.append(stage_free[sl])
        ld = cast_load(stage[sl], x_r[:, tc, :], slot(f"stage{sl}"), waits=w)
        for fc in range(16):
            k = tc * 16 + fc
            bi = 6 + (k % 2)
            bank = psb(bi)[:, 0:128]
            tw = ([ld] if fc == 0 else []) + bg(bi)
            tpose(bank, stage[sl][:, fc * 128:(fc + 1) * 128], waits=tw, inc=s_pe)
            dve("tensor_copy",
                x0_fm[:, fc * 2048 + tc * 128: fc * 2048 + tc * 128 + 128],
                bank, waits=[at(s_pe)], inc=s_dve)
            bguard[bi] = at(s_dve)
            ev_hist.append(at(s_dve))
        stage_free[sl] = at(s_pe)
    t_x0fm = at(s_dve)

    def x0c(kc, lo, n):
        return x0_fm[:, kc * 2048 + lo: kc * 2048 + lo + n]

    bvf = u8v(U4, 4096, 8192, F32)[0:1, 0:NKV * D]
    ld_bv = sync_dma(bvf, d_bv, s_misc, waits=[t_x0fm, stage_free[1]])
    act(sb_bv, bvf, AF.Identity, waits=[ld_bv], inc=s_act)
    t_bvbf = at(s_act)

    sb_z = u8v(U6, 8192, 16384, F32)[0:16, :]
    wr_sb = u8v(U5, 8192, 8704, BF16)
    ld_wr = cast_load(wr_sb.rearrange("p (kc m) -> p kc m", kc=16),
                      d_wr.rearrange("(kc p) m -> p kc m", p=128), slot("wr"))
    for qt in range(4):
        bi = 2 + qt % 2
        zps = ps[bi][0:16, :]
        for kc in range(16):
            tw = []
            if kc == 0:
                tw = [ld_wr, t_x0fm] + bg(bi)
            mm(wr_sb[:, kc * 16:(kc + 1) * 16], x0c(kc, qt * 512, 512), zps,
               start=(kc == 0), stop=(kc == 15), waits=tw,
               inc=(s_pe if kc == 15 else None))
        act(sb_z[:, qt * 512:(qt + 1) * 512], zps, AF.Identity, bias=sb_br,
            waits=[at(s_pe)], inc=s_act)
        bguard[bi] = at(s_act)
    t_z = at(s_act)
    t_z_pe = at(s_pe)

    zge = u8v(U4, 0, 4096, BF16)[0:16, :]
    pool("memset", sb_lo, -32.0, waits=[t_z, stage_free[0]])
    pool("memset", sb_hi, 32.0, inc=s_pool)
    _bis = {"it": 0, "ptok": at(s_pool), "dtok": None}

    def emit_bisect_iter():
        if _bis["it"] >= BISECT_ITERS:
            return
        _bis["it"] += 1
        pw = [] if _bis["dtok"] is None else [_bis["dtok"]]
        pool("tensor_add", sb_mid, sb_lo, sb_hi, waits=pw)
        pool("tensor_scalar", sb_mid, sb_mid, 0.5, None, ALU.mult, inc=s_pool)
        ptok = at(s_pool)
        dve("tensor_scalar", zge, sb_z, sb_mid, None, ALU.is_ge, ALU.add,
            accum_out=sb_cnt, waits=[ptok], inc=s_dve)
        _bis["dtok"] = at(s_dve)
        pool("tensor_scalar", sb_gs, sb_cnt, float(TOPK) - 0.5, None, ALU.is_ge,
             waits=[_bis["dtok"]])
        pool("tensor_sub", sb_t16, sb_mid, sb_lo)
        pool("tensor_mul", sb_t16, sb_t16, sb_gs)
        pool("tensor_add", sb_lo, sb_lo, sb_t16)
        pool("tensor_sub", sb_t16, sb_hi, sb_mid)
        pool("tensor_mul", sb_t16, sb_t16, sb_gs)
        pool("tensor_add", sb_hi, sb_mid, sb_t16, inc=s_pool)
        _bis["ptok"] = at(s_pool)

    def emit_maskbias():
        t_bis = _bis["ptok"]
        mask01 = u8v(U4, 8192, 16384, F32)[0:16, :]
        dve("tensor_scalar", mask01, sb_z, sb_lo, None, ALU.is_ge,
            waits=[t_bis, t_z, stage_free[1]], inc=s_dve)
        t_m01 = at(s_dve)
        for kc in range(16):
            bi = 4 + kc % 2
            bank = ps[bi][:, 0:16]
            tw = ([t_m01] if kc <= 1 else []) + bg(bi)
            tpose(bank, mask01[:, kc * 128:(kc + 1) * 128], waits=tw, inc=s_pe)
            dve("tensor_copy", sb_zkm[:, kc * 16:(kc + 1) * 16], bank,
                waits=[at(s_pe)], inc=s_dve)
            bguard[bi] = at(s_dve)
        dve("tensor_scalar", sb_mb, sb_zkm, 10000.0, 10000.0 + CSHIFT,
            ALU.mult, ALU.subtract, inc=s_dve)
        return at(s_dve), at(s_pe)

    rope_q = [u8v(U6, 0, 2048, F32), u8v(U6, 4096, 6144, F32)]
    rope_r = [u8v(U6, 2048, 4096, F32), u8v(U6, 6144, 8192, F32)]
    Qslot = [u8v(U6, 8192, 12288, BF16), u8v(U6, 12288, 16384, BF16)]
    rope_i = [0]
    rope_busy = {}

    def rope_block(dst_bf, bank, bias_col, qt, extra=()):
        i = rope_i[0] % 2
        rope_i[0] += 1
        q, r = rope_q[i], rope_r[i]
        cs = sb_cos[:, qt * 512:(qt + 1) * 512]
        sn = sb_sin[:, qt * 512:(qt + 1) * 512]
        aw = [at(s_pe)] + list(extra)
        if i in rope_busy:
            aw.append(rope_busy[i])
        act(q, bank, AF.Identity, bias=bias_col, waits=aw)
        act(r[0:64, :], q[64:128, :], AF.Identity, scale=-1.0)
        act(r[64:128, :], q[0:64, :], AF.Identity, inc=s_act)
        a_tok = at(s_act)
        dve("tensor_mul", q, q, cs, waits=[a_tok])
        dve("tensor_mul", r, r, sn, inc=s_dve)
        dve("tensor_add", dst_bf, q, r, waits=[at(s_dve)], inc=s_dve)
        d_tok = at(s_dve)
        rope_busy[i] = d_tok
        return d_tok, a_tok

    K_fm = U3.bitcast(BF16)
    wslot = [u8v(U5, 0, 4096, BF16), u8v(U5, 4096, 8192, BF16)]
    wk_r = d_wk.rearrange("(kc p) m -> p kc m", p=128)
    wq_r = d_wq.rearrange("(kc p) m -> p kc m", p=128)
    wtok = [None, None]
    dtok = None
    for g in range(NKV):
        sl = g % 2
        w = [wtok[sl]] if wtok[sl] is not None else []
        ldk = cast_load(wslot[sl], wk_r[:, :, g * 128:(g + 1) * 128],
                        slot(f"wkq{sl}"), waits=w)
        for qt in range(4):
            bi = (g * 4 + qt) % 2
            for kc in range(16):
                tw = ([ldk, t_x0fm] + bg(bi)) if kc == 0 else []
                mm(wslot[sl][:, kc * 128:(kc + 1) * 128], x0c(kc, qt * 512, 512),
                   ps[bi], start=(kc == 0), stop=(kc == 15), waits=tw,
                   inc=(s_pe if kc == 15 else None))
            dtok, atok = rope_block(
                K_fm[:, g * 2048 + qt * 512: g * 2048 + (qt + 1) * 512],
                ps[bi], sb_bkfm[:, g:g + 1], qt)
            bguard[bi] = atok
            emit_bisect_iter()
        wtok[sl] = dtok

    V_aug = U2.bitcast(BF16)[:, 0:16 * 1032]
    v4 = V_aug.rearrange("p (tc g e) -> p tc g e", tc=16, g=8, e=129)
    dve("memset", v4[:, :, :, 128:129], 1.0, inc=s_dve)
    t_vones = at(s_dve)

    wv_r = d_wv.rearrange("(kc p) m -> p kc m", p=128)
    vw = [u8v(U5, 8192, 12288, BF16), u8v(U5, 12288, 16384, BF16)]
    vtok = [None, None]
    vev = None
    for vh in range(NKV):
        sl = vh % 2
        w = [vtok[sl]] if vtok[sl] is not None else [t_bvbf, t_z_pe]
        ldv = cast_load(vw[sl], wv_r[:, :, vh * 128:(vh + 1) * 128],
                        slot(f"wv{sl}"), waits=w)
        for tc in range(16):
            bi = 2 + (vh * 16 + tc) % 2
            for kc in range(16):
                tw = ([ldv, t_vones] + bg(bi)) if kc == 0 else []
                mm(x0c(kc, tc * 128, 128), vw[sl][:, kc * 128:(kc + 1) * 128],
                   ps[bi][:, 0:128], start=(kc == 0), stop=False, waits=tw)
            mm(sb_ones_bf, sb_bv[0:1, vh * 128:(vh + 1) * 128],
               ps[bi][:, 0:128], start=False, stop=True, inc=s_pe)
            col = tc * 1032 + vh * 129
            act(V_aug[:, col:col + 128], ps[bi][:, 0:128], AF.Identity,
                waits=[at(s_pe)], inc=s_act)
            bguard[bi] = at(s_act)
            vev = at(s_act)
            if tc % 4 == 3:
                emit_bisect_iter()
        vtok[sl] = vev
    t_vaug = vev
    while _bis["it"] < BISECT_ITERS:
        emit_bisect_iter()

    t_mb, t_mb_pe = emit_maskbias()
    if debug:
        dbg_tok = sync_dma(d_dbgz, sb_z, s_misc, waits=[t_z, t_mb])
        prog.emit("vector", lambda v: v.tensor_copy(sb_st[0:16, 8:9], sb_lo),
                  waits=[t_mb])
        prog.emit("vector", lambda v: v.tensor_copy(sb_st[0:16, 9:10], sb_hi))
        prog.emit("vector", lambda v: v.tensor_copy(sb_st[0:16, 10:11], sb_cnt),
                  inc=s_dve)
        sync_dma(d_dbglo, sb_st[0:16, 8:11], s_misc, waits=[at(s_dve)])
        sync_dma(d_dbgzkm, sb_zkm, s_misc)
        sync_dma(d_dbgmb, sb_mb, s_misc)
        t_mb = at(s_misc)
        prog.emit("vector", lambda v: v.tensor_copy(sb_t16, sb_lo),
                  waits=[at(s_misc)], inc=s_dve)
        t_mb = at(s_dve)

    qsp_w = []
    qtok = [None, None]
    qspill = [None, None]
    for h in range(NH):
        sl = h % 2
        w = [qtok[sl]] if qtok[sl] is not None else []
        ldq = cast_load(wslot[sl], wq_r[:, :, h * 128:(h + 1) * 128],
                        slot(f"wkq{sl}"), waits=w)
        for qt in range(4):
            bi = (h * 4 + qt) % 2
            for kc in range(16):
                tw = []
                if kc == 0:
                    tw = [ldq] + bg(bi)
                    if qt == 0 and qspill[sl] is not None:
                        tw.append(qspill[sl])
                    if h == 0 and qt == 0:
                        tw.append(t_mb_pe)
                mm(wslot[sl][:, kc * 128:(kc + 1) * 128], x0c(kc, qt * 512, 512),
                   ps[bi], start=(kc == 0), stop=(kc == 15), waits=tw,
                   inc=(s_pe if kc == 15 else None))
            ex = [t_mb] if (h == 0 and qt == 0) else []
            dtok, atok = rope_block(Qslot[sl][:, qt * 512:(qt + 1) * 512],
                                    ps[bi], sb_bqfm[:, h:h + 1], qt, extra=ex)
            bguard[bi] = atok
        qspill[sl] = sync_dma(d_qsp[h * 128:(h + 1) * 128, :], Qslot[sl],
                              slot(f"qsp{sl}"), waits=[dtok])
        qsp_w.append(qspill[sl])
        qtok[sl] = dtok
    t_a_pe = at(s_pe)
    t_a_dve = at(s_dve)
    if debug:
        sync_dma(d_dbgk, K_fm, s_misc, waits=[t_a_dve])


    d_wgbf = nc.dram_tensor("wgbf", [64, 128, 2048], BF16).ap()
    d_wubf = nc.dram_tensor("wubf", [64, 128, 2048], BF16).ap()
    d_wdbf = nc.dram_tensor("wdbf", [128, 64 * 2048], BF16).ap()
    d_wobf = nc.dram_tensor("wobf", [4, 128, 16 * 512], BF16).ap()
    wg_r0 = d_wg.rearrange("(kc p) m -> p kc m", p=128)
    wu_r0 = d_wu.rearrange("(kc p) m -> p kc m", p=128)
    wd_r0 = d_wd.rearrange("(ic p) o -> p ic o", p=128)
    wo_r0 = d_wo.rearrange("(kc p) o -> p kc o", p=128)
    pc_stage = [U1.bitcast(BF16)[:, 16384 + 8192 * i: 16384 + 8192 * (i + 1)]
                for i in range(2)]
    pc_tok = [None, None]
    _pc_jobs = []
    for osl in range(4):
        _pc_jobs.append((wo_r0[:, :, osl * 512:(osl + 1) * 512],
                         d_wobf[osl], 8192))
    for b4 in range(16):
        _pc_jobs.append((wd_r0[:, b4 * 4:(b4 + 1) * 4, :],
                         d_wdbf[:, b4 * 8192:(b4 + 1) * 8192], 8192))
    for ic in range(64):
        _pc_jobs.append((wg_r0[:, :, ic * 128:(ic + 1) * 128],
                         d_wgbf[ic], 2048))
    for ic in range(64):
        _pc_jobs.append((wu_r0[:, :, ic * 128:(ic + 1) * 128],
                         d_wubf[ic], 2048))
    _pc_i = [0]
    pc_done = {}

    def emit_precache(n):
        for _ in range(n):
            i = _pc_i[0]
            if i >= len(_pc_jobs):
                return
            _pc_i[0] += 1
            srcap, dstap, nel = _pc_jobs[i]
            sl2 = i % 2
            w = [] if pc_tok[sl2] is None else [pc_tok[sl2]]
            if i < 2:
                w.append(t_a_pe)
                w.append(t_a_dve)
            t1 = cast_load(pc_stage[sl2][:, 0:nel], srcap, slot(f"pcl{sl2}"),
                           waits=w)
            pc_tok[sl2] = act_dma(dstap, pc_stage[sl2][:, 0:nel],
                                  slot(f"pcs{sl2}"), waits=[t1])
            pc_done[i] = pc_tok[sl2]

    ebuf = [U1.bitcast(BF16)[:, 8192 * i: 8192 * (i + 1)] for i in range(2)]
    Oslot = [u8v(U4, 0, 4096, BF16), u8v(U4, 4096, 8192, BF16)]
    Ofm = [u8v(U4, 8192, 12288, BF16), u8v(U4, 12288, 16384, BF16)]
    Qb = [u8v(U5, 0, 4096, BF16), u8v(U5, 4096, 8192, BF16)]
    e_free = [None, None]
    osl_free = [None, None]
    ofm_free = [None, None]
    qldw = [None, None]
    osp_w = []
    for h in range(NH):
        g = h // 2
        hs = h % 2
        w = [qsp_w[h]]
        if qldw[hs] is not None:
            w.append(qldw[hs])
        if h < 2:
            w.append(t_a_pe)
        qld = sync_dma(Qb[hs], d_qsp[h * 128:(h + 1) * 128, :],
                       slot(f"qb{hs}"), waits=w)
        for qt in range(4):
            gi = h * 4 + qt
            par = gi % 2
            for kc in range(16):
                bi = 2 + ((gi * 16 + kc) % 3)
                tw = bg(bi)
                if kc == 0:
                    tw = tw + [qld, t_mb]
                    if gi == 0:
                        tw += [t_a_pe, t_a_dve]
                    if e_free[par] is not None:
                        tw.append(e_free[par])
                mm(K_fm[:, g * 2048 + kc * 128: g * 2048 + (kc + 1) * 128],
                   Qb[hs][:, qt * 512:(qt + 1) * 512], ps[bi],
                   start=True, stop=True, waits=tw, inc=s_pe)
                act(ebuf[par][:, kc * 512:(kc + 1) * 512], ps[bi], AF.Exp,
                    bias=sb_mb[:, kc * 16 + h: kc * 16 + h + 1], scale=ISCALE,
                    waits=[at(s_pe)], inc=s_act)
                bguard[bi] = at(s_act)
            t_exp = at(s_act)
            for q2 in range(4):
                qoff = q2 * 128
                bi = 4 + ((gi * 4 + q2) % 4)
                for kc in range(16):
                    tw = []
                    if kc == 0:
                        tw = [t_exp] + bg(bi)
                        if gi == 0 and q2 == 0:
                            tw.append(t_vaug)
                    mm(ebuf[par][:, kc * 512 + qoff: kc * 512 + qoff + 128],
                       V_aug[:, kc * 1032 + g * 129: kc * 1032 + (g + 1) * 129],
                       ps[bi][:, 0:129], start=(kc == 0), stop=(kc == 15),
                       waits=tw, inc=(s_pe if kc == 15 else None))
                t_pv = at(s_pe)
                if debug and h == 15 and qt == 3 and q2 == 3:
                    dbg_pv_sb = Qb[0].bitcast(F32)[:, 0:129]
                    dve("tensor_copy", dbg_pv_sb, ps[bi][:, 0:129],
                        waits=[t_pv], inc=s_dve)
                    sync_dma(d_dbgpv, dbg_pv_sb, s_misc, waits=[at(s_dve)])
                dw = [t_pv]
                if qt == 0 and q2 == 0 and osl_free[hs] is not None:
                    dw.append(osl_free[hs])
                dve("reciprocal", sb_rcp, ps[bi][:, 128:129], waits=dw,
                    inc=s_dve)
                dve("tensor_scalar",
                    Oslot[hs][:, (qt * 4 + q2) * 128:(qt * 4 + q2 + 1) * 128],
                    ps[bi][:, 0:128], sb_rcp, None, ALU.mult,
                    waits=[at(s_dve)], inc=s_dve)
                bguard[bi] = at(s_dve)
            e_free[par] = at(s_pe)
        t_o = at(s_dve)
        for q2 in range(16):
            bi = 2 + (q2 % 2)
            bank = psb(bi)[:, 0:128]
            tw = [t_o] + bg(bi)
            if q2 < 2 and ofm_free[hs] is not None:
                tw.append(ofm_free[hs])
            tpose(bank, Oslot[hs][:, q2 * 128:(q2 + 1) * 128], waits=tw, inc=s_pe)
            dve("tensor_copy", Ofm[hs][:, q2 * 128:(q2 + 1) * 128], bank,
                waits=[at(s_pe)], inc=s_dve)
            bguard[bi] = at(s_dve)
        osl_free[hs] = at(s_pe)
        emit_precache(10)
        ofm_free[hs] = sync_dma(d_osp[h * 128:(h + 1) * 128, :], Ofm[hs],
                                slot(f"osp{hs}"), waits=[at(s_dve)])
        osp_w.append(ofm_free[hs])
        qldw[hs] = at(s_pe)
    t_b_pe = at(s_pe)
    t_b_dve = at(s_dve)
    t_b_act = at(s_act)
    if debug:
        sync_dma(d_dbge, ebuf[(NH * 4 - 1) % 2], s_misc, waits=[t_b_pe, t_b_act])

    Wo_sb = U1.bitcast(BF16)
    wo_tok = []
    emit_precache(1000)
    for osl in range(4):
        dst = Wo_sb.rearrange("p (kc o) -> p kc o", kc=16)[:, :,
                                                           osl * 512:(osl + 1) * 512]
        wo_tok.append(sync_dma(
            dst, d_wobf[osl].rearrange("p (kc o) -> p kc o", kc=16),
            slot(f"wo{osl}"),
            waits=[t_b_pe, t_b_act, pc_done[len(_pc_jobs) - 1],
                   pc_done[len(_pc_jobs) - 2]]))
    bof = u8v(U6, 0, 8192, F32)[0:1, :]
    sb_bo = u8v(U6, 8192, 12288, BF16)[0:1, :]
    ld_bo = sync_dma(bof, d_bo, s_misc, waits=[t_b_pe])
    act(sb_bo, bof, AF.Identity, waits=[ld_bo], inc=s_act)
    t_bobf = at(s_act)

    Ofm_sb = [U3.bitcast(BF16)[:, 8192 * i: 8192 * (i + 1)] for i in range(2)]
    osp_r = d_osp.rearrange("(hc p) t -> p hc t", p=128)
    a_buf = U2.bitcast(F32)[:, 0:2048 * 4]
    x0t = [u8v(U4, 0, 8192, F32), u8v(U4, 8192, 16384, F32)]
    ctile = u8v(U5, 0, 8192, F32)
    ytile = u8v(U5, 8192, 16384, F32)
    sqd = [Ofm_sb[i] for i in range(2)]

    ofmq_tok = [None, None]
    quarter_pe = []
    x0_free = [None, None]
    ysp_w = []
    ln1_prev = None
    cstt_tok = [None, None, None, None]
    for q in range(4):
        qs = q % 2
        w = [t_b_pe, osp_w[-1]]
        if q >= 2:
            w.append(quarter_pe[q - 2])
            w.append(ln1_prev)
        dst4 = Ofm_sb[qs].rearrange("p (hc t) -> p hc t", hc=16)
        ofml = None
        for grp in range(4):
            ofml = sync_dma(dst4[:, grp * 4:(grp + 1) * 4, :],
                            osp_r[:, grp * 4:(grp + 1) * 4, q * 512:(q + 1) * 512],
                            slot(f"ofm{qs}"), waits=(w if grp == 0 else []))
        ofmq_tok[qs] = ofml
        for osl in range(4):
            for tc in range(4):
                bi = (osl * 4 + tc) % 2
                for hc in range(16):
                    tw = []
                    if hc == 0:
                        tw = [ofmq_tok[qs], wo_tok[osl], t_bobf] + bg(bi)
                        if cstt_tok[tc] is not None:
                            tw.append(cstt_tok[tc])
                    mm(Ofm_sb[qs][:, hc * 512 + tc * 128: hc * 512 + tc * 128 + 128],
                       Wo_sb[:, hc * 2048 + osl * 512: hc * 2048 + (osl + 1) * 512],
                       ps[bi], start=(hc == 0), stop=False, waits=tw)
                mm(sb_ones_bf, sb_bo[0:1, osl * 512:(osl + 1) * 512],
                   ps[bi], start=False, stop=True, inc=s_pe)
                act(a_buf[:, tc * 2048 + osl * 512: tc * 2048 + (osl + 1) * 512],
                    ps[bi], AF.Identity, waits=[at(s_pe)], inc=s_act)
                bguard[bi] = at(s_act)
        quarter_pe.append(at(s_pe))
        t_aq = at(s_act)
        if debug and q == 0:
            sync_dma(d_dbga, a_buf, s_misc, waits=[t_aq])
        for tc in range(4):
            tcg = q * 4 + tc
            xs = tc % 2
            xw = [t_aq]
            if x0_free[xs] is not None:
                xw.append(x0_free[xs])
            if q == 0 and tc < 2:
                xw.append(osp_w[-1])
            xld = sync_dma(x0t[xs], d_x[tcg * 128:(tcg + 1) * 128, :],
                           slot(f"x0t{xs}"), waits=xw)
            cw = [xld]
            if ln1_prev is not None:
                cw.append(ln1_prev)
            if ysp_w:
                cw.append(ysp_w[-1])
            dve("scalar_tensor_tensor", ctile, x0t[xs], 1.0,
                a_buf[:, tc * 2048:(tc + 1) * 2048], ALU.mult, ALU.add,
                accum_out=sb_st[:, 0:1], waits=cw, inc=s_dve)
            cstt_tok[tc] = at(s_dve)
            if debug and q == 0 and tc == 0:
                dbgct_tok = sync_dma(d_dbgct, ctile, s_misc, waits=[cstt_tok[0]])
            dve("scalar_tensor_tensor", sqd[qs][:, tc * 2048:(tc + 1) * 2048],
                ctile, 1.0, ctile, ALU.mult, ALU.mult, accum_out=sb_st[:, 1:2],
                inc=s_dve)
            if debug and q == 0 and tc == 0:
                sync_dma(d_dbgsq, sqd[qs][:, 0:2048], s_misc, waits=[at(s_dve)])
            dve("tensor_scalar", sb_st[:, 2:3], sb_st[:, 0:1], 1.0 / 2048.0,
                None, ALU.mult, waits=[at(s_dve)])
            dve("tensor_scalar", sb_st[:, 3:4], sb_st[:, 1:2], 1.0 / 2048.0,
                None, ALU.mult, inc=s_dve)
            dve("tensor_mul", sb_st[:, 4:5], sb_st[:, 2:3], sb_st[:, 2:3],
                waits=[at(s_dve)], inc=s_dve)
            dve("tensor_sub", sb_st[:, 5:6], sb_st[:, 3:4], sb_st[:, 4:5],
                waits=[at(s_dve)], inc=s_dve)
            act(sb_st[:, 6:7], sb_st[:, 5:6], AF.Ln, bias=sb_eps, waits=[at(s_dve)])
            act(sb_st[:, 7:8], sb_st[:, 6:7], AF.Exp, scale=-0.5, inc=s_act)
            yw = [at(s_act)]
            if debug and q == 0 and tc == 0:
                sync_dma(d_dbgst, sb_st, s_misc, waits=[at(s_act)])
                yw.append(at(s_misc))
                yw.append(dbgct_tok)
            dve("tensor_scalar", ytile, ctile, sb_st[:, 2:3], sb_st[:, 7:8],
                ALU.subtract, ALU.mult, waits=yw, inc=s_dve)
            ln1_prev = at(s_dve)
            tok = sync_dma(d_ysp[tcg * 128:(tcg + 1) * 128, :], ytile,
                           slot(f"ysp{tcg % 2}"), waits=[ln1_prev])
            ysp_w.append(tok)
            x0_free[xs] = ln1_prev
    t_c_pe = at(s_pe)
    t_c_dve = at(s_dve)
    t_c_act = at(s_act)

    h_fm = U1.bitcast(BF16)
    wgslot = [U2.bitcast(BF16)[:, 2048 * i: 2048 * (i + 1)] for i in range(2)]
    wuslot = [U2.bitcast(BF16)[:, 4096 + 2048 * i: 4096 + 2048 * (i + 1)]
              for i in range(2)]
    yfm = U2.bitcast(BF16)[:, 8192:16384]
    wdslot = [U3.bitcast(BF16)[:, 8192 * i: 8192 * (i + 1)] for i in range(2)]
    hqd = u8v(U4, 0, 8192, F32)
    ab = u8v(U4, 8192, 16384, F32)
    ystg = u8v(U5, 0, 4096, BF16)
    sgf = u8v(U5, 4096, 6144, F32)
    scr = [u8v(U5, 6144 + 2048 * i, 8192 + 2048 * i, F32) for i in range(5)]
    m1, m2, M1, M2, t3b = scr
    b_buf = U6.bitcast(BF16)[:, 0:8192]
    c2 = u8v(U4, 0, 8192, F32)
    yln = u8v(U4, 8192, 16384, F32)
    outt = u8v(U4, 8192, 16384, F32)


    wg_r = d_wg.rearrange("(kc p) m -> p kc m", p=128)
    wu_r = d_wu.rearrange("(kc p) m -> p kc m", p=128)
    wd_r = d_wd.rearrange("(ic p) o -> p ic o", p=128)
    ysp_rr = d_ysp.rearrange("(tt p) f -> p tt f", p=128)

    wg_tok = [None, None]
    wu_tok = [None, None]
    wd_tok = None
    ystg_free = None
    yfm_pe_free = None
    ln2_prev = None
    out_prev = None
    hfm_free = None
    out_w = []
    t_ysp_all = ysp_w[-1]
    spar_busy = None
    prev_mult = None
    bevac_tok = {}

    for q in range(4):
        for tc in range(4):
            tcg = q * 4 + tc
            w = [ysp_w[tcg]]
            if ystg_free is not None:
                w.append(ystg_free)
            if q == 0 and tc == 0:
                w += [t_c_pe, t_c_act, t_c_dve, t_ysp_all]
            yld = cast_load(ystg, ysp_rr[:, tcg, :], slot("ystg"), waits=w)
            for fc in range(16):
                k = tc * 16 + fc
                bi = 6 + (k % 2)
                bank = psb(bi)[:, 0:128]
                tw = ([yld] if fc == 0 else []) + bg(bi)
                if k < 2 and yfm_pe_free is not None:
                    tw.append(yfm_pe_free)
                tpose(bank, ystg[:, fc * 128:(fc + 1) * 128], waits=tw, inc=s_pe)
                dve("tensor_copy",
                    yfm[:, fc * 512 + tc * 128: fc * 512 + tc * 128 + 128],
                    bank, waits=[at(s_pe)], inc=s_dve)
                bguard[bi] = at(s_dve)
            ystg_free = at(s_pe)
        t_yfm = at(s_dve)


        for qd in range(16):
            for j in range(4):
                ic = qd * 4 + j
                sl = ic % 2
                wgw = [wg_tok[sl]] if wg_tok[sl] is not None else \
                    ([t_b_pe, t_c_dve] if q == 0 else [])
                wuw = [wu_tok[sl]] if wu_tok[sl] is not None else \
                    ([t_b_pe, t_c_dve] if q == 0 else [])
                if q == 0:
                    wgw.append(pc_done[20 + ic])
                    wuw.append(pc_done[84 + ic])
                lg = sync_dma(wgslot[sl], d_wgbf[ic], slot(f"wg{sl}"), waits=wgw)
                lu = sync_dma(wuslot[sl], d_wubf[ic], slot(f"wu{sl}"), waits=wuw)
                pg = (ic % 2)
                pu = 6 + (ic % 2)
                for kc in range(16):
                    tw = ([lg, t_yfm] + bg(pg)) if kc == 0 else []
                    mm(wgslot[sl][:, kc * 128:(kc + 1) * 128],
                       yfm[:, kc * 512:(kc + 1) * 512], ps[pg],
                       start=(kc == 0), stop=(kc == 15), waits=tw,
                       inc=(s_pe if kc == 15 else None))
                t_g = at(s_pe)
                for kc in range(16):
                    tw = ([lu] + bg(pu)) if kc == 0 else []
                    mm(wuslot[sl][:, kc * 128:(kc + 1) * 128],
                       yfm[:, kc * 512:(kc + 1) * 512], ps[pu],
                       start=(kc == 0), stop=(kc == 15), waits=tw,
                       inc=(s_pe if kc == 15 else None))
                t_u = at(s_pe)
                aw = [t_g]
                if prev_mult is not None:
                    aw.append(prev_mult)
                act(sgf, ps[pg], AF.Silu, waits=aw, inc=s_act)
                bguard[pg] = at(s_act)
                mw = [at(s_act), t_u]
                dve("tensor_tensor", hqd[:, j * 512:(j + 1) * 512], sgf,
                    ps[pu], ALU.mult, waits=mw, inc=s_dve)
                prev_mult = at(s_dve)
                bguard[pu] = at(s_dve)
                wg_tok[sl] = t_g
                wu_tok[sl] = t_u
            aw2 = [at(s_dve)]
            if q == 0 and qd == 0:
                aw2.append(t_c_act)
            act(ab, hqd, AF.Abs, waits=aw2, inc=s_act)
            t_abs = at(s_act)
            A_ = [hqd[:, j * 512:(j + 1) * 512] for j in range(4)]
            B_ = [ab[:, j * 512:(j + 1) * 512] for j in range(4)]
            sw = [t_abs]
            if q == 0 and qd == 0:
                sw.append(t_c_dve)
            dve("tensor_tensor", m1, B_[0], B_[1], ALU.min, waits=sw)
            dve("tensor_tensor", M1, B_[0], B_[1], ALU.max)
            dve("tensor_tensor", m2, B_[2], B_[3], ALU.min)
            dve("tensor_tensor", M2, B_[2], B_[3], ALU.max, inc=s_dve)
            dve("tensor_tensor", m1, m1, m2, ALU.max, waits=[at(s_dve)])
            dve("tensor_tensor", M1, M1, M2, ALU.min, inc=s_dve)
            dve("tensor_tensor", t3b, m1, M1, ALU.max, waits=[at(s_dve)],
                inc=s_dve)
            mbuf = [m1, m2, M1, M2]
            hw = [at(s_dve)]
            if hfm_free is not None and qd == 0:
                hw.append(hfm_free)
            for j in range(4):
                dve("tensor_tensor", mbuf[j], B_[j], t3b, ALU.is_ge,
                    waits=(hw if j == 0 else []),
                    inc=(s_dve if j == 3 else None))
            for j in range(4):
                dve("tensor_tensor",
                    h_fm[:, (qd * 4 + j) * 512:(qd * 4 + j + 1) * 512],
                    mbuf[j], A_[j], ALU.mult,
                    waits=([at(s_dve)] if j == 0 else []),
                    inc=(s_dve if j == 3 else None))
            spar_busy = at(s_dve)
        t_hfm = at(s_dve)
        yfm_pe_free = at(s_pe)
        if debug and q == 0:
            sync_dma(d_dbghfm, h_fm, s_misc, waits=[t_hfm])
            sync_dma(d_dbghq[:, 0:2048], hqd, s_misc, waits=[t_hfm])
            sync_dma(d_dbghq[:, 2048:4096], ab, s_misc)
            for si in range(5):
                dbghq_tok = sync_dma(d_dbgscr[:, si * 512:(si + 1) * 512],
                                     scr[si], s_misc)

        wtok_d = [None, None]
        for tp in range(2):
            for blk4 in range(16):
                sl = blk4 % 2
                w = [] if wtok_d[sl] is None else [wtok_d[sl]]
                if blk4 < 2 and tp == 0:
                    w.append(t_hfm)
                    if wd_tok is not None:
                        w.append(wd_tok)
                dst = wdslot[sl].rearrange("p (ic o) -> p ic o", ic=4)
                if q == 0 and tp == 0:
                    w = list(w) + [pc_done[4 + blk4]]
                ldw_ = sync_dma(dst,
                                d_wdbf[:, blk4 * 8192:(blk4 + 1) * 8192]
                                .rearrange("p (ic o) -> p ic o", ic=4),
                                slot(f"wdl{sl}"), waits=w)
                for u in range(4):
                    ic = blk4 * 4 + u
                    for ti in range(2):
                        tc = 2 * tp + ti
                        for osl in range(4):
                            bi = ti * 4 + osl
                            tw = []
                            if ic == 0 and osl == 0:
                                tw = [ldw_, t_hfm] + bg(bi)
                                if bevac_tok.get((tp, bi)) is not None:
                                    tw.append(bevac_tok[(tp, bi)])
                            elif ic == 0:
                                tw = bg(bi)
                                if bevac_tok.get((tp, bi)) is not None:
                                    tw.append(bevac_tok[(tp, bi)])
                            elif osl == 0 and u == 0:
                                tw = [ldw_]
                            mm(h_fm[:, ic * 512 + tc * 128: ic * 512 + tc * 128 + 128],
                               wdslot[sl][:, u * 2048 + osl * 512:
                                          u * 2048 + (osl + 1) * 512],
                               ps[bi], start=(ic == 0), stop=(ic == 63),
                               waits=tw,
                               inc=(s_pe if (u == 3 and bi == 7) else None))
                wtok_d[sl] = at(s_pe)
            t_dn = at(s_pe)
            bw = [t_dn]
            if q > 0 and tp == 0 and ln2_prev is not None:
                bw.append(ln2_prev)
            for ti in range(2):
                tc = 2 * tp + ti
                for osl in range(4):
                    bi = ti * 4 + osl
                    dve("tensor_copy",
                        b_buf[:, tc * 2048 + osl * 512: tc * 2048 + (osl + 1) * 512],
                        ps[bi], waits=(bw if (ti == 0 and osl == 0) else []),
                        inc=(s_dve if (ti == 1 and osl == 3) else None))
            bwr = at(s_dve)
            for ti in range(2):
                for osl in range(4):
                    bguard[ti * 4 + osl] = bwr
                    bevac_tok[(tp, ti * 4 + osl)] = bwr
        wd_tok = bwr
        hfm_free = at(s_pe)
        if debug and q == 0:
            dbgb_tok = sync_dma(d_dbgb, b_buf, s_misc, waits=[bwr])

        for tcq in range(4):
            tcg = q * 4 + tcq
            w = [t_ysp_all, bwr]
            if out_prev is not None:
                w.append(out_prev)
            if ln2_prev is not None:
                w.append(ln2_prev)
            if tcq == 0:
                w.append(spar_busy)
                if debug and q == 0:
                    w.append(dbghq_tok)
                    w.append(dbgb_tok)
            yld = sync_dma(yln, d_ysp[tcg * 128:(tcg + 1) * 128, :],
                           slot("yln"), waits=w)
            dve("scalar_tensor_tensor", c2, yln, 1.0,
                b_buf[:, tcq * 2048:(tcq + 1) * 2048], ALU.mult, ALU.add,
                accum_out=sb_st[:, 0:1], waits=[yld])
            dve("scalar_tensor_tensor", b_buf[:, tcq * 2048:(tcq + 1) * 2048],
                c2, 1.0, c2, ALU.mult, ALU.mult, accum_out=sb_st[:, 1:2],
                inc=s_dve)
            dve("tensor_scalar", sb_st[:, 2:3], sb_st[:, 0:1], 1.0 / 2048.0,
                None, ALU.mult, waits=[at(s_dve)])
            dve("tensor_scalar", sb_st[:, 3:4], sb_st[:, 1:2], 1.0 / 2048.0,
                None, ALU.mult, inc=s_dve)
            dve("tensor_mul", sb_st[:, 4:5], sb_st[:, 2:3], sb_st[:, 2:3],
                waits=[at(s_dve)], inc=s_dve)
            dve("tensor_sub", sb_st[:, 5:6], sb_st[:, 3:4], sb_st[:, 4:5],
                waits=[at(s_dve)], inc=s_dve)
            act(sb_st[:, 6:7], sb_st[:, 5:6], AF.Ln, bias=sb_eps, waits=[at(s_dve)])
            act(sb_st[:, 7:8], sb_st[:, 6:7], AF.Exp, scale=-0.5, inc=s_act)
            dve("tensor_scalar", outt, c2, sb_st[:, 2:3], sb_st[:, 7:8],
                ALU.subtract, ALU.mult, waits=[at(s_act)], inc=s_dve)
            ln2_prev = at(s_dve)
            out_prev = sync_dma(d_out[tcg * 128:(tcg + 1) * 128, :], outt,
                                s_out, waits=[ln2_prev])
            out_w.append(out_prev)

    prog.emit("sync", lambda e: e.nop(nofuse=True), waits=[at(s_out)])

    with nc.Block() as block:
        prog.run_in(block)
    stk.close()
    return nc


def host_constants():
    pos = np.arange(S, dtype=np.float64)
    j = np.arange(128) % 64
    inv = (1.0 / (1.0e6 ** (2.0 * j / 128.0)))[:, None]
    ang = pos[None, :] * inv
    return dict(
        ident=np.eye(128, dtype=ml_dtypes.bfloat16),
        identf=np.eye(128, dtype=np.float32),
        cosfm=np.cos(ang).astype(np.float32),
        sinfm=np.sin(ang).astype(np.float32),
        ones_bf=np.ones((1, 128), dtype=ml_dtypes.bfloat16),
        ones_f32=np.ones((1, 128), dtype=np.float32),
    )


def ffn_perm():
    d = np.arange(I)
    ic, p = d // 128, d % 128
    qd, j = ic // 4, ic % 4
    return (qd * 128 + p) * 4 + j


def make_in_maps(inputs):
    consts = host_constants()
    perm = ffn_perm()
    x = np.ascontiguousarray(np.asarray(inputs["L0_attn_hidden_states"]))
    shared = dict(
        wq=np.ascontiguousarray(np.asarray(inputs["L0_attn_Wq"])),
        wk=np.ascontiguousarray(np.asarray(inputs["L0_attn_Wk"])),
        wv=np.ascontiguousarray(np.asarray(inputs["L0_attn_Wv"])),
        wo=np.ascontiguousarray(np.asarray(inputs["L0_attn_Wo"])),
        wr=np.ascontiguousarray(np.asarray(inputs["L0_attn_Wr"])),
        wg=np.ascontiguousarray(np.asarray(inputs["L0_ffn_w_gate"])[:, perm]),
        wu=np.ascontiguousarray(np.asarray(inputs["L0_ffn_w_up"])[:, perm]),
        wd=np.ascontiguousarray(np.asarray(inputs["L0_ffn_w_down"])[perm, :]),
        bqfm=np.ascontiguousarray(
            np.asarray(inputs["L0_attn_bq"]).reshape(NH, 128).T),
        bkfm=np.ascontiguousarray(
            np.asarray(inputs["L0_attn_bk"]).reshape(NKV, 128).T),
        bv=np.ascontiguousarray(np.asarray(inputs["L0_attn_bv"]).reshape(1, -1)),
        bo=np.ascontiguousarray(np.asarray(inputs["L0_attn_bo"]).reshape(1, -1)),
        br=np.ascontiguousarray(np.asarray(inputs["L0_attn_br"]).reshape(NH, 1)),
        **consts)
    return [dict(x=np.ascontiguousarray(x[c]), **shared) for c in range(8)]


def kernel(**inputs):
    nc = build_program(debug=False)
    in_maps = make_in_maps(inputs)
    res = run_bass_kernel_spmd(nc, in_maps, core_ids=list(range(8)))
    return np.stack([np.asarray(res.results[c]["out"], dtype=np.float32)
                     for c in range(8)], axis=0)


# version 05
# speedup vs baseline: 1.1444x; 1.1444x over previous
import contextlib
import numpy as np
import ml_dtypes
import concourse.bass as bass
import concourse.mybir as mybir
from concourse.bass_utils import run_bass_kernel_spmd

F32 = mybir.dt.float32
BF16 = mybir.dt.bfloat16
AF = mybir.ActivationFunctionType
ALU = mybir.AluOpType

S, H, NH, NKV, D, I = 2048, 2048, 16, 8, 128, 8192
TOPK = 819
EPS = 1e-5
ISCALE = float(1.0 / np.sqrt(128.0))
CSHIFT = 25.0
BISECT_ITERS = 42

ENGINES = ("tensor", "vector", "scalar", "gpsimd", "sync")


class Sem:
    __slots__ = ("h", "n", "name")

    def __init__(self, h, name):
        self.h, self.n, self.name = h, 0, name


def at(s):
    return (s, s.n)


class Prog:
    def __init__(self):
        self.ops = {e: [] for e in ENGINES}

    def emit(self, eng, fn, waits=(), inc=None, incn=1):
        w = [(s.h, v) for (s, v) in waits if s is not None and v > 0]
        self.ops[eng].append((fn, w, None if inc is None else inc.h, incn))
        if inc is not None:
            inc.n += incn

    def run_in(self, block):
        for eng in ENGINES:
            ops = self.ops[eng]

            def body(engine, _ops=ops):
                for fn, waits, inch, incn in _ops:
                    for (sh, v) in waits:
                        engine.wait_ge(sh, v)
                    inst = fn(engine)
                    if inch is not None:
                        inst.then_inc(inch, incn)

            getattr(block, eng)(body)


def build_program(debug=False):
    nc = bass.Bass("TRN2", target_bir_lowering=False)
    prog = Prog()
    stk = contextlib.ExitStack()
    E = stk.enter_context

    d_x = nc.dram_tensor("x", [S, H], F32, kind="ExternalInput").ap()
    d_wq = nc.dram_tensor("wq", [H, NH * D], F32, kind="ExternalInput").ap()
    d_wk = nc.dram_tensor("wk", [H, NKV * D], F32, kind="ExternalInput").ap()
    d_wv = nc.dram_tensor("wv", [H, NKV * D], F32, kind="ExternalInput").ap()
    d_wo = nc.dram_tensor("wo", [NH * D, H], F32, kind="ExternalInput").ap()
    d_wr = nc.dram_tensor("wr", [H, NH], F32, kind="ExternalInput").ap()
    d_bqfm = nc.dram_tensor("bqfm", [128, NH], F32, kind="ExternalInput").ap()
    d_bkfm = nc.dram_tensor("bkfm", [128, NKV], F32, kind="ExternalInput").ap()
    d_bv = nc.dram_tensor("bv", [1, NKV * D], F32, kind="ExternalInput").ap()
    d_bo = nc.dram_tensor("bo", [1, H], F32, kind="ExternalInput").ap()
    d_br = nc.dram_tensor("br", [NH, 1], F32, kind="ExternalInput").ap()
    d_wg = nc.dram_tensor("wg", [H, I], F32, kind="ExternalInput").ap()
    d_wu = nc.dram_tensor("wu", [H, I], F32, kind="ExternalInput").ap()
    d_wd = nc.dram_tensor("wd", [I, H], F32, kind="ExternalInput").ap()
    d_ident = nc.dram_tensor("ident", [128, 128], BF16, kind="ExternalInput").ap()
    d_identf = nc.dram_tensor("identf", [128, 128], F32, kind="ExternalInput").ap()
    d_cos = nc.dram_tensor("cosfm", [128, S], F32, kind="ExternalInput").ap()
    d_sin = nc.dram_tensor("sinfm", [128, S], F32, kind="ExternalInput").ap()
    d_ones_bf = nc.dram_tensor("ones_bf", [1, 128], BF16, kind="ExternalInput").ap()
    d_ones_f32 = nc.dram_tensor("ones_f32", [1, 128], F32, kind="ExternalInput").ap()
    spk = "ExternalOutput" if debug else "Internal"
    d_qsp = nc.dram_tensor("qsp", [NH * D, S], BF16, kind=spk).ap()
    d_osp = nc.dram_tensor("osp", [NH * D, S], BF16, kind=spk).ap()
    d_ysp = nc.dram_tensor("ysp", [S, H], F32, kind=spk).ap()
    d_out = nc.dram_tensor("out", [S, H], F32, kind="ExternalOutput").ap()
    if debug:
        d_dbgz = nc.dram_tensor("dbgz", [NH, S], F32, kind="ExternalOutput").ap()
        d_dbgmb = nc.dram_tensor("dbgmb", [128, 256], F32, kind="ExternalOutput").ap()
        d_dbglo = nc.dram_tensor("dbglo", [NH, 3], F32, kind="ExternalOutput").ap()
        d_dbgzkm = nc.dram_tensor("dbgzkm", [128, 256], F32, kind="ExternalOutput").ap()
        d_dbgk = nc.dram_tensor("dbgk", [128, NKV * S], BF16, kind="ExternalOutput").ap()
        d_dbge = nc.dram_tensor("dbge", [128, 16 * 512], BF16, kind="ExternalOutput").ap()
        d_dbgpv = nc.dram_tensor("dbgpv", [128, 129], F32, kind="ExternalOutput").ap()
        d_dbga = nc.dram_tensor("dbga", [128, 8192], F32, kind="ExternalOutput").ap()
        d_dbghfm = nc.dram_tensor("dbghfm", [128, 64 * 512], BF16, kind="ExternalOutput").ap()
        d_dbgb = nc.dram_tensor("dbgb", [128, 4 * 2048], BF16, kind="ExternalOutput").ap()
        d_dbgct = nc.dram_tensor("dbgct", [128, 2048], F32, kind="ExternalOutput").ap()
        d_dbgst = nc.dram_tensor("dbgst", [128, 16], F32, kind="ExternalOutput").ap()
        d_dbgsq = nc.dram_tensor("dbgsq", [128, 2048], BF16, kind="ExternalOutput").ap()
        d_dbghq = nc.dram_tensor("dbghq", [128, 4096], F32, kind="ExternalOutput").ap()
        d_dbgscr = nc.dram_tensor("dbgscr", [128, 2560], F32, kind="ExternalOutput").ap()

    U1 = E(nc.sbuf_tensor("U1", [128, 65536], mybir.dt.uint8)).ap()
    U2 = E(nc.sbuf_tensor("U2", [128, 33024], mybir.dt.uint8)).ap()
    U3 = E(nc.sbuf_tensor("U3", [128, 32768], mybir.dt.uint8)).ap()
    U4 = E(nc.sbuf_tensor("U4", [128, 16384], mybir.dt.uint8)).ap()
    U5 = E(nc.sbuf_tensor("U5", [128, 16384], mybir.dt.uint8)).ap()
    U6 = E(nc.sbuf_tensor("U6", [128, 16384], mybir.dt.uint8)).ap()

    def u8v(U, b0, b1, dt):
        n = b1 - b0
        return U[:, b0:b1].bitcast(dt)

    sb_cos = E(nc.sbuf_tensor("sb_cos", [128, S], F32)).ap()
    sb_sin = E(nc.sbuf_tensor("sb_sin", [128, S], F32)).ap()
    sb_ident = E(nc.sbuf_tensor("sb_ident", [128, 128], BF16)).ap()
    sb_identf = E(nc.sbuf_tensor("sb_identf", [128, 128], F32)).ap()
    sb_ones_bf = E(nc.sbuf_tensor("sb_ones_bf", [1, 128], BF16)).ap()
    sb_ones_f32 = E(nc.sbuf_tensor("sb_ones_f32", [1, 128], F32)).ap()
    sb_bqfm = E(nc.sbuf_tensor("sb_bqfm", [128, NH], F32)).ap()
    sb_bkfm = E(nc.sbuf_tensor("sb_bkfm", [128, NKV], F32)).ap()
    sb_br = E(nc.sbuf_tensor("sb_br", [NH, 1], F32)).ap()
    sb_bv = E(nc.sbuf_tensor("sb_bv", [1, NKV * D], BF16)).ap()
    sb_zkm = E(nc.sbuf_tensor("sb_zkm", [128, 256], F32)).ap()
    sb_mb = E(nc.sbuf_tensor("sb_mb", [128, 256], F32)).ap()
    sb_lo = E(nc.sbuf_tensor("sb_lo", [NH, 1], F32)).ap()
    sb_hi = E(nc.sbuf_tensor("sb_hi", [NH, 1], F32)).ap()
    sb_mid = E(nc.sbuf_tensor("sb_mid", [NH, 1], F32)).ap()
    sb_cnt = E(nc.sbuf_tensor("sb_cnt", [NH, 1], F32)).ap()
    sb_gs = E(nc.sbuf_tensor("sb_gs", [NH, 1], F32)).ap()
    sb_t16 = E(nc.sbuf_tensor("sb_t16", [NH, 1], F32)).ap()
    sb_lor = E(nc.sbuf_tensor("sb_lor", [1, NH], F32)).ap()
    sb_lob = E(nc.sbuf_tensor("sb_lob", [128, NH], F32)).ap()
    sb_st = E(nc.sbuf_tensor("sb_st", [128, 16], F32)).ap()
    sb_rcp = E(nc.sbuf_tensor("sb_rcp", [128, 1], F32)).ap()
    sb_eps = E(nc.sbuf_tensor("sb_eps", [128, 1], F32)).ap()

    ps = [E(nc.psum_tensor(f"ps{i}", [128, 512], F32)).ap() for i in range(8)]

    def psb(i):
        return ps[i].bitcast(BF16)

    def mksem(name):
        return Sem(E(nc.semaphore(name)), name)

    s_pe, s_act, s_dve = mksem("s_pe"), mksem("s_act"), mksem("s_dve")
    s_pool = mksem("s_pool")
    s_out, s_misc = mksem("s_out"), mksem("s_misc")
    _dma_sems = [mksem(f"s_d{i}") for i in range(40)]
    _free = list(range(40))
    _slots = {}

    def slot(name):
        if name not in _slots:
            _slots[name] = _dma_sems[_free.pop(0)]
        return _slots[name]

    bguard = {}

    def bg(bi):
        return [bguard[bi]] if bi in bguard else []

    def cast_load(dst, src, sem, waits=()):
        prog.emit("gpsimd", lambda g, d=dst, s_=src: g.dma_start(d, s_),
                  waits=waits, inc=sem, incn=16)
        return at(sem)

    def sync_dma(dst, src, sem, waits=()):
        prog.emit("sync", lambda e, d=dst, s_=src: e.dma_start(d, s_),
                  waits=waits, inc=sem, incn=16)
        return at(sem)

    def act_dma(dst, src, sem, waits=()):
        prog.emit("scalar", lambda e, d=dst, s_=src: e.dma_start(d, s_),
                  waits=waits, inc=sem, incn=16)
        return at(sem)

    def mm(lhsT, rhs, out, start, stop, waits=(), inc=None):
        prog.emit("tensor",
                  lambda t, a=lhsT, b=rhs, o=out, s1=start, s2=stop:
                  t.matmul(o, a, b, start=s1, stop=s2), waits=waits, inc=inc)

    def tpose(out, in_, waits=(), inc=None):
        kdim = in_.shape[0]
        idt = sb_identf if in_.dtype == F32 else sb_ident
        prog.emit("tensor",
                  lambda t, o=out, i=in_, k=kdim, ii=idt:
                  t.transpose(o, i, ii[0:k, 0:k]),
                  waits=waits, inc=inc)

    def act(out, in_, func, bias=0.0, scale=1.0, accum_out=None, waits=(), inc=None):
        prog.emit("scalar",
                  lambda s, o=out, i=in_, f=func, b=bias, sc=scale, ao=accum_out:
                  s.activation(o, i, f, bias=b, scale=sc, accum_out=ao),
                  waits=waits, inc=inc)

    def dve(fname, *args, waits=(), inc=None, **kw):
        prog.emit("vector",
                  lambda v, f=fname, a=args, k=kw: getattr(v, f)(*a, **k),
                  waits=waits, inc=inc)

    def pool(fname, *args, waits=(), inc=None, **kw):
        prog.emit("gpsimd",
                  lambda g, f=fname, a=args, k=kw: getattr(g, f)(*a, **k),
                  waits=waits, inc=inc)

    c_tok = None
    for dst, src in ((sb_ident, d_ident), (sb_identf, d_identf), (sb_cos, d_cos), (sb_sin, d_sin),
                     (sb_ones_bf, d_ones_bf), (sb_ones_f32, d_ones_f32),
                     (sb_bqfm, d_bqfm), (sb_bkfm, d_bkfm), (sb_br, d_br)):
        c_tok = sync_dma(dst, src, s_misc)
    t_consts = c_tok
    dve("memset", sb_eps, EPS, inc=s_dve)

    x0_fm = U1.bitcast(BF16)
    stage = [u8v(U4, 0, 4096, BF16), u8v(U4, 4096, 8192, BF16)]
    x_r = d_x.rearrange("(tc p) f -> p tc f", p=128)
    stage_free = [None, None]
    ev_hist = []
    for tc in range(16):
        sl = tc % 2
        w = [t_consts] if tc == 0 else []
        if stage_free[sl] is not None:
            w.append(stage_free[sl])
        ld = cast_load(stage[sl], x_r[:, tc, :], slot(f"stage{sl}"), waits=w)
        for fc in range(16):
            k = tc * 16 + fc
            bi = 6 + (k % 2)
            bank = psb(bi)[:, 0:128]
            tw = ([ld] if fc == 0 else []) + bg(bi)
            tpose(bank, stage[sl][:, fc * 128:(fc + 1) * 128], waits=tw, inc=s_pe)
            dve("tensor_copy",
                x0_fm[:, fc * 2048 + tc * 128: fc * 2048 + tc * 128 + 128],
                bank, waits=[at(s_pe)], inc=s_dve)
            bguard[bi] = at(s_dve)
            ev_hist.append(at(s_dve))
        stage_free[sl] = at(s_pe)
    t_x0fm = at(s_dve)

    def x0c(kc, lo, n):
        return x0_fm[:, kc * 2048 + lo: kc * 2048 + lo + n]

    bvf = u8v(U4, 4096, 8192, F32)[0:1, 0:NKV * D]
    ld_bv = sync_dma(bvf, d_bv, s_misc, waits=[t_x0fm, stage_free[1]])
    act(sb_bv, bvf, AF.Identity, waits=[ld_bv], inc=s_act)
    t_bvbf = at(s_act)

    sb_z = u8v(U6, 8192, 16384, F32)[0:16, :]
    wr_sb = u8v(U5, 8192, 8704, BF16)
    ld_wr = cast_load(wr_sb.rearrange("p (kc m) -> p kc m", kc=16),
                      d_wr.rearrange("(kc p) m -> p kc m", p=128), slot("wr"))
    for qt in range(4):
        bi = 2 + qt % 2
        zps = ps[bi][0:16, :]
        for kc in range(16):
            tw = []
            if kc == 0:
                tw = [ld_wr, t_x0fm] + bg(bi)
            mm(wr_sb[:, kc * 16:(kc + 1) * 16], x0c(kc, qt * 512, 512), zps,
               start=(kc == 0), stop=(kc == 15), waits=tw,
               inc=(s_pe if kc == 15 else None))
        act(sb_z[:, qt * 512:(qt + 1) * 512], zps, AF.Identity, bias=sb_br,
            waits=[at(s_pe)], inc=s_act)
        bguard[bi] = at(s_act)
    t_z = at(s_act)
    t_z_pe = at(s_pe)

    zge = u8v(U4, 0, 4096, BF16)[0:16, :]
    pool("memset", sb_lo, -32.0, waits=[t_z, stage_free[0]])
    pool("memset", sb_hi, 32.0, inc=s_pool)
    _bis = {"it": 0, "ptok": at(s_pool), "dtok": None}

    def emit_bisect_iter():
        if _bis["it"] >= BISECT_ITERS:
            return
        _bis["it"] += 1
        pw = [] if _bis["dtok"] is None else [_bis["dtok"]]
        pool("tensor_add", sb_mid, sb_lo, sb_hi, waits=pw)
        pool("tensor_scalar", sb_mid, sb_mid, 0.5, None, ALU.mult, inc=s_pool)
        ptok = at(s_pool)
        dve("tensor_scalar", zge, sb_z, sb_mid, None, ALU.is_ge, ALU.add,
            accum_out=sb_cnt, waits=[ptok], inc=s_dve)
        _bis["dtok"] = at(s_dve)
        pool("tensor_scalar", sb_gs, sb_cnt, float(TOPK) - 0.5, None, ALU.is_ge,
             waits=[_bis["dtok"]])
        pool("tensor_sub", sb_t16, sb_mid, sb_lo)
        pool("tensor_mul", sb_t16, sb_t16, sb_gs)
        pool("tensor_add", sb_lo, sb_lo, sb_t16)
        pool("tensor_sub", sb_t16, sb_hi, sb_mid)
        pool("tensor_mul", sb_t16, sb_t16, sb_gs)
        pool("tensor_add", sb_hi, sb_mid, sb_t16, inc=s_pool)
        _bis["ptok"] = at(s_pool)

    def emit_maskbias():
        t_bis = _bis["ptok"]
        mask01 = u8v(U4, 8192, 16384, F32)[0:16, :]
        dve("tensor_scalar", mask01, sb_z, sb_lo, None, ALU.is_ge,
            waits=[t_bis, t_z, stage_free[1]], inc=s_dve)
        t_m01 = at(s_dve)
        for kc in range(16):
            bi = 4 + kc % 2
            bank = ps[bi][:, 0:16]
            tw = ([t_m01] if kc <= 1 else []) + bg(bi)
            tpose(bank, mask01[:, kc * 128:(kc + 1) * 128], waits=tw, inc=s_pe)
            dve("tensor_copy", sb_zkm[:, kc * 16:(kc + 1) * 16], bank,
                waits=[at(s_pe)], inc=s_dve)
            bguard[bi] = at(s_dve)
        dve("tensor_scalar", sb_mb, sb_zkm, 10000.0, 10000.0 + CSHIFT,
            ALU.mult, ALU.subtract, inc=s_dve)
        return at(s_dve), at(s_pe)

    rope_q = [u8v(U6, 0, 2048, F32), u8v(U6, 4096, 6144, F32)]
    rope_r = [u8v(U6, 2048, 4096, F32), u8v(U6, 6144, 8192, F32)]
    Qslot = [u8v(U6, 8192, 12288, BF16), u8v(U6, 12288, 16384, BF16)]
    rope_i = [0]
    rope_busy = {}

    def rope_block(dst_bf, bank, bias_col, qt, extra=()):
        i = rope_i[0] % 2
        rope_i[0] += 1
        q, r = rope_q[i], rope_r[i]
        cs = sb_cos[:, qt * 512:(qt + 1) * 512]
        sn = sb_sin[:, qt * 512:(qt + 1) * 512]
        aw = [at(s_pe)] + list(extra)
        if i in rope_busy:
            aw.append(rope_busy[i])
        act(q, bank, AF.Identity, bias=bias_col, waits=aw)
        act(r[0:64, :], q[64:128, :], AF.Identity, scale=-1.0)
        act(r[64:128, :], q[0:64, :], AF.Identity, inc=s_act)
        a_tok = at(s_act)
        dve("tensor_mul", q, q, cs, waits=[a_tok])
        dve("tensor_mul", r, r, sn, inc=s_dve)
        dve("tensor_add", dst_bf, q, r, waits=[at(s_dve)], inc=s_dve)
        d_tok = at(s_dve)
        rope_busy[i] = d_tok
        return d_tok, a_tok

    K_fm = U3.bitcast(BF16)
    wslot = [u8v(U5, 0, 4096, BF16), u8v(U5, 4096, 8192, BF16)]
    wk_r = d_wk.rearrange("(kc p) m -> p kc m", p=128)
    wq_r = d_wq.rearrange("(kc p) m -> p kc m", p=128)
    wtok = [None, None]
    dtok = None
    for g in range(NKV):
        sl = g % 2
        w = [wtok[sl]] if wtok[sl] is not None else []
        ldk = cast_load(wslot[sl], wk_r[:, :, g * 128:(g + 1) * 128],
                        slot(f"wkq{sl}"), waits=w)
        for qt in range(4):
            bi = (g * 4 + qt) % 2
            for kc in range(16):
                tw = ([ldk, t_x0fm] + bg(bi)) if kc == 0 else []
                mm(wslot[sl][:, kc * 128:(kc + 1) * 128], x0c(kc, qt * 512, 512),
                   ps[bi], start=(kc == 0), stop=(kc == 15), waits=tw,
                   inc=(s_pe if kc == 15 else None))
            dtok, atok = rope_block(
                K_fm[:, g * 2048 + qt * 512: g * 2048 + (qt + 1) * 512],
                ps[bi], sb_bkfm[:, g:g + 1], qt)
            bguard[bi] = atok
            emit_bisect_iter()
        wtok[sl] = dtok

    V_aug = U2.bitcast(BF16)[:, 0:16 * 1032]
    v4 = V_aug.rearrange("p (tc g e) -> p tc g e", tc=16, g=8, e=129)
    dve("memset", v4[:, :, :, 128:129], 1.0, inc=s_dve)
    t_vones = at(s_dve)

    wv_r = d_wv.rearrange("(kc p) m -> p kc m", p=128)
    vw = [u8v(U5, 8192, 12288, BF16), u8v(U5, 12288, 16384, BF16)]
    vtok = [None, None]
    vev = None
    for vh in range(NKV):
        sl = vh % 2
        w = [vtok[sl]] if vtok[sl] is not None else [t_bvbf, t_z_pe]
        ldv = cast_load(vw[sl], wv_r[:, :, vh * 128:(vh + 1) * 128],
                        slot(f"wv{sl}"), waits=w)
        for tc in range(16):
            bi = 2 + (vh * 16 + tc) % 2
            for kc in range(16):
                tw = ([ldv, t_vones] + bg(bi)) if kc == 0 else []
                mm(x0c(kc, tc * 128, 128), vw[sl][:, kc * 128:(kc + 1) * 128],
                   ps[bi][:, 0:128], start=(kc == 0), stop=False, waits=tw)
            mm(sb_ones_bf, sb_bv[0:1, vh * 128:(vh + 1) * 128],
               ps[bi][:, 0:128], start=False, stop=True, inc=s_pe)
            col = tc * 1032 + vh * 129
            act(V_aug[:, col:col + 128], ps[bi][:, 0:128], AF.Identity,
                waits=[at(s_pe)], inc=s_act)
            bguard[bi] = at(s_act)
            vev = at(s_act)
            if tc % 4 == 3:
                emit_bisect_iter()
        vtok[sl] = vev
    t_vaug = vev
    while _bis["it"] < BISECT_ITERS:
        emit_bisect_iter()

    t_mb, t_mb_pe = emit_maskbias()
    if debug:
        dbg_tok = sync_dma(d_dbgz, sb_z, s_misc, waits=[t_z, t_mb])
        prog.emit("vector", lambda v: v.tensor_copy(sb_st[0:16, 8:9], sb_lo),
                  waits=[t_mb])
        prog.emit("vector", lambda v: v.tensor_copy(sb_st[0:16, 9:10], sb_hi))
        prog.emit("vector", lambda v: v.tensor_copy(sb_st[0:16, 10:11], sb_cnt),
                  inc=s_dve)
        sync_dma(d_dbglo, sb_st[0:16, 8:11], s_misc, waits=[at(s_dve)])
        sync_dma(d_dbgzkm, sb_zkm, s_misc)
        sync_dma(d_dbgmb, sb_mb, s_misc)
        t_mb = at(s_misc)
        prog.emit("vector", lambda v: v.tensor_copy(sb_t16, sb_lo),
                  waits=[at(s_misc)], inc=s_dve)
        t_mb = at(s_dve)

    qsp_w = []
    qtok = [None, None]
    qspill = [None, None]
    for h in range(NH):
        sl = h % 2
        w = [qtok[sl]] if qtok[sl] is not None else []
        ldq = cast_load(wslot[sl], wq_r[:, :, h * 128:(h + 1) * 128],
                        slot(f"wkq{sl}"), waits=w)
        for qt in range(4):
            bi = (h * 4 + qt) % 2
            for kc in range(16):
                tw = []
                if kc == 0:
                    tw = [ldq] + bg(bi)
                    if qt == 0 and qspill[sl] is not None:
                        tw.append(qspill[sl])
                    if h == 0 and qt == 0:
                        tw.append(t_mb_pe)
                mm(wslot[sl][:, kc * 128:(kc + 1) * 128], x0c(kc, qt * 512, 512),
                   ps[bi], start=(kc == 0), stop=(kc == 15), waits=tw,
                   inc=(s_pe if kc == 15 else None))
            ex = [t_mb] if (h == 0 and qt == 0) else []
            dtok, atok = rope_block(Qslot[sl][:, qt * 512:(qt + 1) * 512],
                                    ps[bi], sb_bqfm[:, h:h + 1], qt, extra=ex)
            bguard[bi] = atok
        qspill[sl] = sync_dma(d_qsp[h * 128:(h + 1) * 128, :], Qslot[sl],
                              slot(f"qsp{sl}"), waits=[dtok])
        qsp_w.append(qspill[sl])
        qtok[sl] = dtok
    t_a_pe = at(s_pe)
    t_a_dve = at(s_dve)
    if debug:
        sync_dma(d_dbgk, K_fm, s_misc, waits=[t_a_dve])


    d_wgbf = nc.dram_tensor("wgbf", [64, 128, 2048], BF16).ap()
    d_wubf = nc.dram_tensor("wubf", [64, 128, 2048], BF16).ap()
    d_wdbf = nc.dram_tensor("wdbf", [128, 64 * 2048], BF16).ap()
    d_wobf = nc.dram_tensor("wobf", [4, 128, 16 * 512], BF16).ap()
    wg_r0 = d_wg.rearrange("(kc p) m -> p kc m", p=128)
    wu_r0 = d_wu.rearrange("(kc p) m -> p kc m", p=128)
    wd_r0 = d_wd.rearrange("(ic p) o -> p ic o", p=128)
    wo_r0 = d_wo.rearrange("(kc p) o -> p kc o", p=128)
    pc_stage = [U1.bitcast(BF16)[:, 16384 + 8192 * i: 16384 + 8192 * (i + 1)]
                for i in range(2)]
    pc_tok = [None, None]
    _pc_jobs = []
    for osl in range(4):
        _pc_jobs.append((wo_r0[:, :, osl * 512:(osl + 1) * 512],
                         d_wobf[osl], 8192))
    for b4 in range(16):
        _pc_jobs.append((wd_r0[:, b4 * 4:(b4 + 1) * 4, :],
                         d_wdbf[:, b4 * 8192:(b4 + 1) * 8192], 8192))
    _pc_i = [0]
    pc_done = {}

    def emit_precache(n):
        for _ in range(n):
            i = _pc_i[0]
            if i >= len(_pc_jobs):
                return
            _pc_i[0] += 1
            srcap, dstap, nel = _pc_jobs[i]
            sl2 = i % 2
            w = [] if pc_tok[sl2] is None else [pc_tok[sl2]]
            if i < 2:
                w.append(t_a_pe)
                w.append(t_a_dve)
            t1 = cast_load(pc_stage[sl2][:, 0:nel], srcap, slot(f"pcl{sl2}"),
                           waits=w)
            pc_tok[sl2] = act_dma(dstap, pc_stage[sl2][:, 0:nel],
                                  slot(f"pcs{sl2}"), waits=[t1])
            pc_done[i] = pc_tok[sl2]

    ebuf = [U1.bitcast(BF16)[:, 8192 * i: 8192 * (i + 1)] for i in range(2)]
    Oslot = [u8v(U4, 0, 4096, BF16), u8v(U4, 4096, 8192, BF16)]
    Ofm = [u8v(U4, 8192, 12288, BF16), u8v(U4, 12288, 16384, BF16)]
    Qb = [u8v(U5, 0, 4096, BF16), u8v(U5, 4096, 8192, BF16)]
    e_free = [None, None]
    osl_free = [None, None]
    ofm_free = [None, None]
    qldw = [None, None]
    osp_w = []
    pend = []
    t_exp_of = {}

    def emit_scores(h, qt, qld):
        gi = h * 4 + qt
        par = gi % 2
        g = h // 2
        for kc in range(16):
            bi = 2 + ((gi * 16 + kc) % 3)
            tw = bg(bi)
            if kc == 0:
                tw = tw + [qld, t_mb]
                if gi == 0:
                    tw += [t_a_pe, t_a_dve]
                if e_free[par] is not None:
                    tw.append(e_free[par])
            mm(K_fm[:, g * 2048 + kc * 128: g * 2048 + (kc + 1) * 128],
               Qb[h % 2][:, qt * 512:(qt + 1) * 512], ps[bi],
               start=True, stop=True, waits=tw, inc=s_pe)
            act(ebuf[par][:, kc * 512:(kc + 1) * 512], ps[bi], AF.Exp,
                bias=sb_mb[:, kc * 16 + h: kc * 16 + h + 1], scale=ISCALE,
                waits=[at(s_pe)], inc=s_act)
            bguard[bi] = at(s_act)
        t_exp_of[gi] = at(s_act)

    def emit_pv(gi):
        h, qt = gi // 4, gi % 4
        par = gi % 2
        g = h // 2
        hs = h % 2
        for q2 in range(4):
            qoff = q2 * 128
            bi = 4 + ((gi * 4 + q2) % 4)
            for kc in range(16):
                tw = []
                if kc == 0:
                    tw = [t_exp_of[gi]] + bg(bi)
                    if gi == 0 and q2 == 0:
                        tw.append(t_vaug)
                mm(ebuf[par][:, kc * 512 + qoff: kc * 512 + qoff + 128],
                   V_aug[:, kc * 1032 + g * 129: kc * 1032 + (g + 1) * 129],
                   ps[bi][:, 0:129], start=(kc == 0), stop=(kc == 15),
                   waits=tw, inc=(s_pe if kc == 15 else None))
            t_pv = at(s_pe)
            dw = [t_pv]
            if qt == 0 and q2 == 0 and osl_free[hs] is not None:
                dw.append(osl_free[hs])
            dve("reciprocal", sb_rcp, ps[bi][:, 128:129], waits=dw,
                inc=s_dve)
            dve("tensor_scalar",
                Oslot[hs][:, (qt * 4 + q2) * 128:(qt * 4 + q2 + 1) * 128],
                ps[bi][:, 0:128], sb_rcp, None, ALU.mult,
                waits=[at(s_dve)], inc=s_dve)
            bguard[bi] = at(s_dve)
        e_free[par] = at(s_pe)

    def emit_ohead(h):
        hs = h % 2
        t_o = at(s_dve)
        for q2 in range(16):
            bi = 2 + (q2 % 2)
            bank = psb(bi)[:, 0:128]
            tw = [t_o] + bg(bi)
            if q2 < 2 and ofm_free[hs] is not None:
                tw.append(ofm_free[hs])
            tpose(bank, Oslot[hs][:, q2 * 128:(q2 + 1) * 128], waits=tw, inc=s_pe)
            dve("tensor_copy", Ofm[hs][:, q2 * 128:(q2 + 1) * 128], bank,
                waits=[at(s_pe)], inc=s_dve)
            bguard[bi] = at(s_dve)
        osl_free[hs] = at(s_pe)
        emit_precache(2)
        ofm_free[hs] = sync_dma(d_osp[h * 128:(h + 1) * 128, :], Ofm[hs],
                                slot(f"osp{hs}"), waits=[at(s_dve)])
        osp_w.append(ofm_free[hs])
        qldw[hs] = at(s_pe)

    for h in range(NH):
        hs = h % 2
        w = [qsp_w[h]]
        if qldw[hs] is not None:
            w.append(qldw[hs])
        if h < 2:
            w.append(t_a_pe)
        qld = sync_dma(Qb[hs], d_qsp[h * 128:(h + 1) * 128, :],
                       slot(f"qb{hs}"), waits=w)
        for qt in range(4):
            gi = h * 4 + qt
            emit_scores(h, qt, qld)
            if pend:
                pgi = pend.pop(0)
                emit_pv(pgi)
                if pgi % 4 == 3:
                    emit_ohead(pgi // 4)
            pend.append(gi)
    while pend:
        pgi = pend.pop(0)
        emit_pv(pgi)
        if pgi % 4 == 3:
            emit_ohead(pgi // 4)
    t_b_pe = at(s_pe)
    t_b_dve = at(s_dve)
    t_b_act = at(s_act)
    if debug:
        sync_dma(d_dbge, ebuf[(NH * 4 - 1) % 2], s_misc, waits=[t_b_pe, t_b_act])

    Wo_sb = U1.bitcast(BF16)
    wo_tok = []
    emit_precache(1000)
    for osl in range(4):
        dst = Wo_sb.rearrange("p (kc o) -> p kc o", kc=16)[:, :,
                                                           osl * 512:(osl + 1) * 512]
        wo_tok.append(sync_dma(
            dst, d_wobf[osl].rearrange("p (kc o) -> p kc o", kc=16),
            slot(f"wo{osl}"),
            waits=[t_b_pe, t_b_act, pc_done[len(_pc_jobs) - 1],
                   pc_done[len(_pc_jobs) - 2]]))
    bof = u8v(U6, 0, 8192, F32)[0:1, :]
    sb_bo = u8v(U6, 8192, 12288, BF16)[0:1, :]
    ld_bo = sync_dma(bof, d_bo, s_misc, waits=[t_b_pe])
    act(sb_bo, bof, AF.Identity, waits=[ld_bo], inc=s_act)
    t_bobf = at(s_act)

    Ofm_sb = [U3.bitcast(BF16)[:, 8192 * i: 8192 * (i + 1)] for i in range(2)]
    osp_r = d_osp.rearrange("(hc p) t -> p hc t", p=128)
    a_buf = U2.bitcast(F32)[:, 0:2048 * 4]
    x0t = [u8v(U4, 0, 8192, F32), u8v(U4, 8192, 16384, F32)]
    ctile = u8v(U5, 0, 8192, F32)
    ytile = u8v(U5, 8192, 16384, F32)
    sqd = [Ofm_sb[i] for i in range(2)]

    ofmq_tok = [None, None]
    quarter_pe = []
    x0_free = [None, None]
    ysp_w = []
    ln1_prev = None
    cstt_tok = [None, None, None, None]
    for q in range(4):
        qs = q % 2
        w = [t_b_pe, osp_w[-1]]
        if q >= 2:
            w.append(quarter_pe[q - 2])
            w.append(ln1_prev)
        dst4 = Ofm_sb[qs].rearrange("p (hc t) -> p hc t", hc=16)
        ofml = None
        for grp in range(4):
            ofml = sync_dma(dst4[:, grp * 4:(grp + 1) * 4, :],
                            osp_r[:, grp * 4:(grp + 1) * 4, q * 512:(q + 1) * 512],
                            slot(f"ofm{qs}"), waits=(w if grp == 0 else []))
        ofmq_tok[qs] = ofml
        for osl in range(4):
            for tc in range(4):
                bi = (osl * 4 + tc) % 2
                for hc in range(16):
                    tw = []
                    if hc == 0:
                        tw = [ofmq_tok[qs], wo_tok[osl], t_bobf] + bg(bi)
                        if cstt_tok[tc] is not None:
                            tw.append(cstt_tok[tc])
                    mm(Ofm_sb[qs][:, hc * 512 + tc * 128: hc * 512 + tc * 128 + 128],
                       Wo_sb[:, hc * 2048 + osl * 512: hc * 2048 + (osl + 1) * 512],
                       ps[bi], start=(hc == 0), stop=False, waits=tw)
                mm(sb_ones_bf, sb_bo[0:1, osl * 512:(osl + 1) * 512],
                   ps[bi], start=False, stop=True, inc=s_pe)
                act(a_buf[:, tc * 2048 + osl * 512: tc * 2048 + (osl + 1) * 512],
                    ps[bi], AF.Identity, waits=[at(s_pe)], inc=s_act)
                bguard[bi] = at(s_act)
        quarter_pe.append(at(s_pe))
        t_aq = at(s_act)
        if debug and q == 0:
            sync_dma(d_dbga, a_buf, s_misc, waits=[t_aq])
        for tc in range(4):
            tcg = q * 4 + tc
            xs = tc % 2
            xw = [t_aq]
            if x0_free[xs] is not None:
                xw.append(x0_free[xs])
            if q == 0 and tc < 2:
                xw.append(osp_w[-1])
            xld = sync_dma(x0t[xs], d_x[tcg * 128:(tcg + 1) * 128, :],
                           slot(f"x0t{xs}"), waits=xw)
            cw = [xld]
            if ln1_prev is not None:
                cw.append(ln1_prev)
            if ysp_w:
                cw.append(ysp_w[-1])
            dve("scalar_tensor_tensor", ctile, x0t[xs], 1.0,
                a_buf[:, tc * 2048:(tc + 1) * 2048], ALU.mult, ALU.add,
                accum_out=sb_st[:, 0:1], waits=cw, inc=s_dve)
            cstt_tok[tc] = at(s_dve)
            if debug and q == 0 and tc == 0:
                dbgct_tok = sync_dma(d_dbgct, ctile, s_misc, waits=[cstt_tok[0]])
            dve("scalar_tensor_tensor", sqd[qs][:, tc * 2048:(tc + 1) * 2048],
                ctile, 1.0, ctile, ALU.mult, ALU.mult, accum_out=sb_st[:, 1:2],
                inc=s_dve)
            if debug and q == 0 and tc == 0:
                sync_dma(d_dbgsq, sqd[qs][:, 0:2048], s_misc, waits=[at(s_dve)])
            dve("tensor_scalar", sb_st[:, 2:3], sb_st[:, 0:1], 1.0 / 2048.0,
                None, ALU.mult, waits=[at(s_dve)])
            dve("tensor_scalar", sb_st[:, 3:4], sb_st[:, 1:2], 1.0 / 2048.0,
                None, ALU.mult, inc=s_dve)
            dve("tensor_mul", sb_st[:, 4:5], sb_st[:, 2:3], sb_st[:, 2:3],
                waits=[at(s_dve)], inc=s_dve)
            dve("tensor_sub", sb_st[:, 5:6], sb_st[:, 3:4], sb_st[:, 4:5],
                waits=[at(s_dve)], inc=s_dve)
            act(sb_st[:, 6:7], sb_st[:, 5:6], AF.Ln, bias=sb_eps, waits=[at(s_dve)])
            act(sb_st[:, 7:8], sb_st[:, 6:7], AF.Exp, scale=-0.5, inc=s_act)
            yw = [at(s_act)]
            if debug and q == 0 and tc == 0:
                sync_dma(d_dbgst, sb_st, s_misc, waits=[at(s_act)])
                yw.append(at(s_misc))
                yw.append(dbgct_tok)
            dve("tensor_scalar", ytile, ctile, sb_st[:, 2:3], sb_st[:, 7:8],
                ALU.subtract, ALU.mult, waits=yw, inc=s_dve)
            ln1_prev = at(s_dve)
            tok = sync_dma(d_ysp[tcg * 128:(tcg + 1) * 128, :], ytile,
                           slot(f"ysp{tcg % 2}"), waits=[ln1_prev])
            ysp_w.append(tok)
            x0_free[xs] = ln1_prev
    t_c_pe = at(s_pe)
    t_c_dve = at(s_dve)
    t_c_act = at(s_act)

    h_fm = U1.bitcast(BF16)
    wgslot = [U2.bitcast(BF16)[:, 2048 * i: 2048 * (i + 1)] for i in range(2)]
    wuslot = [U2.bitcast(BF16)[:, 4096 + 2048 * i: 4096 + 2048 * (i + 1)]
              for i in range(2)]
    yfm = U2.bitcast(BF16)[:, 8192:16384]
    wdslot = [U3.bitcast(BF16)[:, 8192 * i: 8192 * (i + 1)] for i in range(2)]
    hqd = u8v(U4, 0, 8192, F32)
    ab = u8v(U4, 8192, 16384, F32)
    ystg = u8v(U5, 0, 4096, BF16)
    sgf = u8v(U5, 4096, 6144, F32)
    scr = [u8v(U5, 6144 + 2048 * i, 8192 + 2048 * i, F32) for i in range(5)]
    m1, m2, M1, M2, t3b = scr
    b_buf = U6.bitcast(BF16)[:, 0:8192]
    c2 = u8v(U4, 0, 8192, F32)
    yln = u8v(U4, 8192, 16384, F32)
    outt = u8v(U4, 8192, 16384, F32)


    wg_r = d_wg.rearrange("(kc p) m -> p kc m", p=128)
    wu_r = d_wu.rearrange("(kc p) m -> p kc m", p=128)
    wd_r = d_wd.rearrange("(ic p) o -> p ic o", p=128)
    ysp_rr = d_ysp.rearrange("(tt p) f -> p tt f", p=128)

    wg_tok = [None, None]
    wu_tok = [None, None]
    wd_tok = None
    ystg_free = None
    yfm_pe_free = None
    ln2_prev = None
    out_prev = None
    hfm_free = None
    out_w = []
    t_ysp_all = ysp_w[-1]
    spar_busy = None
    prev_mult = None
    bevac_tok = {}

    for q in range(4):
        for tc in range(4):
            tcg = q * 4 + tc
            w = [ysp_w[tcg]]
            if ystg_free is not None:
                w.append(ystg_free)
            if q == 0 and tc == 0:
                w += [t_c_pe, t_c_act, t_c_dve, t_ysp_all]
            yld = cast_load(ystg, ysp_rr[:, tcg, :], slot("ystg"), waits=w)
            for fc in range(16):
                k = tc * 16 + fc
                bi = 6 + (k % 2)
                bank = psb(bi)[:, 0:128]
                tw = ([yld] if fc == 0 else []) + bg(bi)
                if k < 2 and yfm_pe_free is not None:
                    tw.append(yfm_pe_free)
                tpose(bank, ystg[:, fc * 128:(fc + 1) * 128], waits=tw, inc=s_pe)
                dve("tensor_copy",
                    yfm[:, fc * 512 + tc * 128: fc * 512 + tc * 128 + 128],
                    bank, waits=[at(s_pe)], inc=s_dve)
                bguard[bi] = at(s_dve)
            ystg_free = at(s_pe)
        t_yfm = at(s_dve)


        for qd in range(16):
            for j in range(4):
                ic = qd * 4 + j
                sl = ic % 2
                wgw = [wg_tok[sl]] if wg_tok[sl] is not None else \
                    ([t_b_pe, t_c_dve] if q == 0 else [])
                wuw = [wu_tok[sl]] if wu_tok[sl] is not None else \
                    ([t_b_pe, t_c_dve] if q == 0 else [])
                if q == 0:
                    lg = cast_load(wgslot[sl], wg_r0[:, :, ic * 128:(ic + 1) * 128],
                                   slot(f"wg{sl}"), waits=wgw)
                    lu = cast_load(wuslot[sl], wu_r0[:, :, ic * 128:(ic + 1) * 128],
                                   slot(f"wu{sl}"), waits=wuw)
                else:
                    lg = sync_dma(wgslot[sl], d_wgbf[ic], slot(f"wg{sl}"), waits=wgw)
                    lu = sync_dma(wuslot[sl], d_wubf[ic], slot(f"wu{sl}"), waits=wuw)
                pg = (ic % 2)
                pu = 6 + (ic % 2)
                for kc in range(16):
                    tw = ([lg, t_yfm] + bg(pg)) if kc == 0 else []
                    mm(wgslot[sl][:, kc * 128:(kc + 1) * 128],
                       yfm[:, kc * 512:(kc + 1) * 512], ps[pg],
                       start=(kc == 0), stop=(kc == 15), waits=tw,
                       inc=(s_pe if kc == 15 else None))
                t_g = at(s_pe)
                for kc in range(16):
                    tw = ([lu] + bg(pu)) if kc == 0 else []
                    mm(wuslot[sl][:, kc * 128:(kc + 1) * 128],
                       yfm[:, kc * 512:(kc + 1) * 512], ps[pu],
                       start=(kc == 0), stop=(kc == 15), waits=tw,
                       inc=(s_pe if kc == 15 else None))
                t_u = at(s_pe)
                aw = [t_g]
                if prev_mult is not None:
                    aw.append(prev_mult)
                act(sgf, ps[pg], AF.Silu, waits=aw, inc=s_act)
                bguard[pg] = at(s_act)
                mw = [at(s_act), t_u]
                dve("tensor_tensor", hqd[:, j * 512:(j + 1) * 512], sgf,
                    ps[pu], ALU.mult, waits=mw, inc=s_dve)
                prev_mult = at(s_dve)
                bguard[pu] = at(s_dve)
                if q == 0:
                    wg_tok[sl] = act_dma(d_wgbf[ic], wgslot[sl],
                                         slot(f"wgc{sl}"), waits=[t_g])
                    wu_tok[sl] = act_dma(d_wubf[ic], wuslot[sl],
                                         slot(f"wuc{sl}"), waits=[t_u])
                else:
                    wg_tok[sl] = t_g
                    wu_tok[sl] = t_u
            aw2 = [at(s_dve)]
            if q == 0 and qd == 0:
                aw2.append(t_c_act)
            act(ab, hqd, AF.Abs, waits=aw2, inc=s_act)
            t_abs = at(s_act)
            A_ = [hqd[:, j * 512:(j + 1) * 512] for j in range(4)]
            B_ = [ab[:, j * 512:(j + 1) * 512] for j in range(4)]
            sw = [t_abs]
            if q == 0 and qd == 0:
                sw.append(t_c_dve)
            dve("tensor_tensor", m1, B_[0], B_[1], ALU.min, waits=sw)
            dve("tensor_tensor", M1, B_[0], B_[1], ALU.max)
            dve("tensor_tensor", m2, B_[2], B_[3], ALU.min)
            dve("tensor_tensor", M2, B_[2], B_[3], ALU.max, inc=s_dve)
            dve("tensor_tensor", m1, m1, m2, ALU.max, waits=[at(s_dve)])
            dve("tensor_tensor", M1, M1, M2, ALU.min, inc=s_dve)
            dve("tensor_tensor", t3b, m1, M1, ALU.max, waits=[at(s_dve)],
                inc=s_dve)
            mbuf = [m1, m2, M1, M2]
            hw = [at(s_dve)]
            if hfm_free is not None and qd == 0:
                hw.append(hfm_free)
            for j in range(4):
                dve("tensor_tensor", mbuf[j], B_[j], t3b, ALU.is_ge,
                    waits=(hw if j == 0 else []),
                    inc=(s_dve if j == 3 else None))
            for j in range(4):
                dve("tensor_tensor",
                    h_fm[:, (qd * 4 + j) * 512:(qd * 4 + j + 1) * 512],
                    mbuf[j], A_[j], ALU.mult,
                    waits=([at(s_dve)] if j == 0 else []),
                    inc=(s_dve if j == 3 else None))
            spar_busy = at(s_dve)
        t_hfm = at(s_dve)
        yfm_pe_free = at(s_pe)
        if debug and q == 0:
            sync_dma(d_dbghfm, h_fm, s_misc, waits=[t_hfm])
            sync_dma(d_dbghq[:, 0:2048], hqd, s_misc, waits=[t_hfm])
            sync_dma(d_dbghq[:, 2048:4096], ab, s_misc)
            for si in range(5):
                dbghq_tok = sync_dma(d_dbgscr[:, si * 512:(si + 1) * 512],
                                     scr[si], s_misc)

        wtok_d = [None, None]
        for tp in range(2):
            for blk4 in range(16):
                sl = blk4 % 2
                w = [] if wtok_d[sl] is None else [wtok_d[sl]]
                if blk4 < 2 and tp == 0:
                    w.append(t_hfm)
                    if wd_tok is not None:
                        w.append(wd_tok)
                dst = wdslot[sl].rearrange("p (ic o) -> p ic o", ic=4)
                if q == 0 and tp == 0:
                    w = list(w) + [pc_done[4 + blk4]]
                ldw_ = sync_dma(dst,
                                d_wdbf[:, blk4 * 8192:(blk4 + 1) * 8192]
                                .rearrange("p (ic o) -> p ic o", ic=4),
                                slot(f"wdl{sl}"), waits=w)
                for u in range(4):
                    ic = blk4 * 4 + u
                    for ti in range(2):
                        tc = 2 * tp + ti
                        for osl in range(4):
                            bi = ti * 4 + osl
                            tw = []
                            if ic == 0 and osl == 0:
                                tw = [ldw_, t_hfm] + bg(bi)
                                if bevac_tok.get((tp, bi)) is not None:
                                    tw.append(bevac_tok[(tp, bi)])
                            elif ic == 0:
                                tw = bg(bi)
                                if bevac_tok.get((tp, bi)) is not None:
                                    tw.append(bevac_tok[(tp, bi)])
                            elif osl == 0 and u == 0:
                                tw = [ldw_]
                            mm(h_fm[:, ic * 512 + tc * 128: ic * 512 + tc * 128 + 128],
                               wdslot[sl][:, u * 2048 + osl * 512:
                                          u * 2048 + (osl + 1) * 512],
                               ps[bi], start=(ic == 0), stop=(ic == 63),
                               waits=tw,
                               inc=(s_pe if (u == 3 and bi == 7) else None))
                wtok_d[sl] = at(s_pe)
            t_dn = at(s_pe)
            bw = [t_dn]
            if q > 0 and tp == 0 and ln2_prev is not None:
                bw.append(ln2_prev)
            for ti in range(2):
                tc = 2 * tp + ti
                for osl in range(4):
                    bi = ti * 4 + osl
                    dve("tensor_copy",
                        b_buf[:, tc * 2048 + osl * 512: tc * 2048 + (osl + 1) * 512],
                        ps[bi], waits=(bw if (ti == 0 and osl == 0) else []),
                        inc=(s_dve if (ti == 1 and osl == 3) else None))
            bwr = at(s_dve)
            for ti in range(2):
                for osl in range(4):
                    bguard[ti * 4 + osl] = bwr
                    bevac_tok[(tp, ti * 4 + osl)] = bwr
        wd_tok = bwr
        hfm_free = at(s_pe)
        if debug and q == 0:
            dbgb_tok = sync_dma(d_dbgb, b_buf, s_misc, waits=[bwr])

        for tcq in range(4):
            tcg = q * 4 + tcq
            w = [t_ysp_all, bwr]
            if out_prev is not None:
                w.append(out_prev)
            if ln2_prev is not None:
                w.append(ln2_prev)
            if tcq == 0:
                w.append(spar_busy)
                if debug and q == 0:
                    w.append(dbghq_tok)
                    w.append(dbgb_tok)
            yld = sync_dma(yln, d_ysp[tcg * 128:(tcg + 1) * 128, :],
                           slot("yln"), waits=w)
            dve("scalar_tensor_tensor", c2, yln, 1.0,
                b_buf[:, tcq * 2048:(tcq + 1) * 2048], ALU.mult, ALU.add,
                accum_out=sb_st[:, 0:1], waits=[yld])
            dve("scalar_tensor_tensor", b_buf[:, tcq * 2048:(tcq + 1) * 2048],
                c2, 1.0, c2, ALU.mult, ALU.mult, accum_out=sb_st[:, 1:2],
                inc=s_dve)
            dve("tensor_scalar", sb_st[:, 2:3], sb_st[:, 0:1], 1.0 / 2048.0,
                None, ALU.mult, waits=[at(s_dve)])
            dve("tensor_scalar", sb_st[:, 3:4], sb_st[:, 1:2], 1.0 / 2048.0,
                None, ALU.mult, inc=s_dve)
            dve("tensor_mul", sb_st[:, 4:5], sb_st[:, 2:3], sb_st[:, 2:3],
                waits=[at(s_dve)], inc=s_dve)
            dve("tensor_sub", sb_st[:, 5:6], sb_st[:, 3:4], sb_st[:, 4:5],
                waits=[at(s_dve)], inc=s_dve)
            act(sb_st[:, 6:7], sb_st[:, 5:6], AF.Ln, bias=sb_eps, waits=[at(s_dve)])
            act(sb_st[:, 7:8], sb_st[:, 6:7], AF.Exp, scale=-0.5, inc=s_act)
            dve("tensor_scalar", outt, c2, sb_st[:, 2:3], sb_st[:, 7:8],
                ALU.subtract, ALU.mult, waits=[at(s_act)], inc=s_dve)
            ln2_prev = at(s_dve)
            out_prev = sync_dma(d_out[tcg * 128:(tcg + 1) * 128, :], outt,
                                s_out, waits=[ln2_prev])
            out_w.append(out_prev)

    prog.emit("sync", lambda e: e.nop(nofuse=True), waits=[at(s_out)])

    with nc.Block() as block:
        prog.run_in(block)
    stk.close()
    return nc


def host_constants():
    pos = np.arange(S, dtype=np.float64)
    j = np.arange(128) % 64
    inv = (1.0 / (1.0e6 ** (2.0 * j / 128.0)))[:, None]
    ang = pos[None, :] * inv
    return dict(
        ident=np.eye(128, dtype=ml_dtypes.bfloat16),
        identf=np.eye(128, dtype=np.float32),
        cosfm=np.cos(ang).astype(np.float32),
        sinfm=np.sin(ang).astype(np.float32),
        ones_bf=np.ones((1, 128), dtype=ml_dtypes.bfloat16),
        ones_f32=np.ones((1, 128), dtype=np.float32),
    )


def ffn_perm():
    d = np.arange(I)
    ic, p = d // 128, d % 128
    qd, j = ic // 4, ic % 4
    return (qd * 128 + p) * 4 + j


def make_in_maps(inputs):
    consts = host_constants()
    perm = ffn_perm()
    x = np.ascontiguousarray(np.asarray(inputs["L0_attn_hidden_states"]))
    shared = dict(
        wq=np.ascontiguousarray(np.asarray(inputs["L0_attn_Wq"])),
        wk=np.ascontiguousarray(np.asarray(inputs["L0_attn_Wk"])),
        wv=np.ascontiguousarray(np.asarray(inputs["L0_attn_Wv"])),
        wo=np.ascontiguousarray(np.asarray(inputs["L0_attn_Wo"])),
        wr=np.ascontiguousarray(np.asarray(inputs["L0_attn_Wr"])),
        wg=np.ascontiguousarray(np.asarray(inputs["L0_ffn_w_gate"])[:, perm]),
        wu=np.ascontiguousarray(np.asarray(inputs["L0_ffn_w_up"])[:, perm]),
        wd=np.ascontiguousarray(np.asarray(inputs["L0_ffn_w_down"])[perm, :]),
        bqfm=np.ascontiguousarray(
            np.asarray(inputs["L0_attn_bq"]).reshape(NH, 128).T),
        bkfm=np.ascontiguousarray(
            np.asarray(inputs["L0_attn_bk"]).reshape(NKV, 128).T),
        bv=np.ascontiguousarray(np.asarray(inputs["L0_attn_bv"]).reshape(1, -1)),
        bo=np.ascontiguousarray(np.asarray(inputs["L0_attn_bo"]).reshape(1, -1)),
        br=np.ascontiguousarray(np.asarray(inputs["L0_attn_br"]).reshape(NH, 1)),
        **consts)
    return [dict(x=np.ascontiguousarray(x[c]), **shared) for c in range(8)]


def kernel(**inputs):
    nc = build_program(debug=False)
    in_maps = make_in_maps(inputs)
    res = run_bass_kernel_spmd(nc, in_maps, core_ids=list(range(8)))
    return np.stack([np.asarray(res.results[c]["out"], dtype=np.float32)
                     for c in range(8)], axis=0)


# version 06
# speedup vs baseline: 1.1709x; 1.0231x over previous
import contextlib
import numpy as np
import ml_dtypes
import concourse.bass as bass
import concourse.mybir as mybir
from concourse.bass_utils import run_bass_kernel_spmd

F32 = mybir.dt.float32
BF16 = mybir.dt.bfloat16
AF = mybir.ActivationFunctionType
ALU = mybir.AluOpType

S, H, NH, NKV, D, I = 2048, 2048, 16, 8, 128, 8192
TOPK = 819
EPS = 1e-5
ISCALE = float(1.0 / np.sqrt(128.0))
CSHIFT = 25.0
BISECT_ITERS = 42

ENGINES = ("tensor", "vector", "scalar", "gpsimd", "sync")


class Sem:
    __slots__ = ("h", "n", "name")

    def __init__(self, h, name):
        self.h, self.n, self.name = h, 0, name


def at(s):
    return (s, s.n)


class Prog:
    def __init__(self):
        self.ops = {e: [] for e in ENGINES}

    def emit(self, eng, fn, waits=(), inc=None, incn=1):
        w = [(s.h, v) for (s, v) in waits if s is not None and v > 0]
        self.ops[eng].append((fn, w, None if inc is None else inc.h, incn))
        if inc is not None:
            inc.n += incn

    def run_in(self, block):
        for eng in ENGINES:
            ops = self.ops[eng]

            def body(engine, _ops=ops):
                for fn, waits, inch, incn in _ops:
                    for (sh, v) in waits:
                        engine.wait_ge(sh, v)
                    inst = fn(engine)
                    if inch is not None:
                        inst.then_inc(inch, incn)

            getattr(block, eng)(body)


def build_program(debug=False):
    nc = bass.Bass("TRN2", target_bir_lowering=False)
    prog = Prog()
    stk = contextlib.ExitStack()
    E = stk.enter_context

    d_x = nc.dram_tensor("x", [S, H], F32, kind="ExternalInput").ap()
    d_wq = nc.dram_tensor("wq", [H, NH * D], F32, kind="ExternalInput").ap()
    d_wk = nc.dram_tensor("wk", [H, NKV * D], F32, kind="ExternalInput").ap()
    d_wv = nc.dram_tensor("wv", [H, NKV * D], F32, kind="ExternalInput").ap()
    d_wo = nc.dram_tensor("wo", [NH * D, H], F32, kind="ExternalInput").ap()
    d_wr = nc.dram_tensor("wr", [H, NH], F32, kind="ExternalInput").ap()
    d_bqfm = nc.dram_tensor("bqfm", [128, NH], F32, kind="ExternalInput").ap()
    d_bkfm = nc.dram_tensor("bkfm", [128, NKV], F32, kind="ExternalInput").ap()
    d_bv = nc.dram_tensor("bv", [1, NKV * D], F32, kind="ExternalInput").ap()
    d_bo = nc.dram_tensor("bo", [1, H], F32, kind="ExternalInput").ap()
    d_br = nc.dram_tensor("br", [NH, 1], F32, kind="ExternalInput").ap()
    d_wg = nc.dram_tensor("wg", [H, I], F32, kind="ExternalInput").ap()
    d_wu = nc.dram_tensor("wu", [H, I], F32, kind="ExternalInput").ap()
    d_wd = nc.dram_tensor("wd", [I, H], F32, kind="ExternalInput").ap()
    d_ident = nc.dram_tensor("ident", [128, 128], BF16, kind="ExternalInput").ap()
    d_identf = nc.dram_tensor("identf", [128, 128], F32, kind="ExternalInput").ap()
    d_cos = nc.dram_tensor("cosfm", [128, S], F32, kind="ExternalInput").ap()
    d_sin = nc.dram_tensor("sinfm", [128, S], F32, kind="ExternalInput").ap()
    d_ones_bf = nc.dram_tensor("ones_bf", [1, 128], BF16, kind="ExternalInput").ap()
    d_ones_f32 = nc.dram_tensor("ones_f32", [1, 128], F32, kind="ExternalInput").ap()
    spk = "ExternalOutput" if debug else "Internal"
    d_ybf = nc.dram_tensor("ybf", [S, H], BF16).ap()
    d_qsp = nc.dram_tensor("qsp", [NH * D, S], BF16, kind=spk).ap()
    d_osp = nc.dram_tensor("osp", [NH * S, D], BF16, kind=spk).ap()
    d_ysp = nc.dram_tensor("ysp", [S, H], F32, kind=spk).ap()
    d_out = nc.dram_tensor("out", [S, H], F32, kind="ExternalOutput").ap()
    if debug:
        d_dbgz = nc.dram_tensor("dbgz", [NH, S], F32, kind="ExternalOutput").ap()
        d_dbgmb = nc.dram_tensor("dbgmb", [128, 256], F32, kind="ExternalOutput").ap()
        d_dbglo = nc.dram_tensor("dbglo", [NH, 3], F32, kind="ExternalOutput").ap()
        d_dbgzkm = nc.dram_tensor("dbgzkm", [128, 256], F32, kind="ExternalOutput").ap()
        d_dbgk = nc.dram_tensor("dbgk", [128, NKV * S], BF16, kind="ExternalOutput").ap()
        d_dbge = nc.dram_tensor("dbge", [128, 16 * 512], BF16, kind="ExternalOutput").ap()
        d_dbgpv = nc.dram_tensor("dbgpv", [128, 129], F32, kind="ExternalOutput").ap()
        d_dbga = nc.dram_tensor("dbga", [128, 8192], F32, kind="ExternalOutput").ap()
        d_dbghfm = nc.dram_tensor("dbghfm", [128, 64 * 512], BF16, kind="ExternalOutput").ap()
        d_dbgb = nc.dram_tensor("dbgb", [128, 4 * 2048], BF16, kind="ExternalOutput").ap()
        d_dbgct = nc.dram_tensor("dbgct", [128, 2048], F32, kind="ExternalOutput").ap()
        d_dbgst = nc.dram_tensor("dbgst", [128, 16], F32, kind="ExternalOutput").ap()
        d_dbgsq = nc.dram_tensor("dbgsq", [128, 2048], BF16, kind="ExternalOutput").ap()
        d_dbghq = nc.dram_tensor("dbghq", [128, 4096], F32, kind="ExternalOutput").ap()
        d_dbgscr = nc.dram_tensor("dbgscr", [128, 2560], F32, kind="ExternalOutput").ap()

    U1 = E(nc.sbuf_tensor("U1", [128, 65536], mybir.dt.uint8)).ap()
    U2 = E(nc.sbuf_tensor("U2", [128, 33024], mybir.dt.uint8)).ap()
    U3 = E(nc.sbuf_tensor("U3", [128, 32768], mybir.dt.uint8)).ap()
    U4 = E(nc.sbuf_tensor("U4", [128, 16384], mybir.dt.uint8)).ap()
    U5 = E(nc.sbuf_tensor("U5", [128, 16384], mybir.dt.uint8)).ap()
    U6 = E(nc.sbuf_tensor("U6", [128, 16384], mybir.dt.uint8)).ap()

    def u8v(U, b0, b1, dt):
        n = b1 - b0
        return U[:, b0:b1].bitcast(dt)

    sb_cos = E(nc.sbuf_tensor("sb_cos", [128, S], F32)).ap()
    sb_sin = E(nc.sbuf_tensor("sb_sin", [128, S], F32)).ap()
    sb_ident = E(nc.sbuf_tensor("sb_ident", [128, 128], BF16)).ap()
    sb_identf = E(nc.sbuf_tensor("sb_identf", [128, 128], F32)).ap()
    sb_ones_bf = E(nc.sbuf_tensor("sb_ones_bf", [1, 128], BF16)).ap()
    sb_ones_f32 = E(nc.sbuf_tensor("sb_ones_f32", [1, 128], F32)).ap()
    sb_bqfm = E(nc.sbuf_tensor("sb_bqfm", [128, NH], F32)).ap()
    sb_bkfm = E(nc.sbuf_tensor("sb_bkfm", [128, NKV], F32)).ap()
    sb_br = E(nc.sbuf_tensor("sb_br", [NH, 1], F32)).ap()
    sb_bv = E(nc.sbuf_tensor("sb_bv", [1, NKV * D], BF16)).ap()
    sb_zkm = E(nc.sbuf_tensor("sb_zkm", [128, 256], F32)).ap()
    sb_mb = E(nc.sbuf_tensor("sb_mb", [128, 256], F32)).ap()
    sb_lo = E(nc.sbuf_tensor("sb_lo", [NH, 1], F32)).ap()
    sb_hi = E(nc.sbuf_tensor("sb_hi", [NH, 1], F32)).ap()
    sb_mid = E(nc.sbuf_tensor("sb_mid", [NH, 1], F32)).ap()
    sb_cnt = E(nc.sbuf_tensor("sb_cnt", [NH, 1], F32)).ap()
    sb_gs = E(nc.sbuf_tensor("sb_gs", [NH, 1], F32)).ap()
    sb_t16 = E(nc.sbuf_tensor("sb_t16", [NH, 1], F32)).ap()
    sb_lor = E(nc.sbuf_tensor("sb_lor", [1, NH], F32)).ap()
    sb_lob = E(nc.sbuf_tensor("sb_lob", [128, NH], F32)).ap()
    sb_st = E(nc.sbuf_tensor("sb_st", [128, 16], F32)).ap()
    sb_rcp = E(nc.sbuf_tensor("sb_rcp", [128, 1], F32)).ap()
    sb_eps = E(nc.sbuf_tensor("sb_eps", [128, 1], F32)).ap()

    ps = [E(nc.psum_tensor(f"ps{i}", [128, 512], F32)).ap() for i in range(8)]

    def psb(i):
        return ps[i].bitcast(BF16)

    def mksem(name):
        return Sem(E(nc.semaphore(name)), name)

    s_pe, s_act, s_dve = mksem("s_pe"), mksem("s_act"), mksem("s_dve")
    s_pool = mksem("s_pool")
    s_out, s_misc = mksem("s_out"), mksem("s_misc")
    _dma_sems = [mksem(f"s_d{i}") for i in range(40)]
    _free = list(range(40))
    _slots = {}

    def slot(name):
        if name not in _slots:
            _slots[name] = _dma_sems[_free.pop(0)]
        return _slots[name]

    bguard = {}

    def bg(bi):
        return [bguard[bi]] if bi in bguard else []

    def cast_load(dst, src, sem, waits=()):
        prog.emit("gpsimd", lambda g, d=dst, s_=src: g.dma_start(d, s_),
                  waits=waits, inc=sem, incn=16)
        return at(sem)

    def sync_dma(dst, src, sem, waits=()):
        prog.emit("sync", lambda e, d=dst, s_=src: e.dma_start(d, s_),
                  waits=waits, inc=sem, incn=16)
        return at(sem)

    def act_dma(dst, src, sem, waits=()):
        prog.emit("scalar", lambda e, d=dst, s_=src: e.dma_start(d, s_),
                  waits=waits, inc=sem, incn=16)
        return at(sem)

    def mm(lhsT, rhs, out, start, stop, waits=(), inc=None):
        prog.emit("tensor",
                  lambda t, a=lhsT, b=rhs, o=out, s1=start, s2=stop:
                  t.matmul(o, a, b, start=s1, stop=s2), waits=waits, inc=inc)

    def tpose(out, in_, waits=(), inc=None):
        kdim = in_.shape[0]
        idt = sb_identf if in_.dtype == F32 else sb_ident
        prog.emit("tensor",
                  lambda t, o=out, i=in_, k=kdim, ii=idt:
                  t.transpose(o, i, ii[0:k, 0:k]),
                  waits=waits, inc=inc)

    def act(out, in_, func, bias=0.0, scale=1.0, accum_out=None, waits=(), inc=None):
        prog.emit("scalar",
                  lambda s, o=out, i=in_, f=func, b=bias, sc=scale, ao=accum_out:
                  s.activation(o, i, f, bias=b, scale=sc, accum_out=ao),
                  waits=waits, inc=inc)

    def dve(fname, *args, waits=(), inc=None, **kw):
        prog.emit("vector",
                  lambda v, f=fname, a=args, k=kw: getattr(v, f)(*a, **k),
                  waits=waits, inc=inc)

    def pool(fname, *args, waits=(), inc=None, **kw):
        prog.emit("gpsimd",
                  lambda g, f=fname, a=args, k=kw: getattr(g, f)(*a, **k),
                  waits=waits, inc=inc)

    c_tok = None
    for dst, src in ((sb_ident, d_ident), (sb_identf, d_identf), (sb_cos, d_cos), (sb_sin, d_sin),
                     (sb_ones_bf, d_ones_bf), (sb_ones_f32, d_ones_f32),
                     (sb_bqfm, d_bqfm), (sb_bkfm, d_bkfm), (sb_br, d_br)):
        c_tok = sync_dma(dst, src, s_misc)
    t_consts = c_tok
    dve("memset", sb_eps, EPS, inc=s_dve)

    x0_fm = U1.bitcast(BF16)
    stage = [u8v(U4, 0, 4096, BF16), u8v(U4, 4096, 8192, BF16)]
    x_r = d_x.rearrange("(tc p) f -> p tc f", p=128)
    stage_free = [None, None]
    ev_hist = []
    for tc in range(16):
        sl = tc % 2
        w = [t_consts] if tc == 0 else []
        if stage_free[sl] is not None:
            w.append(stage_free[sl])
        ld = cast_load(stage[sl], x_r[:, tc, :], slot(f"stage{sl}"), waits=w)
        for fc in range(16):
            k = tc * 16 + fc
            bi = 6 + (k % 2)
            bank = psb(bi)[:, 0:128]
            tw = ([ld] if fc == 0 else []) + bg(bi)
            tpose(bank, stage[sl][:, fc * 128:(fc + 1) * 128], waits=tw, inc=s_pe)
            dve("tensor_copy",
                x0_fm[:, fc * 2048 + tc * 128: fc * 2048 + tc * 128 + 128],
                bank, waits=[at(s_pe)], inc=s_dve)
            bguard[bi] = at(s_dve)
            ev_hist.append(at(s_dve))
        stage_free[sl] = at(s_pe)
    t_x0fm = at(s_dve)

    def x0c(kc, lo, n):
        return x0_fm[:, kc * 2048 + lo: kc * 2048 + lo + n]

    bvf = u8v(U4, 4096, 8192, F32)[0:1, 0:NKV * D]
    ld_bv = sync_dma(bvf, d_bv, s_misc, waits=[t_x0fm, stage_free[1]])
    act(sb_bv, bvf, AF.Identity, waits=[ld_bv], inc=s_act)
    t_bvbf = at(s_act)

    sb_z = u8v(U6, 8192, 16384, F32)[0:16, :]
    wr_sb = u8v(U5, 8192, 8704, BF16)
    ld_wr = cast_load(wr_sb.rearrange("p (kc m) -> p kc m", kc=16),
                      d_wr.rearrange("(kc p) m -> p kc m", p=128), slot("wr"))
    for qt in range(4):
        bi = 2 + qt % 2
        zps = ps[bi][0:16, :]
        for kc in range(16):
            tw = []
            if kc == 0:
                tw = [ld_wr, t_x0fm] + bg(bi)
            mm(wr_sb[:, kc * 16:(kc + 1) * 16], x0c(kc, qt * 512, 512), zps,
               start=(kc == 0), stop=(kc == 15), waits=tw,
               inc=(s_pe if kc == 15 else None))
        act(sb_z[:, qt * 512:(qt + 1) * 512], zps, AF.Identity, bias=sb_br,
            waits=[at(s_pe)], inc=s_act)
        bguard[bi] = at(s_act)
    t_z = at(s_act)
    t_z_pe = at(s_pe)

    zge = u8v(U4, 0, 4096, BF16)[0:16, :]
    pool("memset", sb_lo, -32.0, waits=[t_z, stage_free[0]])
    pool("memset", sb_hi, 32.0, inc=s_pool)
    _bis = {"it": 0, "ptok": at(s_pool), "dtok": None}

    def emit_bisect_iter():
        if _bis["it"] >= BISECT_ITERS:
            return
        _bis["it"] += 1
        pw = [] if _bis["dtok"] is None else [_bis["dtok"]]
        pool("tensor_add", sb_mid, sb_lo, sb_hi, waits=pw)
        pool("tensor_scalar", sb_mid, sb_mid, 0.5, None, ALU.mult, inc=s_pool)
        ptok = at(s_pool)
        dve("tensor_scalar", zge, sb_z, sb_mid, None, ALU.is_ge, ALU.add,
            accum_out=sb_cnt, waits=[ptok], inc=s_dve)
        _bis["dtok"] = at(s_dve)
        pool("tensor_scalar", sb_gs, sb_cnt, float(TOPK) - 0.5, None, ALU.is_ge,
             waits=[_bis["dtok"]])
        pool("tensor_sub", sb_t16, sb_mid, sb_lo)
        pool("tensor_mul", sb_t16, sb_t16, sb_gs)
        pool("tensor_add", sb_lo, sb_lo, sb_t16)
        pool("tensor_sub", sb_t16, sb_hi, sb_mid)
        pool("tensor_mul", sb_t16, sb_t16, sb_gs)
        pool("tensor_add", sb_hi, sb_mid, sb_t16, inc=s_pool)
        _bis["ptok"] = at(s_pool)

    def emit_maskbias():
        t_bis = _bis["ptok"]
        mask01 = u8v(U4, 8192, 16384, F32)[0:16, :]
        dve("tensor_scalar", mask01, sb_z, sb_lo, None, ALU.is_ge,
            waits=[t_bis, t_z, stage_free[1]], inc=s_dve)
        t_m01 = at(s_dve)
        for kc in range(16):
            bi = 4 + kc % 2
            bank = ps[bi][:, 0:16]
            tw = ([t_m01] if kc <= 1 else []) + bg(bi)
            tpose(bank, mask01[:, kc * 128:(kc + 1) * 128], waits=tw, inc=s_pe)
            dve("tensor_copy", sb_zkm[:, kc * 16:(kc + 1) * 16], bank,
                waits=[at(s_pe)], inc=s_dve)
            bguard[bi] = at(s_dve)
        dve("tensor_scalar", sb_mb, sb_zkm, 10000.0, 10000.0 + CSHIFT,
            ALU.mult, ALU.subtract, inc=s_dve)
        return at(s_dve), at(s_pe)

    rope_q = [u8v(U6, 0, 2048, F32), u8v(U6, 4096, 6144, F32)]
    rope_r = [u8v(U6, 2048, 4096, F32), u8v(U6, 6144, 8192, F32)]
    Qslot = [u8v(U6, 8192, 12288, BF16), u8v(U6, 12288, 16384, BF16)]
    rope_i = [0]
    rope_busy = {}

    def rope_block(dst_bf, bank, bias_col, qt, extra=()):
        i = rope_i[0] % 2
        rope_i[0] += 1
        q, r = rope_q[i], rope_r[i]
        cs = sb_cos[:, qt * 512:(qt + 1) * 512]
        sn = sb_sin[:, qt * 512:(qt + 1) * 512]
        aw = [at(s_pe)] + list(extra)
        if i in rope_busy:
            aw.append(rope_busy[i])
        act(q, bank, AF.Identity, bias=bias_col, waits=aw)
        act(r[0:64, :], q[64:128, :], AF.Identity, scale=-1.0)
        act(r[64:128, :], q[0:64, :], AF.Identity, inc=s_act)
        a_tok = at(s_act)
        dve("tensor_mul", q, q, cs, waits=[a_tok])
        dve("tensor_mul", r, r, sn, inc=s_dve)
        dve("tensor_add", dst_bf, q, r, waits=[at(s_dve)], inc=s_dve)
        d_tok = at(s_dve)
        rope_busy[i] = d_tok
        return d_tok, a_tok

    K_fm = U3.bitcast(BF16)
    wslot = [u8v(U5, 0, 4096, BF16), u8v(U5, 4096, 8192, BF16)]
    wk_r = d_wk.rearrange("(kc p) m -> p kc m", p=128)
    wq_r = d_wq.rearrange("(kc p) m -> p kc m", p=128)
    wtok = [None, None]
    dtok = None
    for g in range(NKV):
        sl = g % 2
        w = [wtok[sl]] if wtok[sl] is not None else []
        ldk = cast_load(wslot[sl], wk_r[:, :, g * 128:(g + 1) * 128],
                        slot(f"wkq{sl}"), waits=w)
        for qt in range(4):
            bi = (g * 4 + qt) % 2
            for kc in range(16):
                tw = ([ldk, t_x0fm] + bg(bi)) if kc == 0 else []
                mm(wslot[sl][:, kc * 128:(kc + 1) * 128], x0c(kc, qt * 512, 512),
                   ps[bi], start=(kc == 0), stop=(kc == 15), waits=tw,
                   inc=(s_pe if kc == 15 else None))
            dtok, atok = rope_block(
                K_fm[:, g * 2048 + qt * 512: g * 2048 + (qt + 1) * 512],
                ps[bi], sb_bkfm[:, g:g + 1], qt)
            bguard[bi] = atok
            emit_bisect_iter()
        wtok[sl] = dtok

    V_aug = U2.bitcast(BF16)[:, 0:16 * 1032]
    v4 = V_aug.rearrange("p (tc g e) -> p tc g e", tc=16, g=8, e=129)
    dve("memset", v4[:, :, :, 128:129], 1.0, inc=s_dve)
    t_vones = at(s_dve)

    wv_r = d_wv.rearrange("(kc p) m -> p kc m", p=128)
    vw = [u8v(U5, 8192, 12288, BF16), u8v(U5, 12288, 16384, BF16)]
    vtok = [None, None]
    vev = None
    for vh in range(NKV):
        sl = vh % 2
        w = [vtok[sl]] if vtok[sl] is not None else [t_bvbf, t_z_pe]
        ldv = cast_load(vw[sl], wv_r[:, :, vh * 128:(vh + 1) * 128],
                        slot(f"wv{sl}"), waits=w)
        for tc in range(16):
            bi = 2 + (vh * 16 + tc) % 2
            for kc in range(16):
                tw = ([ldv, t_vones] + bg(bi)) if kc == 0 else []
                mm(x0c(kc, tc * 128, 128), vw[sl][:, kc * 128:(kc + 1) * 128],
                   ps[bi][:, 0:128], start=(kc == 0), stop=False, waits=tw)
            mm(sb_ones_bf, sb_bv[0:1, vh * 128:(vh + 1) * 128],
               ps[bi][:, 0:128], start=False, stop=True, inc=s_pe)
            col = tc * 1032 + vh * 129
            act(V_aug[:, col:col + 128], ps[bi][:, 0:128], AF.Identity,
                waits=[at(s_pe)], inc=s_act)
            bguard[bi] = at(s_act)
            vev = at(s_act)
            if tc % 4 == 3:
                emit_bisect_iter()
        vtok[sl] = vev
    t_vaug = vev
    while _bis["it"] < BISECT_ITERS:
        emit_bisect_iter()

    t_mb, t_mb_pe = emit_maskbias()
    if debug:
        dbg_tok = sync_dma(d_dbgz, sb_z, s_misc, waits=[t_z, t_mb])
        prog.emit("vector", lambda v: v.tensor_copy(sb_st[0:16, 8:9], sb_lo),
                  waits=[t_mb])
        prog.emit("vector", lambda v: v.tensor_copy(sb_st[0:16, 9:10], sb_hi))
        prog.emit("vector", lambda v: v.tensor_copy(sb_st[0:16, 10:11], sb_cnt),
                  inc=s_dve)
        sync_dma(d_dbglo, sb_st[0:16, 8:11], s_misc, waits=[at(s_dve)])
        sync_dma(d_dbgzkm, sb_zkm, s_misc)
        sync_dma(d_dbgmb, sb_mb, s_misc)
        t_mb = at(s_misc)
        prog.emit("vector", lambda v: v.tensor_copy(sb_t16, sb_lo),
                  waits=[at(s_misc)], inc=s_dve)
        t_mb = at(s_dve)

    qsp_w = []
    qtok = [None, None]
    qspill = [None, None]
    for h in range(NH):
        sl = h % 2
        w = [qtok[sl]] if qtok[sl] is not None else []
        ldq = cast_load(wslot[sl], wq_r[:, :, h * 128:(h + 1) * 128],
                        slot(f"wkq{sl}"), waits=w)
        for qt in range(4):
            bi = (h * 4 + qt) % 2
            for kc in range(16):
                tw = []
                if kc == 0:
                    tw = [ldq] + bg(bi)
                    if qt == 0 and qspill[sl] is not None:
                        tw.append(qspill[sl])
                    if h == 0 and qt == 0:
                        tw.append(t_mb_pe)
                mm(wslot[sl][:, kc * 128:(kc + 1) * 128], x0c(kc, qt * 512, 512),
                   ps[bi], start=(kc == 0), stop=(kc == 15), waits=tw,
                   inc=(s_pe if kc == 15 else None))
            ex = [t_mb] if (h == 0 and qt == 0) else []
            dtok, atok = rope_block(Qslot[sl][:, qt * 512:(qt + 1) * 512],
                                    ps[bi], sb_bqfm[:, h:h + 1], qt, extra=ex)
            bguard[bi] = atok
        qspill[sl] = sync_dma(d_qsp[h * 128:(h + 1) * 128, :], Qslot[sl],
                              slot(f"qsp{sl}"), waits=[dtok])
        qsp_w.append(qspill[sl])
        qtok[sl] = dtok
    t_a_pe = at(s_pe)
    t_a_dve = at(s_dve)
    if debug:
        sync_dma(d_dbgk, K_fm, s_misc, waits=[t_a_dve])


    d_wgbf = nc.dram_tensor("wgbf", [64, 128, 2048], BF16).ap()
    d_wubf = nc.dram_tensor("wubf", [64, 128, 2048], BF16).ap()
    d_wdbf = nc.dram_tensor("wdbf", [128, 64 * 2048], BF16).ap()
    d_wobf = nc.dram_tensor("wobf", [4, 128, 16 * 512], BF16).ap()
    wg_r0 = d_wg.rearrange("(kc p) m -> p kc m", p=128)
    wu_r0 = d_wu.rearrange("(kc p) m -> p kc m", p=128)
    wd_r0 = d_wd.rearrange("(ic p) o -> p ic o", p=128)
    wo_r0 = d_wo.rearrange("(kc p) o -> p kc o", p=128)
    pc_stage = [U1.bitcast(BF16)[:, 16384 + 8192 * i: 16384 + 8192 * (i + 1)]
                for i in range(2)]
    pc_tok = [None, None]
    _pc_jobs = []
    for osl in range(4):
        _pc_jobs.append((wo_r0[:, :, osl * 512:(osl + 1) * 512],
                         d_wobf[osl], 8192))
    for b4 in range(16):
        _pc_jobs.append((wd_r0[:, b4 * 4:(b4 + 1) * 4, :],
                         d_wdbf[:, b4 * 8192:(b4 + 1) * 8192], 8192))
    _pc_i = [0]
    pc_done = {}

    def emit_precache(n):
        for _ in range(n):
            i = _pc_i[0]
            if i >= len(_pc_jobs):
                return
            _pc_i[0] += 1
            srcap, dstap, nel = _pc_jobs[i]
            sl2 = i % 2
            w = [] if pc_tok[sl2] is None else [pc_tok[sl2]]
            if i < 2:
                w.append(t_a_pe)
                w.append(t_a_dve)
            t1 = cast_load(pc_stage[sl2][:, 0:nel], srcap, slot(f"pcl{sl2}"),
                           waits=w)
            pc_tok[sl2] = act_dma(dstap, pc_stage[sl2][:, 0:nel],
                                  slot(f"pcs{sl2}"), waits=[t1])
            pc_done[i] = pc_tok[sl2]

    ebuf = [U1.bitcast(BF16)[:, 8192 * i: 8192 * (i + 1)] for i in range(2)]
    Oslot = [u8v(U4, 0, 4096, BF16), u8v(U4, 4096, 8192, BF16)]
    Ofm = [u8v(U4, 8192, 12288, BF16), u8v(U4, 12288, 16384, BF16)]
    Qb = [u8v(U5, 0, 4096, BF16), u8v(U5, 4096, 8192, BF16)]
    e_free = [None, None]
    osl_free = [None, None]
    ofm_free = [None, None]
    qldw = [None, None]
    osp_w = []
    pend = []
    t_exp_of = {}

    def emit_scores(h, qt, qld):
        gi = h * 4 + qt
        par = gi % 2
        g = h // 2
        for kc in range(16):
            bi = 2 + ((gi * 16 + kc) % 3)
            tw = bg(bi)
            if kc == 0:
                tw = tw + [qld, t_mb]
                if gi == 0:
                    tw += [t_a_pe, t_a_dve]
                if e_free[par] is not None:
                    tw.append(e_free[par])
            mm(K_fm[:, g * 2048 + kc * 128: g * 2048 + (kc + 1) * 128],
               Qb[h % 2][:, qt * 512:(qt + 1) * 512], ps[bi],
               start=True, stop=True, waits=tw, inc=s_pe)
            act(ebuf[par][:, kc * 512:(kc + 1) * 512], ps[bi], AF.Exp,
                bias=sb_mb[:, kc * 16 + h: kc * 16 + h + 1], scale=ISCALE,
                waits=[at(s_pe)], inc=s_act)
            bguard[bi] = at(s_act)
        t_exp_of[gi] = at(s_act)

    def emit_pv(gi):
        h, qt = gi // 4, gi % 4
        par = gi % 2
        g = h // 2
        hs = h % 2
        for q2 in range(4):
            qoff = q2 * 128
            bi = 4 + ((gi * 4 + q2) % 4)
            for kc in range(16):
                tw = []
                if kc == 0:
                    tw = [t_exp_of[gi]] + bg(bi)
                    if gi == 0 and q2 == 0:
                        tw.append(t_vaug)
                mm(ebuf[par][:, kc * 512 + qoff: kc * 512 + qoff + 128],
                   V_aug[:, kc * 1032 + g * 129: kc * 1032 + (g + 1) * 129],
                   ps[bi][:, 0:129], start=(kc == 0), stop=(kc == 15),
                   waits=tw, inc=(s_pe if kc == 15 else None))
            t_pv = at(s_pe)
            dw = [t_pv]
            if qt == 0 and q2 == 0 and osl_free[hs] is not None:
                dw.append(osl_free[hs])
            dve("reciprocal", sb_rcp, ps[bi][:, 128:129], waits=dw,
                inc=s_dve)
            dve("tensor_scalar",
                Oslot[hs][:, (qt * 4 + q2) * 128:(qt * 4 + q2 + 1) * 128],
                ps[bi][:, 0:128], sb_rcp, None, ALU.mult,
                waits=[at(s_dve)], inc=s_dve)
            bguard[bi] = at(s_dve)
        e_free[par] = at(s_pe)

    def emit_ohead(h):
        hs = h % 2
        t_o = at(s_dve)
        emit_precache(2)
        tok = sync_dma(
            d_osp[h * 2048:(h + 1) * 2048, :].rearrange("(b p) d -> p b d", p=128),
            Oslot[hs].rearrange("p (b d) -> p b d", b=16),
            slot(f"osp{hs}"), waits=[t_o])
        osl_free[hs] = tok
        ofm_free[hs] = tok
        osp_w.append(tok)
        qldw[hs] = at(s_pe)

    for h in range(NH):
        hs = h % 2
        w = [qsp_w[h]]
        if qldw[hs] is not None:
            w.append(qldw[hs])
        if h < 2:
            w.append(t_a_pe)
        qld = sync_dma(Qb[hs], d_qsp[h * 128:(h + 1) * 128, :],
                       slot(f"qb{hs}"), waits=w)
        for qt in range(4):
            gi = h * 4 + qt
            emit_scores(h, qt, qld)
            if pend:
                pgi = pend.pop(0)
                emit_pv(pgi)
                if pgi % 4 == 3:
                    emit_ohead(pgi // 4)
            pend.append(gi)
    while pend:
        pgi = pend.pop(0)
        emit_pv(pgi)
        if pgi % 4 == 3:
            emit_ohead(pgi // 4)
    t_b_pe = at(s_pe)
    t_b_dve = at(s_dve)
    t_b_act = at(s_act)
    if debug:
        sync_dma(d_dbge, ebuf[(NH * 4 - 1) % 2], s_misc, waits=[t_b_pe, t_b_act])

    Wo_sb = U1.bitcast(BF16)
    wo_tok = []
    emit_precache(1000)
    for osl in range(4):
        dst = Wo_sb.rearrange("p (kc o) -> p kc o", kc=16)[:, :,
                                                           osl * 512:(osl + 1) * 512]
        wo_tok.append(sync_dma(
            dst, d_wobf[osl].rearrange("p (kc o) -> p kc o", kc=16),
            slot(f"wo{osl}"),
            waits=[t_b_pe, t_b_act, pc_done[len(_pc_jobs) - 1],
                   pc_done[len(_pc_jobs) - 2]]))
    bof = u8v(U6, 0, 8192, F32)[0:1, :]
    sb_bo = u8v(U6, 8192, 12288, BF16)[0:1, :]
    ld_bo = sync_dma(bof, d_bo, s_misc, waits=[t_b_pe])
    act(sb_bo, bof, AF.Identity, waits=[ld_bo], inc=s_act)
    t_bobf = at(s_act)

    Ofm_sb = [U3.bitcast(BF16)[:, 8192 * i: 8192 * (i + 1)] for i in range(2)]
    a_buf = U2.bitcast(F32)[:, 0:2048 * 4]
    x0t = [u8v(U4, 0, 8192, F32), u8v(U4, 8192, 16384, F32)]
    ctile = u8v(U5, 0, 8192, F32)
    ytile = u8v(U5, 8192, 16384, F32)
    sqdummy = u8v(U6, 12288, 16384, BF16)
    ybt = u8v(U6, 0, 4096, BF16)

    ofmq_tok = [None, None]
    quarter_pe = []
    x0_free = [None, None]
    ysp_w = []
    ybf_w = []
    ln1_prev = None
    cstt_tok = [None, None, None, None]
    for q in range(4):
        qs = q % 2
        w = [t_b_pe, osp_w[-1]]
        if q >= 2:
            w.append(quarter_pe[q - 2])
        ofml = None
        for hc in range(16):
            prog.emit("sync",
                      lambda e, d=Ofm_sb[qs][:, hc * 512:(hc + 1) * 512],
                      s_=d_osp[hc * 2048 + q * 512: hc * 2048 + (q + 1) * 512, :]:
                      e.dma_start(d, s_, transpose=True),
                      waits=(w if hc == 0 else []), inc=slot(f"ofm{qs}"), incn=16)
        ofml = at(slot(f"ofm{qs}"))
        ofmq_tok[qs] = ofml
        for osl in range(4):
            for tc in range(4):
                bi = (osl * 4 + tc) % 2
                for hc in range(16):
                    tw = []
                    if hc == 0:
                        tw = [ofmq_tok[qs], wo_tok[osl], t_bobf] + bg(bi)
                        if cstt_tok[tc] is not None:
                            tw.append(cstt_tok[tc])
                    mm(Ofm_sb[qs][:, hc * 512 + tc * 128: hc * 512 + tc * 128 + 128],
                       Wo_sb[:, hc * 2048 + osl * 512: hc * 2048 + (osl + 1) * 512],
                       ps[bi], start=(hc == 0), stop=False, waits=tw)
                mm(sb_ones_bf, sb_bo[0:1, osl * 512:(osl + 1) * 512],
                   ps[bi], start=False, stop=True, inc=s_pe)
                act(a_buf[:, tc * 2048 + osl * 512: tc * 2048 + (osl + 1) * 512],
                    ps[bi], AF.Identity, waits=[at(s_pe)], inc=s_act)
                bguard[bi] = at(s_act)
        quarter_pe.append(at(s_pe))
        t_aq = at(s_act)
        if debug and q == 0:
            sync_dma(d_dbga, a_buf, s_misc, waits=[t_aq])
        for tc in range(4):
            tcg = q * 4 + tc
            xs = tc % 2
            xw = [t_aq]
            if x0_free[xs] is not None:
                xw.append(x0_free[xs])
            if q == 0 and tc < 2:
                xw.append(osp_w[-1])
            xld = sync_dma(x0t[xs], d_x[tcg * 128:(tcg + 1) * 128, :],
                           slot(f"x0t{xs}"), waits=xw)
            cw = [xld]
            if ln1_prev is not None:
                cw.append(ln1_prev)
            if ysp_w:
                cw.append(ysp_w[-1])
            dve("scalar_tensor_tensor", ctile, x0t[xs], 1.0,
                a_buf[:, tc * 2048:(tc + 1) * 2048], ALU.mult, ALU.add,
                accum_out=sb_st[:, 0:1], waits=cw, inc=s_dve)
            cstt_tok[tc] = at(s_dve)
            if debug and q == 0 and tc == 0:
                dbgct_tok = sync_dma(d_dbgct, ctile, s_misc, waits=[cstt_tok[0]])
            dve("scalar_tensor_tensor", sqdummy, ctile, 1.0, ctile,
                ALU.mult, ALU.mult, accum_out=sb_st[:, 1:2], inc=s_dve)
            if debug and q == 0 and tc == 0:
                sync_dma(d_dbgsq, sqdummy, s_misc, waits=[at(s_dve)])
            dve("tensor_scalar", sb_st[:, 2:3], sb_st[:, 0:1], 1.0 / 2048.0,
                None, ALU.mult, waits=[at(s_dve)])
            dve("tensor_scalar", sb_st[:, 3:4], sb_st[:, 1:2], 1.0 / 2048.0,
                None, ALU.mult, inc=s_dve)
            dve("tensor_mul", sb_st[:, 4:5], sb_st[:, 2:3], sb_st[:, 2:3],
                waits=[at(s_dve)], inc=s_dve)
            dve("tensor_sub", sb_st[:, 5:6], sb_st[:, 3:4], sb_st[:, 4:5],
                waits=[at(s_dve)], inc=s_dve)
            act(sb_st[:, 6:7], sb_st[:, 5:6], AF.Ln, bias=sb_eps, waits=[at(s_dve)])
            act(sb_st[:, 7:8], sb_st[:, 6:7], AF.Exp, scale=-0.5, inc=s_act)
            yw = [at(s_act)]
            if debug and q == 0 and tc == 0:
                sync_dma(d_dbgst, sb_st, s_misc, waits=[at(s_act)])
                yw.append(at(s_misc))
                yw.append(dbgct_tok)
            dve("tensor_scalar", ytile, ctile, sb_st[:, 2:3], sb_st[:, 7:8],
                ALU.subtract, ALU.mult, waits=yw, inc=s_dve)
            ln1_prev = at(s_dve)
            tok = sync_dma(d_ysp[tcg * 128:(tcg + 1) * 128, :], ytile,
                           slot(f"ysp{tcg % 2}"), waits=[ln1_prev])
            ysp_w.append(tok)
            ybw = [ln1_prev]
            if ybf_w:
                ybw.append(ybf_w[-1])
            act(ybt, ytile, AF.Identity, waits=ybw, inc=s_act)
            ybf_w.append(act_dma(d_ybf[tcg * 128:(tcg + 1) * 128, :], ybt,
                                 slot("ybf"), waits=[at(s_act)]))
            x0_free[xs] = ln1_prev
    t_c_pe = at(s_pe)
    t_c_dve = at(s_dve)
    t_c_act = at(s_act)

    h_fm = U1.bitcast(BF16)
    wgslot = [U2.bitcast(BF16)[:, 2048 * i: 2048 * (i + 1)] for i in range(2)]
    wuslot = [U2.bitcast(BF16)[:, 4096 + 2048 * i: 4096 + 2048 * (i + 1)]
              for i in range(2)]
    yfm = U2.bitcast(BF16)[:, 8192:16384]
    wdslot = [U3.bitcast(BF16)[:, 8192 * i: 8192 * (i + 1)] for i in range(2)]
    hqd = u8v(U4, 0, 8192, F32)
    ab = u8v(U4, 8192, 16384, F32)
    ystg = u8v(U5, 0, 4096, BF16)
    sgf = u8v(U5, 4096, 6144, F32)
    scr = [u8v(U5, 6144 + 2048 * i, 8192 + 2048 * i, F32) for i in range(5)]
    m1, m2, M1, M2, t3b = scr
    b_buf = U6.bitcast(BF16)[:, 0:8192]
    c2 = u8v(U4, 0, 8192, F32)
    yln = u8v(U4, 8192, 16384, F32)
    outt = u8v(U4, 8192, 16384, F32)


    wg_r = d_wg.rearrange("(kc p) m -> p kc m", p=128)
    wu_r = d_wu.rearrange("(kc p) m -> p kc m", p=128)
    wd_r = d_wd.rearrange("(ic p) o -> p ic o", p=128)
    ysp_rr = d_ysp.rearrange("(tt p) f -> p tt f", p=128)

    wg_tok = [None, None]
    wu_tok = [None, None]
    wd_tok = None
    ystg_free = None
    yfm_pe_free = None
    ln2_prev = None
    out_prev = None
    hfm_free = None
    out_w = []
    t_ysp_all = ysp_w[-1]
    spar_busy = None
    prev_mult = None
    bevac_tok = {}

    for q in range(4):
        for kc in range(16):
            w = [ybf_w[-1]]
            if kc == 0:
                if yfm_pe_free is not None:
                    w.append(yfm_pe_free)
                if ln2_prev is not None:
                    w.append(ln2_prev)
                if q == 0:
                    w += [t_c_pe, t_c_act, t_c_dve]
            prog.emit("sync",
                      lambda e, d=yfm[:, kc * 512:(kc + 1) * 512],
                      s_=d_ybf[q * 512:(q + 1) * 512, kc * 128:(kc + 1) * 128]:
                      e.dma_start(d, s_, transpose=True),
                      waits=w, inc=slot("yfmld"), incn=16)
        t_yfm = at(slot("yfmld"))


        for qd in range(16):
            for j in range(4):
                ic = qd * 4 + j
                sl = ic % 2
                wgw = [wg_tok[sl]] if wg_tok[sl] is not None else \
                    ([t_b_pe, t_c_dve] if q == 0 else [])
                wuw = [wu_tok[sl]] if wu_tok[sl] is not None else \
                    ([t_b_pe, t_c_dve] if q == 0 else [])
                if q == 0:
                    lg = cast_load(wgslot[sl], wg_r0[:, :, ic * 128:(ic + 1) * 128],
                                   slot(f"wg{sl}"), waits=wgw)
                    lu = cast_load(wuslot[sl], wu_r0[:, :, ic * 128:(ic + 1) * 128],
                                   slot(f"wu{sl}"), waits=wuw)
                else:
                    lg = sync_dma(wgslot[sl], d_wgbf[ic], slot(f"wg{sl}"), waits=wgw)
                    lu = sync_dma(wuslot[sl], d_wubf[ic], slot(f"wu{sl}"), waits=wuw)
                pg = (ic % 2)
                pu = 6 + (ic % 2)
                for kc in range(16):
                    tw = ([lg, t_yfm] + bg(pg)) if kc == 0 else []
                    mm(wgslot[sl][:, kc * 128:(kc + 1) * 128],
                       yfm[:, kc * 512:(kc + 1) * 512], ps[pg],
                       start=(kc == 0), stop=(kc == 15), waits=tw,
                       inc=(s_pe if kc == 15 else None))
                t_g = at(s_pe)
                for kc in range(16):
                    tw = ([lu] + bg(pu)) if kc == 0 else []
                    mm(wuslot[sl][:, kc * 128:(kc + 1) * 128],
                       yfm[:, kc * 512:(kc + 1) * 512], ps[pu],
                       start=(kc == 0), stop=(kc == 15), waits=tw,
                       inc=(s_pe if kc == 15 else None))
                t_u = at(s_pe)
                aw = [t_g]
                if prev_mult is not None:
                    aw.append(prev_mult)
                act(sgf, ps[pg], AF.Silu, waits=aw, inc=s_act)
                bguard[pg] = at(s_act)
                mw = [at(s_act), t_u]
                dve("tensor_tensor", hqd[:, j * 512:(j + 1) * 512], sgf,
                    ps[pu], ALU.mult, waits=mw, inc=s_dve)
                prev_mult = at(s_dve)
                bguard[pu] = at(s_dve)
                if q == 0:
                    wg_tok[sl] = act_dma(d_wgbf[ic], wgslot[sl],
                                         slot(f"wgc{sl}"), waits=[t_g])
                    wu_tok[sl] = act_dma(d_wubf[ic], wuslot[sl],
                                         slot(f"wuc{sl}"), waits=[t_u])
                else:
                    wg_tok[sl] = t_g
                    wu_tok[sl] = t_u
            aw2 = [at(s_dve)]
            if q == 0 and qd == 0:
                aw2.append(t_c_act)
            act(ab, hqd, AF.Abs, waits=aw2, inc=s_act)
            t_abs = at(s_act)
            A_ = [hqd[:, j * 512:(j + 1) * 512] for j in range(4)]
            B_ = [ab[:, j * 512:(j + 1) * 512] for j in range(4)]
            sw = [t_abs]
            if q == 0 and qd == 0:
                sw.append(t_c_dve)
            dve("tensor_tensor", m1, B_[0], B_[1], ALU.min, waits=sw)
            dve("tensor_tensor", M1, B_[0], B_[1], ALU.max)
            dve("tensor_tensor", m2, B_[2], B_[3], ALU.min)
            dve("tensor_tensor", M2, B_[2], B_[3], ALU.max, inc=s_dve)
            dve("tensor_tensor", m1, m1, m2, ALU.max, waits=[at(s_dve)])
            dve("tensor_tensor", M1, M1, M2, ALU.min, inc=s_dve)
            dve("tensor_tensor", t3b, m1, M1, ALU.max, waits=[at(s_dve)],
                inc=s_dve)
            mbuf = [m1, m2, M1, M2]
            hw = [at(s_dve)]
            if hfm_free is not None and qd == 0:
                hw.append(hfm_free)
            for j in range(4):
                dve("tensor_tensor", mbuf[j], B_[j], t3b, ALU.is_ge,
                    waits=(hw if j == 0 else []),
                    inc=(s_dve if j == 3 else None))
            for j in range(4):
                dve("tensor_tensor",
                    h_fm[:, (qd * 4 + j) * 512:(qd * 4 + j + 1) * 512],
                    mbuf[j], A_[j], ALU.mult,
                    waits=([at(s_dve)] if j == 0 else []),
                    inc=(s_dve if j == 3 else None))
            spar_busy = at(s_dve)
        t_hfm = at(s_dve)
        yfm_pe_free = at(s_pe)
        if debug and q == 0:
            sync_dma(d_dbghfm, h_fm, s_misc, waits=[t_hfm])
            sync_dma(d_dbghq[:, 0:2048], hqd, s_misc, waits=[t_hfm])
            sync_dma(d_dbghq[:, 2048:4096], ab, s_misc)
            for si in range(5):
                dbghq_tok = sync_dma(d_dbgscr[:, si * 512:(si + 1) * 512],
                                     scr[si], s_misc)

        wtok_d = [None, None]
        for tp in range(2):
            for blk4 in range(16):
                sl = blk4 % 2
                w = [] if wtok_d[sl] is None else [wtok_d[sl]]
                if blk4 < 2 and tp == 0:
                    w.append(t_hfm)
                    if wd_tok is not None:
                        w.append(wd_tok)
                dst = wdslot[sl].rearrange("p (ic o) -> p ic o", ic=4)
                if q == 0 and tp == 0:
                    w = list(w) + [pc_done[4 + blk4]]
                ldw_ = sync_dma(dst,
                                d_wdbf[:, blk4 * 8192:(blk4 + 1) * 8192]
                                .rearrange("p (ic o) -> p ic o", ic=4),
                                slot(f"wdl{sl}"), waits=w)
                for u in range(4):
                    ic = blk4 * 4 + u
                    for ti in range(2):
                        tc = 2 * tp + ti
                        for osl in range(4):
                            bi = ti * 4 + osl
                            tw = []
                            if ic == 0 and osl == 0:
                                tw = [ldw_, t_hfm] + bg(bi)
                                if bevac_tok.get((tp, bi)) is not None:
                                    tw.append(bevac_tok[(tp, bi)])
                            elif ic == 0:
                                tw = bg(bi)
                                if bevac_tok.get((tp, bi)) is not None:
                                    tw.append(bevac_tok[(tp, bi)])
                            elif osl == 0 and u == 0:
                                tw = [ldw_]
                            mm(h_fm[:, ic * 512 + tc * 128: ic * 512 + tc * 128 + 128],
                               wdslot[sl][:, u * 2048 + osl * 512:
                                          u * 2048 + (osl + 1) * 512],
                               ps[bi], start=(ic == 0), stop=(ic == 63),
                               waits=tw,
                               inc=(s_pe if (u == 3 and bi == 7) else None))
                wtok_d[sl] = at(s_pe)
            t_dn = at(s_pe)
            bw = [t_dn]
            if q > 0 and tp == 0 and ln2_prev is not None:
                bw.append(ln2_prev)
            for ti in range(2):
                tc = 2 * tp + ti
                for osl in range(4):
                    bi = ti * 4 + osl
                    dve("tensor_copy",
                        b_buf[:, tc * 2048 + osl * 512: tc * 2048 + (osl + 1) * 512],
                        ps[bi], waits=(bw if (ti == 0 and osl == 0) else []),
                        inc=(s_dve if (ti == 1 and osl == 3) else None))
            bwr = at(s_dve)
            for ti in range(2):
                for osl in range(4):
                    bguard[ti * 4 + osl] = bwr
                    bevac_tok[(tp, ti * 4 + osl)] = bwr
        wd_tok = bwr
        hfm_free = at(s_pe)
        if debug and q == 0:
            dbgb_tok = sync_dma(d_dbgb, b_buf, s_misc, waits=[bwr])

        for tcq in range(4):
            tcg = q * 4 + tcq
            w = [t_ysp_all, bwr]
            if out_prev is not None:
                w.append(out_prev)
            if ln2_prev is not None:
                w.append(ln2_prev)
            if tcq == 0:
                w.append(spar_busy)
                if debug and q == 0:
                    w.append(dbghq_tok)
                    w.append(dbgb_tok)
            yld = sync_dma(yln, d_ysp[tcg * 128:(tcg + 1) * 128, :],
                           slot("yln"), waits=w)
            dve("scalar_tensor_tensor", c2, yln, 1.0,
                b_buf[:, tcq * 2048:(tcq + 1) * 2048], ALU.mult, ALU.add,
                accum_out=sb_st[:, 0:1], waits=[yld])
            dve("scalar_tensor_tensor", b_buf[:, tcq * 2048:(tcq + 1) * 2048],
                c2, 1.0, c2, ALU.mult, ALU.mult, accum_out=sb_st[:, 1:2],
                inc=s_dve)
            dve("tensor_scalar", sb_st[:, 2:3], sb_st[:, 0:1], 1.0 / 2048.0,
                None, ALU.mult, waits=[at(s_dve)])
            dve("tensor_scalar", sb_st[:, 3:4], sb_st[:, 1:2], 1.0 / 2048.0,
                None, ALU.mult, inc=s_dve)
            dve("tensor_mul", sb_st[:, 4:5], sb_st[:, 2:3], sb_st[:, 2:3],
                waits=[at(s_dve)], inc=s_dve)
            dve("tensor_sub", sb_st[:, 5:6], sb_st[:, 3:4], sb_st[:, 4:5],
                waits=[at(s_dve)], inc=s_dve)
            act(sb_st[:, 6:7], sb_st[:, 5:6], AF.Ln, bias=sb_eps, waits=[at(s_dve)])
            act(sb_st[:, 7:8], sb_st[:, 6:7], AF.Exp, scale=-0.5, inc=s_act)
            dve("tensor_scalar", outt, c2, sb_st[:, 2:3], sb_st[:, 7:8],
                ALU.subtract, ALU.mult, waits=[at(s_act)], inc=s_dve)
            ln2_prev = at(s_dve)
            out_prev = sync_dma(d_out[tcg * 128:(tcg + 1) * 128, :], outt,
                                s_out, waits=[ln2_prev])
            out_w.append(out_prev)

    prog.emit("sync", lambda e: e.nop(nofuse=True), waits=[at(s_out)])

    with nc.Block() as block:
        prog.run_in(block)
    stk.close()
    return nc


def host_constants():
    pos = np.arange(S, dtype=np.float64)
    j = np.arange(128) % 64
    inv = (1.0 / (1.0e6 ** (2.0 * j / 128.0)))[:, None]
    ang = pos[None, :] * inv
    return dict(
        ident=np.eye(128, dtype=ml_dtypes.bfloat16),
        identf=np.eye(128, dtype=np.float32),
        cosfm=np.cos(ang).astype(np.float32),
        sinfm=np.sin(ang).astype(np.float32),
        ones_bf=np.ones((1, 128), dtype=ml_dtypes.bfloat16),
        ones_f32=np.ones((1, 128), dtype=np.float32),
    )


def ffn_perm():
    d = np.arange(I)
    ic, p = d // 128, d % 128
    qd, j = ic // 4, ic % 4
    return (qd * 128 + p) * 4 + j


def make_in_maps(inputs):
    consts = host_constants()
    perm = ffn_perm()
    x = np.ascontiguousarray(np.asarray(inputs["L0_attn_hidden_states"]))
    shared = dict(
        wq=np.ascontiguousarray(np.asarray(inputs["L0_attn_Wq"])),
        wk=np.ascontiguousarray(np.asarray(inputs["L0_attn_Wk"])),
        wv=np.ascontiguousarray(np.asarray(inputs["L0_attn_Wv"])),
        wo=np.ascontiguousarray(np.asarray(inputs["L0_attn_Wo"])),
        wr=np.ascontiguousarray(np.asarray(inputs["L0_attn_Wr"])),
        wg=np.ascontiguousarray(np.asarray(inputs["L0_ffn_w_gate"])[:, perm]),
        wu=np.ascontiguousarray(np.asarray(inputs["L0_ffn_w_up"])[:, perm]),
        wd=np.ascontiguousarray(np.asarray(inputs["L0_ffn_w_down"])[perm, :]),
        bqfm=np.ascontiguousarray(
            np.asarray(inputs["L0_attn_bq"]).reshape(NH, 128).T),
        bkfm=np.ascontiguousarray(
            np.asarray(inputs["L0_attn_bk"]).reshape(NKV, 128).T),
        bv=np.ascontiguousarray(np.asarray(inputs["L0_attn_bv"]).reshape(1, -1)),
        bo=np.ascontiguousarray(np.asarray(inputs["L0_attn_bo"]).reshape(1, -1)),
        br=np.ascontiguousarray(np.asarray(inputs["L0_attn_br"]).reshape(NH, 1)),
        **consts)
    return [dict(x=np.ascontiguousarray(x[c]), **shared) for c in range(8)]


def kernel(**inputs):
    nc = build_program(debug=False)
    in_maps = make_in_maps(inputs)
    res = run_bass_kernel_spmd(nc, in_maps, core_ids=list(range(8)))
    return np.stack([np.asarray(res.results[c]["out"], dtype=np.float32)
                     for c in range(8)], axis=0)


# version 07 (final)
# speedup vs baseline: 1.2240x; 1.0453x over previous
import contextlib
import numpy as np
import ml_dtypes
import concourse.bass as bass
import concourse.mybir as mybir
from concourse.bass_utils import run_bass_kernel_spmd

F32 = mybir.dt.float32
BF16 = mybir.dt.bfloat16
AF = mybir.ActivationFunctionType
ALU = mybir.AluOpType

S, H, NH, NKV, D, I = 2048, 2048, 16, 8, 128, 8192
TOPK = 819
EPS = 1e-5
ISCALE = float(1.0 / np.sqrt(128.0))
CSHIFT = 25.0
BISECT_ITERS = 42

ENGINES = ("tensor", "vector", "scalar", "gpsimd", "sync")


class Sem:
    __slots__ = ("h", "n", "name")

    def __init__(self, h, name):
        self.h, self.n, self.name = h, 0, name


def at(s):
    return (s, s.n)


class Prog:
    def __init__(self):
        self.ops = {e: [] for e in ENGINES}

    def emit(self, eng, fn, waits=(), inc=None, incn=1):
        w = [(s.h, v) for (s, v) in waits if s is not None and v > 0]
        self.ops[eng].append((fn, w, None if inc is None else inc.h, incn))
        if inc is not None:
            inc.n += incn

    def run_in(self, block):
        for eng in ENGINES:
            ops = self.ops[eng]

            def body(engine, _ops=ops):
                for fn, waits, inch, incn in _ops:
                    for (sh, v) in waits:
                        engine.wait_ge(sh, v)
                    inst = fn(engine)
                    if inch is not None:
                        inst.then_inc(inch, incn)

            getattr(block, eng)(body)


def build_program(debug=False):
    nc = bass.Bass("TRN2", target_bir_lowering=False)
    prog = Prog()
    stk = contextlib.ExitStack()
    E = stk.enter_context

    d_x = nc.dram_tensor("x", [S, H], F32, kind="ExternalInput").ap()
    d_wq = nc.dram_tensor("wq", [H, NH * D], F32, kind="ExternalInput").ap()
    d_wk = nc.dram_tensor("wk", [H, NKV * D], F32, kind="ExternalInput").ap()
    d_wv = nc.dram_tensor("wv", [H, NKV * D], F32, kind="ExternalInput").ap()
    d_wo = nc.dram_tensor("wo", [NH * D, H], F32, kind="ExternalInput").ap()
    d_wr = nc.dram_tensor("wr", [H, NH], F32, kind="ExternalInput").ap()
    d_bqfm = nc.dram_tensor("bqfm", [128, NH], F32, kind="ExternalInput").ap()
    d_bkfm = nc.dram_tensor("bkfm", [128, NKV], F32, kind="ExternalInput").ap()
    d_bv = nc.dram_tensor("bv", [1, NKV * D], F32, kind="ExternalInput").ap()
    d_bo = nc.dram_tensor("bo", [1, H], F32, kind="ExternalInput").ap()
    d_br = nc.dram_tensor("br", [NH, 1], F32, kind="ExternalInput").ap()
    d_wg = nc.dram_tensor("wg", [H, I], F32, kind="ExternalInput").ap()
    d_wu = nc.dram_tensor("wu", [H, I], F32, kind="ExternalInput").ap()
    d_wd = nc.dram_tensor("wd", [I, H], F32, kind="ExternalInput").ap()
    d_ident = nc.dram_tensor("ident", [128, 128], BF16, kind="ExternalInput").ap()
    d_identf = nc.dram_tensor("identf", [128, 128], F32, kind="ExternalInput").ap()
    d_cos = nc.dram_tensor("cosfm", [128, S], F32, kind="ExternalInput").ap()
    d_sin = nc.dram_tensor("sinfm", [128, S], F32, kind="ExternalInput").ap()
    d_ones_bf = nc.dram_tensor("ones_bf", [1, 128], BF16, kind="ExternalInput").ap()
    d_ones_f32 = nc.dram_tensor("ones_f32", [1, 128], F32, kind="ExternalInput").ap()
    spk = "ExternalOutput" if debug else "Internal"
    d_ybf = nc.dram_tensor("ybf", [S, H], BF16).ap()
    d_qsp = nc.dram_tensor("qsp", [NH * D, S], BF16, kind=spk).ap()
    d_osp = nc.dram_tensor("osp", [NH * S, D], BF16, kind=spk).ap()
    d_ysp = nc.dram_tensor("ysp", [S, H], F32, kind=spk).ap()
    d_out = nc.dram_tensor("out", [S, H], F32, kind="ExternalOutput").ap()
    if debug:
        d_dbgz = nc.dram_tensor("dbgz", [NH, S], F32, kind="ExternalOutput").ap()
        d_dbgmb = nc.dram_tensor("dbgmb", [128, 256], F32, kind="ExternalOutput").ap()
        d_dbglo = nc.dram_tensor("dbglo", [NH, 3], F32, kind="ExternalOutput").ap()
        d_dbgzkm = nc.dram_tensor("dbgzkm", [128, 256], F32, kind="ExternalOutput").ap()
        d_dbgk = nc.dram_tensor("dbgk", [128, NKV * S], BF16, kind="ExternalOutput").ap()
        d_dbge = nc.dram_tensor("dbge", [128, 16 * 512], BF16, kind="ExternalOutput").ap()
        d_dbgpv = nc.dram_tensor("dbgpv", [128, 129], F32, kind="ExternalOutput").ap()
        d_dbga = nc.dram_tensor("dbga", [128, 8192], F32, kind="ExternalOutput").ap()
        d_dbghfm = nc.dram_tensor("dbghfm", [128, 64 * 512], BF16, kind="ExternalOutput").ap()
        d_dbgb = nc.dram_tensor("dbgb", [128, 4 * 2048], BF16, kind="ExternalOutput").ap()
        d_dbgct = nc.dram_tensor("dbgct", [128, 2048], F32, kind="ExternalOutput").ap()
        d_dbgst = nc.dram_tensor("dbgst", [128, 16], F32, kind="ExternalOutput").ap()
        d_dbgsq = nc.dram_tensor("dbgsq", [128, 2048], BF16, kind="ExternalOutput").ap()
        d_dbghq = nc.dram_tensor("dbghq", [128, 4096], F32, kind="ExternalOutput").ap()
        d_dbgscr = nc.dram_tensor("dbgscr", [128, 2560], F32, kind="ExternalOutput").ap()

    U1 = E(nc.sbuf_tensor("U1", [128, 65536], mybir.dt.uint8)).ap()
    U2 = E(nc.sbuf_tensor("U2", [128, 33024], mybir.dt.uint8)).ap()
    U3 = E(nc.sbuf_tensor("U3", [128, 32768], mybir.dt.uint8)).ap()
    U4 = E(nc.sbuf_tensor("U4", [128, 16384], mybir.dt.uint8)).ap()
    U5 = E(nc.sbuf_tensor("U5", [128, 16384], mybir.dt.uint8)).ap()
    U6 = E(nc.sbuf_tensor("U6", [128, 16384], mybir.dt.uint8)).ap()

    def u8v(U, b0, b1, dt):
        n = b1 - b0
        return U[:, b0:b1].bitcast(dt)

    sb_cos = E(nc.sbuf_tensor("sb_cos", [128, S], F32)).ap()
    sb_sin = E(nc.sbuf_tensor("sb_sin", [128, S], F32)).ap()
    sb_ident = E(nc.sbuf_tensor("sb_ident", [128, 128], BF16)).ap()
    sb_identf = E(nc.sbuf_tensor("sb_identf", [128, 128], F32)).ap()
    sb_ones_bf = E(nc.sbuf_tensor("sb_ones_bf", [1, 128], BF16)).ap()
    sb_ones_f32 = E(nc.sbuf_tensor("sb_ones_f32", [1, 128], F32)).ap()
    sb_bqfm = E(nc.sbuf_tensor("sb_bqfm", [128, NH], F32)).ap()
    sb_bkfm = E(nc.sbuf_tensor("sb_bkfm", [128, NKV], F32)).ap()
    sb_br = E(nc.sbuf_tensor("sb_br", [NH, 1], F32)).ap()
    sb_bv = E(nc.sbuf_tensor("sb_bv", [1, NKV * D], BF16)).ap()
    sb_zkm = E(nc.sbuf_tensor("sb_zkm", [128, 256], F32)).ap()
    sb_mb = E(nc.sbuf_tensor("sb_mb", [128, 256], F32)).ap()
    sb_lo = E(nc.sbuf_tensor("sb_lo", [NH, 1], F32)).ap()
    sb_hi = E(nc.sbuf_tensor("sb_hi", [NH, 1], F32)).ap()
    sb_mid = E(nc.sbuf_tensor("sb_mid", [NH, 1], F32)).ap()
    sb_cnt = E(nc.sbuf_tensor("sb_cnt", [NH, 1], F32)).ap()
    sb_gs = E(nc.sbuf_tensor("sb_gs", [NH, 1], F32)).ap()
    sb_t16 = E(nc.sbuf_tensor("sb_t16", [NH, 1], F32)).ap()
    sb_lor = E(nc.sbuf_tensor("sb_lor", [1, NH], F32)).ap()
    sb_lob = E(nc.sbuf_tensor("sb_lob", [128, NH], F32)).ap()
    sb_st = E(nc.sbuf_tensor("sb_st", [128, 16], F32)).ap()
    sb_rcp = E(nc.sbuf_tensor("sb_rcp", [128, 1], F32)).ap()
    sb_eps = E(nc.sbuf_tensor("sb_eps", [128, 1], F32)).ap()

    ps = [E(nc.psum_tensor(f"ps{i}", [128, 512], F32)).ap() for i in range(8)]

    def psb(i):
        return ps[i].bitcast(BF16)

    def mksem(name):
        return Sem(E(nc.semaphore(name)), name)

    s_pe, s_act, s_dve = mksem("s_pe"), mksem("s_act"), mksem("s_dve")
    s_pool = mksem("s_pool")
    s_out, s_misc = mksem("s_out"), mksem("s_misc")
    _dma_sems = [mksem(f"s_d{i}") for i in range(40)]
    _free = list(range(40))
    _slots = {}

    def slot(name):
        if name not in _slots:
            _slots[name] = _dma_sems[_free.pop(0)]
        return _slots[name]

    bguard = {}

    def bg(bi):
        return [bguard[bi]] if bi in bguard else []

    def cast_load(dst, src, sem, waits=()):
        prog.emit("gpsimd", lambda g, d=dst, s_=src: g.dma_start(d, s_),
                  waits=waits, inc=sem, incn=16)
        return at(sem)

    def sync_dma(dst, src, sem, waits=()):
        prog.emit("sync", lambda e, d=dst, s_=src: e.dma_start(d, s_),
                  waits=waits, inc=sem, incn=16)
        return at(sem)

    def act_dma(dst, src, sem, waits=()):
        prog.emit("scalar", lambda e, d=dst, s_=src: e.dma_start(d, s_),
                  waits=waits, inc=sem, incn=16)
        return at(sem)

    def mm(lhsT, rhs, out, start, stop, waits=(), inc=None):
        prog.emit("tensor",
                  lambda t, a=lhsT, b=rhs, o=out, s1=start, s2=stop:
                  t.matmul(o, a, b, start=s1, stop=s2), waits=waits, inc=inc)

    def tpose(out, in_, waits=(), inc=None):
        kdim = in_.shape[0]
        idt = sb_identf if in_.dtype == F32 else sb_ident
        prog.emit("tensor",
                  lambda t, o=out, i=in_, k=kdim, ii=idt:
                  t.transpose(o, i, ii[0:k, 0:k]),
                  waits=waits, inc=inc)

    def act(out, in_, func, bias=0.0, scale=1.0, accum_out=None, waits=(), inc=None):
        prog.emit("scalar",
                  lambda s, o=out, i=in_, f=func, b=bias, sc=scale, ao=accum_out:
                  s.activation(o, i, f, bias=b, scale=sc, accum_out=ao),
                  waits=waits, inc=inc)

    def dve(fname, *args, waits=(), inc=None, **kw):
        prog.emit("vector",
                  lambda v, f=fname, a=args, k=kw: getattr(v, f)(*a, **k),
                  waits=waits, inc=inc)

    def pool(fname, *args, waits=(), inc=None, **kw):
        prog.emit("gpsimd",
                  lambda g, f=fname, a=args, k=kw: getattr(g, f)(*a, **k),
                  waits=waits, inc=inc)

    c_tok = None
    for dst, src in ((sb_ident, d_ident), (sb_identf, d_identf), (sb_cos, d_cos), (sb_sin, d_sin),
                     (sb_ones_bf, d_ones_bf), (sb_ones_f32, d_ones_f32),
                     (sb_bqfm, d_bqfm), (sb_bkfm, d_bkfm), (sb_br, d_br)):
        c_tok = sync_dma(dst, src, s_misc)
    t_consts = c_tok
    dve("memset", sb_eps, EPS, inc=s_dve)

    x0_fm = U1.bitcast(BF16)
    stage = [u8v(U4, 0, 4096, BF16), u8v(U4, 4096, 8192, BF16)]
    x_r = d_x.rearrange("(tc p) f -> p tc f", p=128)
    stage_free = [None, None]
    ev_hist = []
    for tc in range(16):
        sl = tc % 2
        w = [t_consts] if tc == 0 else []
        if stage_free[sl] is not None:
            w.append(stage_free[sl])
        ld = cast_load(stage[sl], x_r[:, tc, :], slot(f"stage{sl}"), waits=w)
        for fc in range(16):
            k = tc * 16 + fc
            bi = 6 + (k % 2)
            bank = psb(bi)[:, 0:128]
            tw = ([ld] if fc == 0 else []) + bg(bi)
            tpose(bank, stage[sl][:, fc * 128:(fc + 1) * 128], waits=tw, inc=s_pe)
            dve("tensor_copy",
                x0_fm[:, fc * 2048 + tc * 128: fc * 2048 + tc * 128 + 128],
                bank, waits=[at(s_pe)], inc=s_dve)
            bguard[bi] = at(s_dve)
            ev_hist.append(at(s_dve))
        stage_free[sl] = at(s_pe)
    t_x0fm = at(s_dve)

    def x0c(kc, lo, n):
        return x0_fm[:, kc * 2048 + lo: kc * 2048 + lo + n]

    bvf = u8v(U4, 4096, 8192, F32)[0:1, 0:NKV * D]
    ld_bv = sync_dma(bvf, d_bv, s_misc, waits=[t_x0fm, stage_free[1]])
    act(sb_bv, bvf, AF.Identity, waits=[ld_bv], inc=s_act)
    t_bvbf = at(s_act)

    sb_z = u8v(U6, 8192, 16384, F32)[0:16, :]
    wr_sb = u8v(U5, 8192, 8704, BF16)
    ld_wr = cast_load(wr_sb.rearrange("p (kc m) -> p kc m", kc=16),
                      d_wr.rearrange("(kc p) m -> p kc m", p=128), slot("wr"))
    for qt in range(4):
        bi = 2 + qt % 2
        zps = ps[bi][0:16, :]
        for kc in range(16):
            tw = []
            if kc == 0:
                tw = [ld_wr, t_x0fm] + bg(bi)
            mm(wr_sb[:, kc * 16:(kc + 1) * 16], x0c(kc, qt * 512, 512), zps,
               start=(kc == 0), stop=(kc == 15), waits=tw,
               inc=(s_pe if kc == 15 else None))
        act(sb_z[:, qt * 512:(qt + 1) * 512], zps, AF.Identity, bias=sb_br,
            waits=[at(s_pe)], inc=s_act)
        bguard[bi] = at(s_act)
    t_z = at(s_act)
    t_z_pe = at(s_pe)

    zge = u8v(U4, 0, 4096, BF16)[0:16, :]
    pool("memset", sb_lo, -32.0, waits=[t_z, stage_free[0]])
    pool("memset", sb_hi, 32.0, inc=s_pool)
    _bis = {"it": 0, "ptok": at(s_pool), "dtok": None}

    def emit_bisect_iter():
        if _bis["it"] >= BISECT_ITERS:
            return
        _bis["it"] += 1
        pw = [] if _bis["dtok"] is None else [_bis["dtok"]]
        pool("tensor_add", sb_mid, sb_lo, sb_hi, waits=pw)
        pool("tensor_scalar", sb_mid, sb_mid, 0.5, None, ALU.mult, inc=s_pool)
        ptok = at(s_pool)
        dve("tensor_scalar", zge, sb_z, sb_mid, None, ALU.is_ge, ALU.add,
            accum_out=sb_cnt, waits=[ptok], inc=s_dve)
        _bis["dtok"] = at(s_dve)
        pool("tensor_scalar", sb_gs, sb_cnt, float(TOPK) - 0.5, None, ALU.is_ge,
             waits=[_bis["dtok"]])
        pool("tensor_sub", sb_t16, sb_mid, sb_lo)
        pool("tensor_mul", sb_t16, sb_t16, sb_gs)
        pool("tensor_add", sb_lo, sb_lo, sb_t16)
        pool("tensor_sub", sb_t16, sb_hi, sb_mid)
        pool("tensor_mul", sb_t16, sb_t16, sb_gs)
        pool("tensor_add", sb_hi, sb_mid, sb_t16, inc=s_pool)
        _bis["ptok"] = at(s_pool)

    def emit_maskbias():
        t_bis = _bis["ptok"]
        mask01 = u8v(U4, 8192, 16384, F32)[0:16, :]
        dve("tensor_scalar", mask01, sb_z, sb_lo, None, ALU.is_ge,
            waits=[t_bis, t_z, stage_free[1]], inc=s_dve)
        t_m01 = at(s_dve)
        for kc in range(16):
            bi = 4 + kc % 2
            bank = ps[bi][:, 0:16]
            tw = ([t_m01] if kc <= 1 else []) + bg(bi)
            tpose(bank, mask01[:, kc * 128:(kc + 1) * 128], waits=tw, inc=s_pe)
            dve("tensor_copy", sb_zkm[:, kc * 16:(kc + 1) * 16], bank,
                waits=[at(s_pe)], inc=s_dve)
            bguard[bi] = at(s_dve)
        dve("tensor_scalar", sb_mb, sb_zkm, 10000.0, 10000.0 + CSHIFT,
            ALU.mult, ALU.subtract, inc=s_dve)
        return at(s_dve), at(s_pe)

    rope_q = [u8v(U6, 0, 2048, F32), u8v(U6, 4096, 6144, F32)]
    rope_r = [u8v(U6, 2048, 4096, F32), u8v(U6, 6144, 8192, F32)]
    Qslot = [u8v(U6, 8192, 12288, BF16), u8v(U6, 12288, 16384, BF16)]
    rope_i = [0]
    rope_busy = {}

    def rope_block(dst_bf, bank, bias_col, qt, extra=()):
        i = rope_i[0] % 2
        rope_i[0] += 1
        q, r = rope_q[i], rope_r[i]
        cs = sb_cos[:, qt * 512:(qt + 1) * 512]
        sn = sb_sin[:, qt * 512:(qt + 1) * 512]
        aw = [at(s_pe)] + list(extra)
        if i in rope_busy:
            aw.append(rope_busy[i])
        act(q, bank, AF.Identity, bias=bias_col, waits=aw)
        act(r[0:64, :], q[64:128, :], AF.Identity, scale=-1.0)
        act(r[64:128, :], q[0:64, :], AF.Identity, inc=s_act)
        a_tok = at(s_act)
        dve("tensor_mul", q, q, cs, waits=[a_tok])
        dve("tensor_mul", r, r, sn, inc=s_dve)
        dve("tensor_add", dst_bf, q, r, waits=[at(s_dve)], inc=s_dve)
        d_tok = at(s_dve)
        rope_busy[i] = d_tok
        return d_tok, a_tok

    K_fm = U3.bitcast(BF16)
    wslot = [u8v(U5, 0, 4096, BF16), u8v(U5, 4096, 8192, BF16)]
    wk_r = d_wk.rearrange("(kc p) m -> p kc m", p=128)
    wq_r = d_wq.rearrange("(kc p) m -> p kc m", p=128)
    wtok = [None, None]
    dtok = None
    for g in range(NKV):
        sl = g % 2
        w = [wtok[sl]] if wtok[sl] is not None else []
        ldk = cast_load(wslot[sl], wk_r[:, :, g * 128:(g + 1) * 128],
                        slot(f"wkq{sl}"), waits=w)
        for qt in range(4):
            bi = (g * 4 + qt) % 2
            for kc in range(16):
                tw = ([ldk, t_x0fm] + bg(bi)) if kc == 0 else []
                mm(wslot[sl][:, kc * 128:(kc + 1) * 128], x0c(kc, qt * 512, 512),
                   ps[bi], start=(kc == 0), stop=(kc == 15), waits=tw,
                   inc=(s_pe if kc == 15 else None))
            dtok, atok = rope_block(
                K_fm[:, g * 2048 + qt * 512: g * 2048 + (qt + 1) * 512],
                ps[bi], sb_bkfm[:, g:g + 1], qt)
            bguard[bi] = atok
            emit_bisect_iter()
        wtok[sl] = dtok

    V_aug = U2.bitcast(BF16)[:, 0:16 * 1032]
    v4 = V_aug.rearrange("p (tc g e) -> p tc g e", tc=16, g=8, e=129)
    dve("memset", v4[:, :, :, 128:129], 1.0, inc=s_dve)
    t_vones = at(s_dve)

    wv_r = d_wv.rearrange("(kc p) m -> p kc m", p=128)
    vw = [u8v(U5, 8192, 12288, BF16), u8v(U5, 12288, 16384, BF16)]
    vtok = [None, None]
    vev = None
    for vh in range(NKV):
        sl = vh % 2
        w = [vtok[sl]] if vtok[sl] is not None else [t_bvbf, t_z_pe]
        ldv = cast_load(vw[sl], wv_r[:, :, vh * 128:(vh + 1) * 128],
                        slot(f"wv{sl}"), waits=w)
        for tc in range(16):
            bi = 2 + (vh * 16 + tc) % 2
            for kc in range(16):
                tw = ([ldv, t_vones] + bg(bi)) if kc == 0 else []
                mm(x0c(kc, tc * 128, 128), vw[sl][:, kc * 128:(kc + 1) * 128],
                   ps[bi][:, 0:128], start=(kc == 0), stop=False, waits=tw)
            mm(sb_ones_bf, sb_bv[0:1, vh * 128:(vh + 1) * 128],
               ps[bi][:, 0:128], start=False, stop=True, inc=s_pe)
            col = tc * 1032 + vh * 129
            act(V_aug[:, col:col + 128], ps[bi][:, 0:128], AF.Identity,
                waits=[at(s_pe)], inc=s_act)
            bguard[bi] = at(s_act)
            vev = at(s_act)
            if tc % 4 == 3:
                emit_bisect_iter()
        vtok[sl] = vev
    t_vaug = vev
    while _bis["it"] < BISECT_ITERS:
        emit_bisect_iter()

    t_mb, t_mb_pe = emit_maskbias()
    if debug:
        dbg_tok = sync_dma(d_dbgz, sb_z, s_misc, waits=[t_z, t_mb])
        prog.emit("vector", lambda v: v.tensor_copy(sb_st[0:16, 8:9], sb_lo),
                  waits=[t_mb])
        prog.emit("vector", lambda v: v.tensor_copy(sb_st[0:16, 9:10], sb_hi))
        prog.emit("vector", lambda v: v.tensor_copy(sb_st[0:16, 10:11], sb_cnt),
                  inc=s_dve)
        sync_dma(d_dbglo, sb_st[0:16, 8:11], s_misc, waits=[at(s_dve)])
        sync_dma(d_dbgzkm, sb_zkm, s_misc)
        sync_dma(d_dbgmb, sb_mb, s_misc)
        t_mb = at(s_misc)
        prog.emit("vector", lambda v: v.tensor_copy(sb_t16, sb_lo),
                  waits=[at(s_misc)], inc=s_dve)
        t_mb = at(s_dve)

    qsp_w = []
    qtok = [None, None]
    qspill = [None, None]
    for h in range(NH):
        sl = h % 2
        w = [qtok[sl]] if qtok[sl] is not None else []
        ldq = cast_load(wslot[sl], wq_r[:, :, h * 128:(h + 1) * 128],
                        slot(f"wkq{sl}"), waits=w)
        for qt in range(4):
            bi = (h * 4 + qt) % 2
            for kc in range(16):
                tw = []
                if kc == 0:
                    tw = [ldq] + bg(bi)
                    if qt == 0 and qspill[sl] is not None:
                        tw.append(qspill[sl])
                    if h == 0 and qt == 0:
                        tw.append(t_mb_pe)
                mm(wslot[sl][:, kc * 128:(kc + 1) * 128], x0c(kc, qt * 512, 512),
                   ps[bi], start=(kc == 0), stop=(kc == 15), waits=tw,
                   inc=(s_pe if kc == 15 else None))
            ex = [t_mb] if (h == 0 and qt == 0) else []
            dtok, atok = rope_block(Qslot[sl][:, qt * 512:(qt + 1) * 512],
                                    ps[bi], sb_bqfm[:, h:h + 1], qt, extra=ex)
            bguard[bi] = atok
        qspill[sl] = sync_dma(d_qsp[h * 128:(h + 1) * 128, :], Qslot[sl],
                              slot(f"qsp{sl}"), waits=[dtok])
        qsp_w.append(qspill[sl])
        qtok[sl] = dtok
    t_a_pe = at(s_pe)
    t_a_dve = at(s_dve)
    if debug:
        sync_dma(d_dbgk, K_fm, s_misc, waits=[t_a_dve])


    d_wgbf = nc.dram_tensor("wgbf", [64, 128, 2048], BF16).ap()
    d_wubf = nc.dram_tensor("wubf", [64, 128, 2048], BF16).ap()
    d_wdbf = nc.dram_tensor("wdbf", [128, 64 * 2048], BF16).ap()
    d_wobf = nc.dram_tensor("wobf", [4, 128, 16 * 512], BF16).ap()
    wg_r0 = d_wg.rearrange("(kc p) m -> p kc m", p=128)
    wu_r0 = d_wu.rearrange("(kc p) m -> p kc m", p=128)
    wd_r0 = d_wd.rearrange("(ic p) o -> p ic o", p=128)
    wo_r0 = d_wo.rearrange("(kc p) o -> p kc o", p=128)
    sem_wo, sem_wd = slot("pc_wo"), slot("pc_wd")
    sem_wg, sem_wu = slot("pc_wg"), slot("pc_wu")

    def emit_precache_all():
        for osl in range(4):
            cast_load(d_wobf[osl], wo_r0[:, :, osl * 512:(osl + 1) * 512], sem_wo)
        for b4 in range(16):
            cast_load(d_wdbf[:, b4 * 8192:(b4 + 1) * 8192]
                      .rearrange("p (ic o) -> p ic o", ic=4),
                      wd_r0[:, b4 * 4:(b4 + 1) * 4, :], sem_wd)
        for ic in range(64):
            cast_load(d_wgbf[ic], wg_r0[:, :, ic * 128:(ic + 1) * 128], sem_wg)
        for ic in range(64):
            cast_load(d_wubf[ic], wu_r0[:, :, ic * 128:(ic + 1) * 128], sem_wu)
        return at(sem_wo), at(sem_wd), at(sem_wg), at(sem_wu)

    t_pc_wo, t_pc_wd, t_pc_wg, t_pc_wu = emit_precache_all()

    ebuf = [U1.bitcast(BF16)[:, 8192 * i: 8192 * (i + 1)] for i in range(2)]
    Oslot = [u8v(U4, 0, 4096, BF16), u8v(U4, 4096, 8192, BF16)]
    Ofm = [u8v(U4, 8192, 12288, BF16), u8v(U4, 12288, 16384, BF16)]
    Qb = [u8v(U5, 0, 4096, BF16), u8v(U5, 4096, 8192, BF16)]
    e_free = [None, None]
    osl_free = [None, None]
    ofm_free = [None, None]
    qldw = [None, None]
    osp_w = []
    pend = []
    t_exp_of = {}

    def emit_scores(h, qt, qld):
        gi = h * 4 + qt
        par = gi % 2
        g = h // 2
        for kc in range(16):
            bi = 2 + ((gi * 16 + kc) % 3)
            tw = bg(bi)
            if kc == 0:
                tw = tw + [qld, t_mb]
                if gi == 0:
                    tw += [t_a_pe, t_a_dve]
                if e_free[par] is not None:
                    tw.append(e_free[par])
            mm(K_fm[:, g * 2048 + kc * 128: g * 2048 + (kc + 1) * 128],
               Qb[h % 2][:, qt * 512:(qt + 1) * 512], ps[bi],
               start=True, stop=True, waits=tw, inc=s_pe)
            act(ebuf[par][:, kc * 512:(kc + 1) * 512], ps[bi], AF.Exp,
                bias=sb_mb[:, kc * 16 + h: kc * 16 + h + 1], scale=ISCALE,
                waits=[at(s_pe)], inc=s_act)
            bguard[bi] = at(s_act)
        t_exp_of[gi] = at(s_act)

    def emit_pv(gi):
        h, qt = gi // 4, gi % 4
        par = gi % 2
        g = h // 2
        hs = h % 2
        for q2 in range(4):
            qoff = q2 * 128
            bi = 4 + ((gi * 4 + q2) % 4)
            for kc in range(16):
                tw = []
                if kc == 0:
                    tw = [t_exp_of[gi]] + bg(bi)
                    if gi == 0 and q2 == 0:
                        tw.append(t_vaug)
                mm(ebuf[par][:, kc * 512 + qoff: kc * 512 + qoff + 128],
                   V_aug[:, kc * 1032 + g * 129: kc * 1032 + (g + 1) * 129],
                   ps[bi][:, 0:129], start=(kc == 0), stop=(kc == 15),
                   waits=tw, inc=(s_pe if kc == 15 else None))
            t_pv = at(s_pe)
            dw = [t_pv]
            if qt == 0 and q2 == 0 and osl_free[hs] is not None:
                dw.append(osl_free[hs])
            dve("reciprocal", sb_rcp, ps[bi][:, 128:129], waits=dw,
                inc=s_dve)
            dve("tensor_scalar",
                Oslot[hs][:, (qt * 4 + q2) * 128:(qt * 4 + q2 + 1) * 128],
                ps[bi][:, 0:128], sb_rcp, None, ALU.mult,
                waits=[at(s_dve)], inc=s_dve)
            bguard[bi] = at(s_dve)
        e_free[par] = at(s_pe)

    def emit_ohead(h):
        hs = h % 2
        t_o = at(s_dve)
        tok = sync_dma(
            d_osp[h * 2048:(h + 1) * 2048, :].rearrange("(b p) d -> p b d", p=128),
            Oslot[hs].rearrange("p (b d) -> p b d", b=16),
            slot(f"osp{hs}"), waits=[t_o])
        osl_free[hs] = tok
        ofm_free[hs] = tok
        osp_w.append(tok)
        qldw[hs] = at(s_pe)

    for h in range(NH):
        hs = h % 2
        w = [qsp_w[h]]
        if qldw[hs] is not None:
            w.append(qldw[hs])
        if h < 2:
            w.append(t_a_pe)
        qld = sync_dma(Qb[hs], d_qsp[h * 128:(h + 1) * 128, :],
                       slot(f"qb{hs}"), waits=w)
        for qt in range(4):
            gi = h * 4 + qt
            emit_scores(h, qt, qld)
            if pend:
                pgi = pend.pop(0)
                emit_pv(pgi)
                if pgi % 4 == 3:
                    emit_ohead(pgi // 4)
            pend.append(gi)
    while pend:
        pgi = pend.pop(0)
        emit_pv(pgi)
        if pgi % 4 == 3:
            emit_ohead(pgi // 4)
    t_b_pe = at(s_pe)
    t_b_dve = at(s_dve)
    t_b_act = at(s_act)
    if debug:
        sync_dma(d_dbge, ebuf[(NH * 4 - 1) % 2], s_misc, waits=[t_b_pe, t_b_act])

    Wo_sb = U1.bitcast(BF16)
    wo_tok = []
    for osl in range(4):
        dst = Wo_sb.rearrange("p (kc o) -> p kc o", kc=16)[:, :,
                                                           osl * 512:(osl + 1) * 512]
        wo_tok.append(sync_dma(
            dst, d_wobf[osl].rearrange("p (kc o) -> p kc o", kc=16),
            slot(f"wo{osl}"),
            waits=([t_b_pe, t_b_act, t_pc_wo] if osl == 0 else [t_pc_wo])))
    bof = u8v(U6, 0, 8192, F32)[0:1, :]
    sb_bo = u8v(U6, 8192, 12288, BF16)[0:1, :]
    ld_bo = sync_dma(bof, d_bo, s_misc, waits=[t_b_pe])
    act(sb_bo, bof, AF.Identity, waits=[ld_bo], inc=s_act)
    t_bobf = at(s_act)

    Ofm_sb = [U3.bitcast(BF16)[:, 8192 * i: 8192 * (i + 1)] for i in range(2)]
    a_buf = U2.bitcast(F32)[:, 0:2048 * 4]
    x0t = [u8v(U4, 0, 8192, F32), u8v(U4, 8192, 16384, F32)]
    ctile = u8v(U5, 0, 8192, F32)
    ytile = u8v(U5, 8192, 16384, F32)
    sqdummy = u8v(U6, 12288, 16384, BF16)
    ybt = u8v(U6, 0, 4096, BF16)

    ofmq_tok = [None, None]
    quarter_pe = []
    x0_free = [None, None]
    ysp_w = []
    ybf_w = []
    ln1_prev = None
    cstt_tok = [None, None, None, None]
    for q in range(4):
        qs = q % 2
        w = [t_b_pe, osp_w[-1]]
        if q >= 2:
            w.append(quarter_pe[q - 2])
        ofml = None
        for hc in range(16):
            prog.emit("sync",
                      lambda e, d=Ofm_sb[qs][:, hc * 512:(hc + 1) * 512],
                      s_=d_osp[hc * 2048 + q * 512: hc * 2048 + (q + 1) * 512, :]:
                      e.dma_start(d, s_, transpose=True),
                      waits=(w if hc == 0 else []), inc=slot(f"ofm{qs}"), incn=16)
        ofml = at(slot(f"ofm{qs}"))
        ofmq_tok[qs] = ofml
        for osl in range(4):
            for tc in range(4):
                bi = (osl * 4 + tc) % 2
                for hc in range(16):
                    tw = []
                    if hc == 0:
                        tw = [ofmq_tok[qs], wo_tok[osl], t_bobf] + bg(bi)
                        if cstt_tok[tc] is not None:
                            tw.append(cstt_tok[tc])
                    mm(Ofm_sb[qs][:, hc * 512 + tc * 128: hc * 512 + tc * 128 + 128],
                       Wo_sb[:, hc * 2048 + osl * 512: hc * 2048 + (osl + 1) * 512],
                       ps[bi], start=(hc == 0), stop=False, waits=tw)
                mm(sb_ones_bf, sb_bo[0:1, osl * 512:(osl + 1) * 512],
                   ps[bi], start=False, stop=True, inc=s_pe)
                act(a_buf[:, tc * 2048 + osl * 512: tc * 2048 + (osl + 1) * 512],
                    ps[bi], AF.Identity, waits=[at(s_pe)], inc=s_act)
                bguard[bi] = at(s_act)
        quarter_pe.append(at(s_pe))
        t_aq = at(s_act)
        if debug and q == 0:
            sync_dma(d_dbga, a_buf, s_misc, waits=[t_aq])
        for tc in range(4):
            tcg = q * 4 + tc
            xs = tc % 2
            xw = [t_aq]
            if x0_free[xs] is not None:
                xw.append(x0_free[xs])
            if q == 0 and tc < 2:
                xw.append(osp_w[-1])
            xld = sync_dma(x0t[xs], d_x[tcg * 128:(tcg + 1) * 128, :],
                           slot(f"x0t{xs}"), waits=xw)
            cw = [xld]
            if ln1_prev is not None:
                cw.append(ln1_prev)
            if ysp_w:
                cw.append(ysp_w[-1])
            dve("scalar_tensor_tensor", ctile, x0t[xs], 1.0,
                a_buf[:, tc * 2048:(tc + 1) * 2048], ALU.mult, ALU.add,
                accum_out=sb_st[:, 0:1], waits=cw, inc=s_dve)
            cstt_tok[tc] = at(s_dve)
            if debug and q == 0 and tc == 0:
                dbgct_tok = sync_dma(d_dbgct, ctile, s_misc, waits=[cstt_tok[0]])
            dve("scalar_tensor_tensor", sqdummy, ctile, 1.0, ctile,
                ALU.mult, ALU.mult, accum_out=sb_st[:, 1:2], inc=s_dve)
            if debug and q == 0 and tc == 0:
                sync_dma(d_dbgsq, sqdummy, s_misc, waits=[at(s_dve)])
            dve("tensor_scalar", sb_st[:, 2:3], sb_st[:, 0:1], 1.0 / 2048.0,
                None, ALU.mult, waits=[at(s_dve)])
            dve("tensor_scalar", sb_st[:, 3:4], sb_st[:, 1:2], 1.0 / 2048.0,
                None, ALU.mult, inc=s_dve)
            dve("tensor_mul", sb_st[:, 4:5], sb_st[:, 2:3], sb_st[:, 2:3],
                waits=[at(s_dve)], inc=s_dve)
            dve("tensor_sub", sb_st[:, 5:6], sb_st[:, 3:4], sb_st[:, 4:5],
                waits=[at(s_dve)], inc=s_dve)
            act(sb_st[:, 6:7], sb_st[:, 5:6], AF.Ln, bias=sb_eps, waits=[at(s_dve)])
            act(sb_st[:, 7:8], sb_st[:, 6:7], AF.Exp, scale=-0.5, inc=s_act)
            yw = [at(s_act)]
            if debug and q == 0 and tc == 0:
                sync_dma(d_dbgst, sb_st, s_misc, waits=[at(s_act)])
                yw.append(at(s_misc))
                yw.append(dbgct_tok)
            dve("tensor_scalar", ytile, ctile, sb_st[:, 2:3], sb_st[:, 7:8],
                ALU.subtract, ALU.mult, waits=yw, inc=s_dve)
            ln1_prev = at(s_dve)
            tok = sync_dma(d_ysp[tcg * 128:(tcg + 1) * 128, :], ytile,
                           slot(f"ysp{tcg % 2}"), waits=[ln1_prev])
            ysp_w.append(tok)
            ybw = [ln1_prev]
            if ybf_w:
                ybw.append(ybf_w[-1])
            act(ybt, ytile, AF.Identity, waits=ybw, inc=s_act)
            ybf_w.append(act_dma(d_ybf[tcg * 128:(tcg + 1) * 128, :], ybt,
                                 slot("ybf"), waits=[at(s_act)]))
            x0_free[xs] = ln1_prev
    t_c_pe = at(s_pe)
    t_c_dve = at(s_dve)
    t_c_act = at(s_act)

    h_fm = U1.bitcast(BF16)
    wgslot = [U2.bitcast(BF16)[:, 2048 * i: 2048 * (i + 1)] for i in range(2)]
    wuslot = [U2.bitcast(BF16)[:, 4096 + 2048 * i: 4096 + 2048 * (i + 1)]
              for i in range(2)]
    yfm = U2.bitcast(BF16)[:, 8192:16384]
    wdslot = [U3.bitcast(BF16)[:, 8192 * i: 8192 * (i + 1)] for i in range(2)]
    hqd = u8v(U4, 0, 8192, F32)
    ab = u8v(U4, 8192, 16384, F32)
    ystg = u8v(U5, 0, 4096, BF16)
    sgf = u8v(U5, 4096, 6144, F32)
    scr = [u8v(U5, 6144 + 2048 * i, 8192 + 2048 * i, F32) for i in range(5)]
    m1, m2, M1, M2, t3b = scr
    b_buf = U6.bitcast(BF16)[:, 0:8192]
    c2 = u8v(U4, 0, 8192, F32)
    yln = u8v(U4, 8192, 16384, F32)
    outt = u8v(U4, 8192, 16384, F32)


    wg_r = d_wg.rearrange("(kc p) m -> p kc m", p=128)
    wu_r = d_wu.rearrange("(kc p) m -> p kc m", p=128)
    wd_r = d_wd.rearrange("(ic p) o -> p ic o", p=128)
    ysp_rr = d_ysp.rearrange("(tt p) f -> p tt f", p=128)

    wg_tok = [None, None]
    wu_tok = [None, None]
    wd_tok = None
    ystg_free = None
    yfm_pe_free = None
    ln2_prev = None
    out_prev = None
    hfm_free = None
    out_w = []
    t_ysp_all = ysp_w[-1]
    spar_busy = None
    prev_mult = None
    bevac_tok = {}

    for q in range(4):
        for kc in range(16):
            w = [ybf_w[-1]]
            if kc == 0:
                if yfm_pe_free is not None:
                    w.append(yfm_pe_free)
                if ln2_prev is not None:
                    w.append(ln2_prev)
                if q == 0:
                    w += [t_c_pe, t_c_act, t_c_dve]
            prog.emit("sync",
                      lambda e, d=yfm[:, kc * 512:(kc + 1) * 512],
                      s_=d_ybf[q * 512:(q + 1) * 512, kc * 128:(kc + 1) * 128]:
                      e.dma_start(d, s_, transpose=True),
                      waits=w, inc=slot("yfmld"), incn=16)
        t_yfm = at(slot("yfmld"))


        for qd in range(16):
            for j in range(4):
                ic = qd * 4 + j
                sl = ic % 2
                wgw = [wg_tok[sl]] if wg_tok[sl] is not None else \
                    ([t_b_pe, t_c_dve] if q == 0 else [])
                wuw = [wu_tok[sl]] if wu_tok[sl] is not None else \
                    ([t_b_pe, t_c_dve] if q == 0 else [])
                if q == 0:
                    wgw.append(t_pc_wg)
                    wuw.append(t_pc_wu)
                lg = sync_dma(wgslot[sl], d_wgbf[ic], slot(f"wg{sl}"), waits=wgw)
                lu = sync_dma(wuslot[sl], d_wubf[ic], slot(f"wu{sl}"), waits=wuw)
                pg = (ic % 2)
                pu = 6 + (ic % 2)
                for kc in range(16):
                    tw = ([lg, t_yfm] + bg(pg)) if kc == 0 else []
                    mm(wgslot[sl][:, kc * 128:(kc + 1) * 128],
                       yfm[:, kc * 512:(kc + 1) * 512], ps[pg],
                       start=(kc == 0), stop=(kc == 15), waits=tw,
                       inc=(s_pe if kc == 15 else None))
                t_g = at(s_pe)
                for kc in range(16):
                    tw = ([lu] + bg(pu)) if kc == 0 else []
                    mm(wuslot[sl][:, kc * 128:(kc + 1) * 128],
                       yfm[:, kc * 512:(kc + 1) * 512], ps[pu],
                       start=(kc == 0), stop=(kc == 15), waits=tw,
                       inc=(s_pe if kc == 15 else None))
                t_u = at(s_pe)
                aw = [t_g]
                if prev_mult is not None:
                    aw.append(prev_mult)
                act(sgf, ps[pg], AF.Silu, waits=aw, inc=s_act)
                bguard[pg] = at(s_act)
                mw = [at(s_act), t_u]
                dve("tensor_tensor", hqd[:, j * 512:(j + 1) * 512], sgf,
                    ps[pu], ALU.mult, waits=mw, inc=s_dve)
                prev_mult = at(s_dve)
                bguard[pu] = at(s_dve)
                wg_tok[sl] = t_g
                wu_tok[sl] = t_u
            aw2 = [at(s_dve)]
            if q == 0 and qd == 0:
                aw2.append(t_c_act)
            act(ab, hqd, AF.Abs, waits=aw2, inc=s_act)
            t_abs = at(s_act)
            A_ = [hqd[:, j * 512:(j + 1) * 512] for j in range(4)]
            B_ = [ab[:, j * 512:(j + 1) * 512] for j in range(4)]
            sw = [t_abs]
            if q == 0 and qd == 0:
                sw.append(t_c_dve)
            dve("tensor_tensor", m1, B_[0], B_[1], ALU.min, waits=sw)
            dve("tensor_tensor", M1, B_[0], B_[1], ALU.max)
            dve("tensor_tensor", m2, B_[2], B_[3], ALU.min)
            dve("tensor_tensor", M2, B_[2], B_[3], ALU.max, inc=s_dve)
            dve("tensor_tensor", m1, m1, m2, ALU.max, waits=[at(s_dve)])
            dve("tensor_tensor", M1, M1, M2, ALU.min, inc=s_dve)
            dve("tensor_tensor", t3b, m1, M1, ALU.max, waits=[at(s_dve)],
                inc=s_dve)
            mbuf = [m1, m2, M1, M2]
            hw = [at(s_dve)]
            if hfm_free is not None and qd == 0:
                hw.append(hfm_free)
            for j in range(4):
                dve("tensor_tensor", mbuf[j], B_[j], t3b, ALU.is_ge,
                    waits=(hw if j == 0 else []),
                    inc=(s_dve if j == 3 else None))
            for j in range(4):
                dve("tensor_tensor",
                    h_fm[:, (qd * 4 + j) * 512:(qd * 4 + j + 1) * 512],
                    mbuf[j], A_[j], ALU.mult,
                    waits=([at(s_dve)] if j == 0 else []),
                    inc=(s_dve if j == 3 else None))
            spar_busy = at(s_dve)
        t_hfm = at(s_dve)
        yfm_pe_free = at(s_pe)
        if debug and q == 0:
            sync_dma(d_dbghfm, h_fm, s_misc, waits=[t_hfm])
            sync_dma(d_dbghq[:, 0:2048], hqd, s_misc, waits=[t_hfm])
            sync_dma(d_dbghq[:, 2048:4096], ab, s_misc)
            for si in range(5):
                dbghq_tok = sync_dma(d_dbgscr[:, si * 512:(si + 1) * 512],
                                     scr[si], s_misc)

        wtok_d = [None, None]
        for tp in range(2):
            for blk4 in range(16):
                sl = blk4 % 2
                w = [] if wtok_d[sl] is None else [wtok_d[sl]]
                if blk4 < 2 and tp == 0:
                    w.append(t_hfm)
                    if wd_tok is not None:
                        w.append(wd_tok)
                dst = wdslot[sl].rearrange("p (ic o) -> p ic o", ic=4)
                if q == 0 and tp == 0:
                    w = list(w) + [t_pc_wd]
                ldw_ = sync_dma(dst,
                                d_wdbf[:, blk4 * 8192:(blk4 + 1) * 8192]
                                .rearrange("p (ic o) -> p ic o", ic=4),
                                slot(f"wdl{sl}"), waits=w)
                for u in range(4):
                    ic = blk4 * 4 + u
                    for ti in range(2):
                        tc = 2 * tp + ti
                        for osl in range(4):
                            bi = ti * 4 + osl
                            tw = []
                            if ic == 0 and osl == 0:
                                tw = [ldw_, t_hfm] + bg(bi)
                                if bevac_tok.get((tp, bi)) is not None:
                                    tw.append(bevac_tok[(tp, bi)])
                            elif ic == 0:
                                tw = bg(bi)
                                if bevac_tok.get((tp, bi)) is not None:
                                    tw.append(bevac_tok[(tp, bi)])
                            elif osl == 0 and u == 0:
                                tw = [ldw_]
                            mm(h_fm[:, ic * 512 + tc * 128: ic * 512 + tc * 128 + 128],
                               wdslot[sl][:, u * 2048 + osl * 512:
                                          u * 2048 + (osl + 1) * 512],
                               ps[bi], start=(ic == 0), stop=(ic == 63),
                               waits=tw,
                               inc=(s_pe if (u == 3 and bi == 7) else None))
                wtok_d[sl] = at(s_pe)
            t_dn = at(s_pe)
            bw = [t_dn]
            if q > 0 and tp == 0 and ln2_prev is not None:
                bw.append(ln2_prev)
            for ti in range(2):
                tc = 2 * tp + ti
                for osl in range(4):
                    bi = ti * 4 + osl
                    dve("tensor_copy",
                        b_buf[:, tc * 2048 + osl * 512: tc * 2048 + (osl + 1) * 512],
                        ps[bi], waits=(bw if (ti == 0 and osl == 0) else []),
                        inc=(s_dve if (ti == 1 and osl == 3) else None))
            bwr = at(s_dve)
            for ti in range(2):
                for osl in range(4):
                    bguard[ti * 4 + osl] = bwr
                    bevac_tok[(tp, ti * 4 + osl)] = bwr
        wd_tok = bwr
        hfm_free = at(s_pe)
        if debug and q == 0:
            dbgb_tok = sync_dma(d_dbgb, b_buf, s_misc, waits=[bwr])

        for tcq in range(4):
            tcg = q * 4 + tcq
            w = [t_ysp_all, bwr]
            if out_prev is not None:
                w.append(out_prev)
            if ln2_prev is not None:
                w.append(ln2_prev)
            if tcq == 0:
                w.append(spar_busy)
                if debug and q == 0:
                    w.append(dbghq_tok)
                    w.append(dbgb_tok)
            yld = sync_dma(yln, d_ysp[tcg * 128:(tcg + 1) * 128, :],
                           slot("yln"), waits=w)
            dve("scalar_tensor_tensor", c2, yln, 1.0,
                b_buf[:, tcq * 2048:(tcq + 1) * 2048], ALU.mult, ALU.add,
                accum_out=sb_st[:, 0:1], waits=[yld])
            dve("scalar_tensor_tensor", b_buf[:, tcq * 2048:(tcq + 1) * 2048],
                c2, 1.0, c2, ALU.mult, ALU.mult, accum_out=sb_st[:, 1:2],
                inc=s_dve)
            dve("tensor_scalar", sb_st[:, 2:3], sb_st[:, 0:1], 1.0 / 2048.0,
                None, ALU.mult, waits=[at(s_dve)])
            dve("tensor_scalar", sb_st[:, 3:4], sb_st[:, 1:2], 1.0 / 2048.0,
                None, ALU.mult, inc=s_dve)
            dve("tensor_mul", sb_st[:, 4:5], sb_st[:, 2:3], sb_st[:, 2:3],
                waits=[at(s_dve)], inc=s_dve)
            dve("tensor_sub", sb_st[:, 5:6], sb_st[:, 3:4], sb_st[:, 4:5],
                waits=[at(s_dve)], inc=s_dve)
            act(sb_st[:, 6:7], sb_st[:, 5:6], AF.Ln, bias=sb_eps, waits=[at(s_dve)])
            act(sb_st[:, 7:8], sb_st[:, 6:7], AF.Exp, scale=-0.5, inc=s_act)
            dve("tensor_scalar", outt, c2, sb_st[:, 2:3], sb_st[:, 7:8],
                ALU.subtract, ALU.mult, waits=[at(s_act)], inc=s_dve)
            ln2_prev = at(s_dve)
            out_prev = sync_dma(d_out[tcg * 128:(tcg + 1) * 128, :], outt,
                                s_out, waits=[ln2_prev])
            out_w.append(out_prev)

    prog.emit("sync", lambda e: e.nop(nofuse=True), waits=[at(s_out)])

    with nc.Block() as block:
        prog.run_in(block)
    stk.close()
    return nc


def host_constants():
    pos = np.arange(S, dtype=np.float64)
    j = np.arange(128) % 64
    inv = (1.0 / (1.0e6 ** (2.0 * j / 128.0)))[:, None]
    ang = pos[None, :] * inv
    return dict(
        ident=np.eye(128, dtype=ml_dtypes.bfloat16),
        identf=np.eye(128, dtype=np.float32),
        cosfm=np.cos(ang).astype(np.float32),
        sinfm=np.sin(ang).astype(np.float32),
        ones_bf=np.ones((1, 128), dtype=ml_dtypes.bfloat16),
        ones_f32=np.ones((1, 128), dtype=np.float32),
    )


def ffn_perm():
    d = np.arange(I)
    ic, p = d // 128, d % 128
    qd, j = ic // 4, ic % 4
    return (qd * 128 + p) * 4 + j


def make_in_maps(inputs):
    consts = host_constants()
    perm = ffn_perm()
    x = np.ascontiguousarray(np.asarray(inputs["L0_attn_hidden_states"]))
    shared = dict(
        wq=np.ascontiguousarray(np.asarray(inputs["L0_attn_Wq"])),
        wk=np.ascontiguousarray(np.asarray(inputs["L0_attn_Wk"])),
        wv=np.ascontiguousarray(np.asarray(inputs["L0_attn_Wv"])),
        wo=np.ascontiguousarray(np.asarray(inputs["L0_attn_Wo"])),
        wr=np.ascontiguousarray(np.asarray(inputs["L0_attn_Wr"])),
        wg=np.ascontiguousarray(np.asarray(inputs["L0_ffn_w_gate"])[:, perm]),
        wu=np.ascontiguousarray(np.asarray(inputs["L0_ffn_w_up"])[:, perm]),
        wd=np.ascontiguousarray(np.asarray(inputs["L0_ffn_w_down"])[perm, :]),
        bqfm=np.ascontiguousarray(
            np.asarray(inputs["L0_attn_bq"]).reshape(NH, 128).T),
        bkfm=np.ascontiguousarray(
            np.asarray(inputs["L0_attn_bk"]).reshape(NKV, 128).T),
        bv=np.ascontiguousarray(np.asarray(inputs["L0_attn_bv"]).reshape(1, -1)),
        bo=np.ascontiguousarray(np.asarray(inputs["L0_attn_bo"]).reshape(1, -1)),
        br=np.ascontiguousarray(np.asarray(inputs["L0_attn_br"]).reshape(NH, 1)),
        **consts)
    return [dict(x=np.ascontiguousarray(x[c]), **shared) for c in range(8)]


def kernel(**inputs):
    nc = build_program(debug=False)
    in_maps = make_in_maps(inputs)
    res = run_bass_kernel_spmd(nc, in_maps, core_ids=list(range(8)))
    return np.stack([np.asarray(res.results[c]["out"], dtype=np.float32)
                     for c in range(8)], axis=0)
